# version 06
# baseline (speedup 1.0000x reference)
import numpy as np
import ml_dtypes
import contextlib

import concourse.bass as bass
import concourse.mybir as mybir
from concourse.bass_utils import run_bass_kernel_spmd

F32 = mybir.dt.float32
BF16 = mybir.dt.bfloat16
AF = mybir.ActivationFunctionType
ALU = mybir.AluOpType
AX = mybir.AxisListType

D = 1024
DIN = 2048
S = 16
RK = 64
NE = 16
H = 512
HS = 1024
EPS = 1e-5
QB = 128.0
NCORES = 8
BF = ml_dtypes.bfloat16
ENGS = ("sync", "tensor", "vector", "scalar", "gpsimd")
L_FULL = 16384
NPHASES = None
FFN_DBG = None


class DSem:
    def __init__(self, nc, ctx, name):
        self.h = ctx.enter_context(nc.semaphore(name))
        self.vn = 0
        self.name = name


class Script:

    def __init__(self):
        self.q = {e: [] for e in ENGS}

    def do(self, eng, fn, inc=None, waits=(), tag=""):
        wl = []
        for w in waits:
            if isinstance(w, DSem):
                wl.append((w, w.vn))
            else:
                wl.append(w)
        target = None
        if inc is not None:
            sem, k = inc
            sem.vn += 1 if k == "cc" else k
            target = (sem, k)

        def run(e, wl=wl, fn=fn, target=target, tag=tag):
            for (ws, wn) in wl:
                if wn > 0:
                    e.wait_ge(ws.h, wn)
            inst = fn(e)
            if target is not None:
                sem, k = target
                assert inst is not None, f"op with inc returned None ({tag})"
                if k == "cc":
                    inst.then_inc(sem.h)
                else:
                    inst.then_inc(sem.h, k)

        self.q[eng].append(run)

    def emit(self, blk):
        for name in ENGS:
            ops = self.q[name]
            if not ops:
                continue

            def body(e, ops=ops):
                for op in ops:
                    op(e)

            getattr(blk, name)(body)


def build_program(L):
    T_loc = L // 4
    NCH1 = T_loc // 512
    SPAN = min(2, NCH1)
    NH = NCH1 // SPAN
    NCH3 = L // 512
    CH4 = 1024
    NCH4 = L // CH4
    DCH = 512
    TPH = SPAN * 512
    NSUB = SPAN * 4

    nc = bass.Bass("TRN2", target_bir_lowering=False)

    def inp(name, shape, dt=F32):
        return nc.dram_tensor(name, list(shape), dt, kind="ExternalInput").ap()

    x_sh = inp("x_sh", (T_loc, D))
    ffn_W1 = inp("ffn_W1", (NE, 128, 8 * 1024))
    ffn_W2 = inp("ffn_W2", (NE, 128, 4 * 1024))
    ffn_Ws1a = inp("ffn_Ws1a", (128, 8 * 1024))
    ffn_Ws1b = inp("ffn_Ws1b", (128, 8 * 1024))
    ffn_Ws2 = inp("ffn_Ws2", (128, 8 * 1024))
    ffn_WgT = inp("ffn_WgT", (128, 8 * 16))
    aW1T = inp("aW1T", (128, 8 * DIN))
    aW2T = inp("aW2T", (128, 16 * D))
    b1_row = inp("b1_row", (1, DIN))
    b2_row = inp("b2_row", (1, D))
    b1ln_cs = inp("b1ln_cs", (128, 8), BF16)
    g1_cs = inp("g1_cs", (128, 8))
    b2ln_cs = inp("b2ln_cs", (128, 8), BF16)
    g2_cs = inp("g2_cs", (128, 8))
    ipT_sl = inp("ipT_sl", (128, 8 * 1024))
    opT = inp("opT", (128, 16 * 1024))
    xpT_sl = inp("xpT_sl", (128, 4 * 96))
    dtpT_sl = inp("dtpT_sl", (RK, DCH))
    dtpb_cs = inp("dtpb_cs", (128, 4))
    convw_cs = inp("convw_cs", (128, 16))
    convb_cs = inp("convb_cs", (128, 4))
    A_cs = inp("A_cs", (128, 64))
    D_cs = inp("D_cs", (128, 4))
    identf = inp("identf", (128, 128))
    ones_row = inp("ones_row", (1, 512), BF16)
    sel16 = inp("sel16", (16, 16 * 128), BF16)
    selbc = inp("selbc", (32, 32 * 128))
    gmask = inp("gmask", (1, 16))

    out_sh = nc.dram_tensor("out_sh", [T_loc, D], F32, kind="ExternalOutput").ap()

    def scr(name, shape, dt=F32):
        return nc.dram_tensor(name, list(shape), dt)

    x1_d = scr("x1_d", (128, 8 * T_loc))
    u1_d = scr("u1_d", (128, 8 * T_loc))
    x2p_d = scr("x2p_d", (128, 8 * T_loc))
    x2_d = scr("x2_d", (128, 8 * T_loc))
    u_d = scr("u_d", (DCH, L))
    sz_d = scr("sz_d", (DCH, L))
    sW1gT_d = scr("sW1gT_d", (128, 8 * DIN), BF16)
    w1bf_d = scr("w1bf_d", (128, NE * 8 * 1024), BF16)
    w2bf_d = scr("w2bf_d", (128, NE * 4 * 1024), BF16)
    ws1a_d = scr("ws1a_d", (128, 8 * 1024), BF16)
    ws1b_d = scr("ws1b_d", (128, 8 * 1024), BF16)
    ws2_d = scr("ws2_d", (128, 8 * 1024), BF16)
    opbf_d = scr("opbf_d", (128, 16 * 1024), BF16)
    sW2T_d = scr("sW2T_d", (128, 16 * D), BF16)
    ipT_d = scr("ipT_d", (128, 8 * 1024), BF16)
    gam_in = scr("gam_in", (1, 16))
    gam_out = scr("gam_out", (1, 16))
    ulnag_in = scr("ulnag_in", (128, 8 * T_loc), BF16)
    ulnag_out = scr("ulnag_out", (4 * 128, 8 * T_loc), BF16)
    xdbl_in = scr("xdbl_in", (96, L))
    xdbl_out = scr("xdbl_out", (96, L))
    ygag_in = scr("ygag_in", (DCH, L), BF16)
    ygag_out = scr("ygag_out", (DIN, L), BF16)

    G4 = [[0, 1, 2, 3], [4, 5, 6, 7]]
    G8 = [list(range(NCORES))]

    ctx = contextlib.ExitStack()
    sb = lambda name, shape, dt=F32: ctx.enter_context(nc.sbuf_tensor(name, list(shape), dt))

    s_identf = sb("s_identf", (128, 128))
    s_ones = sb("s_ones", (1, 512), BF16)
    s_sel16 = sb("s_sel16", (16, 16 * 128), BF16)
    s_b1eff = sb("s_b1eff", (1, DIN), BF16)
    s_b2row = sb("s_b2row", (1, D), BF16)
    s_WgT = sb("s_WgT", (128, 8 * 16), BF16)
    s_qscale = sb("s_qscale", (128, 1))
    s_gmaskb = sb("s_gmaskb", (128, 16))
    s_negA = sb("s_negA", (128, 64))
    s_Dcs = sb("s_Dcs", (128, 4))
    s_cbeff = sb("s_cbeff", (128, 4))
    s_ipbz = sb("s_ipbz", (128, 4))
    s_dtpb = sb("s_dtpb", (128, 4))
    s_convw = sb("s_convw", (128, 16))
    s_xpT = sb("s_xpT", (128, 4 * 96), BF16)
    s_dtpT = sb("s_dtpT", (64, 4 * 128), BF16)
    s_hinit = sb("s_hinit", (128, 64))
    s_gsc = sb("s_gsc", (1, 16))
    s_gcols = sb("s_gcols", (128, 64))

    SEM = {}
    for name in ["ld", "wld", "wls", "st", "mm", "ev", "ve", "gp", "cc1", "cc2", "cc3", "cc4"]:
        SEM[name] = DSem(nc, ctx, name)
    ld, wld, st = SEM["ld"], SEM["wld"], SEM["st"]
    wls = SEM["wls"]
    gp = SEM["gp"]
    mm, ev, ve = SEM["mm"], SEM["ev"], SEM["ve"]
    cc1, cc2, cc3, cc4 = SEM["cc1"], SEM["cc2"], SEM["cc3"], SEM["cc4"]

    def run_phase(builder):
        with nc.Block() as blk, contextlib.ExitStack() as pctx:
            s = Script()
            builder(s, pctx)
            s.do("sync", lambda e: e.nop(), waits=[st, wld, wls])
            s.emit(blk)
        nc.all_engine_barrier()

    def emit_ln(s, s_x, s_stats, s_scr, norm_out, wait_loaded):
        for sub in range(4):
            s.do("scalar", lambda e, sub=sub: e.activation(
                s_scr[:, 0:D], s_x[:, sub * D:(sub + 1) * D], AF.Square,
                accum_out=s_stats[:, 16 + sub:17 + sub]),
                inc=(ev, 1), waits=[wait_loaded, ev] if sub == 0 else [ev])
            s.do("scalar", lambda e, sub=sub: e.activation(
                s_scr[:, D:2 * D], s_x[:, sub * D:(sub + 1) * D], AF.Copy,
                accum_out=s_stats[:, sub:sub + 1]),
                inc=(ev, 1), waits=[ev])
        ST = s_stats
        s.do("vector", lambda e: e.tensor_scalar_mul(ST[:, 32:36], ST[:, 0:4], 1.0 / D),
             inc=(ve, 1), waits=[ev])
        s.do("vector", lambda e: e.scalar_tensor_tensor(
            ST[:, 48:52], ST[:, 32:36], -1.0, ST[:, 32:36], ALU.mult, ALU.mult),
            inc=(ve, 1), waits=[ve])
        s.do("vector", lambda e: e.scalar_tensor_tensor(
            ST[:, 64:68], ST[:, 16:20], 1.0 / D, ST[:, 48:52], ALU.mult, ALU.add),
            inc=(ve, 1), waits=[ve])
        s.do("vector", lambda e: e.tensor_scalar_add(ST[:, 80:84], ST[:, 64:68], EPS),
             inc=(ve, 1), waits=[ve])
        s.do("scalar", lambda e: e.activation(ST[:, 64:68], ST[:, 80:84], AF.Ln),
             inc=(ev, 1), waits=[ve])
        s.do("scalar", lambda e: e.activation(ST[:, 96:100], ST[:, 64:68], AF.Exp,
                                              scale=-0.5), inc=(ev, 1), waits=[ev])
        s.do("vector", lambda e: e.scalar_tensor_tensor(
            ST[:, 112:116], ST[:, 32:36], -1.0, ST[:, 96:100], ALU.mult, ALU.mult),
            inc=(ve, 1), waits=[ev])
        for sub in range(4):
            s.do("vector", lambda e, sub=sub: e.tensor_scalar(
                norm_out(sub), s_x[:, sub * D:(sub + 1) * D],
                ST[:, 96 + sub:97 + sub], ST[:, 112 + sub:113 + sub],
                ALU.mult, ALU.add),
                inc=(ve, 1), waits=[ve] if sub == 0 else ())

    def emit_tr(s, src_ap, dst_col, p4, slots, first_wait=None, evac_first_wait=None):
        for sub in range(4):
            for kt in range(8):
                g = sub * 8 + kt
                slot = g % 2
                w = []
                if first_wait is not None and g == 0:
                    w.append(first_wait)
                if slot in slots:
                    w.append((ev, slots[slot]))
                s.do("tensor", lambda e, sub=sub, kt=kt, slot=slot: e.transpose(
                    p4[:, slot * 512:slot * 512 + 128],
                    src_ap(sub, kt), s_identf[:, :]),
                    inc=(mm, 1), waits=w)
                ew = [mm]
                if evac_first_wait is not None and g == 0:
                    ew.append(evac_first_wait)
                s.do("scalar", lambda e, sub=sub, kt=kt, slot=slot: e.copy(
                    dst_col(sub, kt), p4[:, slot * 512:slot * 512 + 128]),
                    inc=(ev, 1), waits=ew)
                slots[slot] = ev.vn

    def phaseW(s, pctx):
        psb = lambda n, sh, dt=F32: pctx.enter_context(nc.sbuf_tensor(n, list(sh), dt))
        s_w1u = psb("w_w1u", (128, 8 * DIN), BF16)
        s_ipTu = psb("w_ipTu", (128, 8 * 1024), BF16)
        s_sc2T = psb("w_sc2T", (128, 16 * D), BF16)
        s_b1c = psb("w_b1c", (128, 8), BF16)
        s_b2c = psb("w_b2c", (128, 8), BF16)
        s_g1 = psb("w_g1", (128, 8))
        s_g2 = psb("w_g2", (128, 8))
        s_bacc = psb("w_bacc", (1, DIN))
        s_b1raw = psb("w_b1raw", (1, DIN))
        s_ipbrow = psb("w_ipbrow", (1, 1024))
        s_tmpc = psb("w_tmpc", (128, 16))
        s_w1g = psb("w_w1g", (128, 8 * DIN), BF16)
        p_b = pctx.enter_context(nc.psum_tensor("w_pb", [1, 512], F32))
        p_t = pctx.enter_context(nc.psum_tensor("w_pt", [128, 8], F32))

        for o, i in [
            (s_identf[:], identf[:, :]), (s_ones[:], ones_row[:, :]),
            (s_sel16[:], sel16[:, :]),
            (s_g1[:], g1_cs[:, :]), (s_g2[:], g2_cs[:, :]),
            (s_negA[:], A_cs[:, :]), (s_Dcs[:], D_cs[:, :]),
            (s_dtpb[:], dtpb_cs[:, :]), (s_convw[:], convw_cs[:, :]),
            (s_cbeff[:], convb_cs[:, :]), (s_b1raw[:], b1_row[:, :]),
            (s_gmaskb[:], bass.AP(gmask.tensor, 0, [[0, 128], [1, 16]])),
            (s_b1c[:], b1ln_cs[:, :]), (s_b2c[:], b2ln_cs[:, :]),
        ]:
            s.do("sync", lambda e, o=o, i=i: e.dma_start(out=o, in_=i), inc=(ld, 16))
        for o, i in [
            (s_w1u[:], aW1T[:, :]),
            (s_sc2T[:], aW2T[:, :]),
            (s_ipTu[:], ipT_sl[:, :]),
            (s_WgT[:], ffn_WgT[:, :]),
            (s_xpT[:], xpT_sl[:, :]),
            (s_dtpT[:], dtpT_sl[:, :]),
            (s_b2row[:], b2_row[:, :]),
        ]:
            s.do("gpsimd", lambda e, o=o, i=i: e.dma_start(out=o, in_=i), inc=(wld, 16))
        for o, i in [
            (w1bf_d.ap().rearrange("p (e m) -> p e m", e=NE),
             ffn_W1.rearrange("e p m -> p e m")),
            (w2bf_d.ap().rearrange("p (e m) -> p e m", e=NE),
             ffn_W2.rearrange("e p m -> p e m")),
            (ws1a_d.ap(), ffn_Ws1a[:, :]),
            (ws1b_d.ap(), ffn_Ws1b[:, :]),
            (ws2_d.ap(), ffn_Ws2[:, :]),
            (opbf_d.ap(), opT[:, :]),
        ]:
            s.do("gpsimd", lambda e, o=o, i=i: e.dma_start(out=o, in_=i), inc=(wld, 16))

        s.do("scalar", lambda e: e.activation(s_w1u[:], s_w1u[:], AF.Sign),
             inc=(ev, 1), waits=[ld, wld])
        s.do("scalar", lambda e: e.activation(s_sc2T[:], s_sc2T[:], AF.Sign),
             inc=(ev, 1))
        s.do("scalar", lambda e: e.activation(s_negA[:], s_negA[:], AF.Exp),
             inc=(ev, 1))
        s.do("vector", lambda e: e.tensor_scalar_mul(s_negA[:], s_negA[:], -1.0),
             inc=(ve, 1), waits=[ev])
        for c in range(4):
            if c > 0:
                s.do("tensor", lambda e: e.nop(), waits=[ev])
            for k in range(8):
                s.do("tensor", lambda e, c=c, k=k: e.matmul(
                    p_b[:, :], s_b1c[:, k:k + 1],
                    s_w1u[:, k * DIN + c * 512:k * DIN + (c + 1) * 512],
                    start=(k == 0), stop=(k == 7)),
                    inc=(mm, 1) if k == 7 else None,
                    waits=[ev, ve] if (c == 0 and k == 0) else ())
            s.do("scalar", lambda e, c=c: e.copy(
                s_bacc[0:1, c * 512:(c + 1) * 512], p_b[0:1, :]), inc=(ev, 1), waits=[mm])
        for c in range(2):
            s.do("tensor", lambda e: e.nop(), waits=[ev])
            for k in range(8):
                s.do("tensor", lambda e, c=c, k=k: e.matmul(
                    p_b[:, :], s_b2c[:, k:k + 1],
                    s_ipTu[:, k * 1024 + c * 512:k * 1024 + (c + 1) * 512],
                    start=(k == 0), stop=(k == 7)),
                    inc=(mm, 1) if k == 7 else None)
            s.do("scalar", lambda e, c=c: e.copy(
                s_ipbrow[0:1, c * 512:(c + 1) * 512], p_b[0:1, :]), inc=(ev, 1), waits=[mm])
        s.do("vector", lambda e: e.tensor_add(s_bacc[0:1, :], s_bacc[0:1, :],
                                              s_b1raw[0:1, :]), inc=(ve, 1), waits=[ev])
        s.do("vector", lambda e: e.tensor_copy(s_b1eff[0:1, :], s_bacc[0:1, :]),
             inc=(ve, 1), waits=[ve])
        for k in range(8):
            s.do("scalar", lambda e, k=k: e.mul(
                s_w1g[:, k * DIN:(k + 1) * DIN], s_w1u[:, k * DIN:(k + 1) * DIN],
                s_g1[:, k:k + 1]), inc=(ev, 1), waits=[ev] if k == 0 else ())
        for k in range(8):
            s.do("scalar", lambda e, k=k: e.mul(
                s_ipTu[:, k * 1024:(k + 1) * 1024], s_ipTu[:, k * 1024:(k + 1) * 1024],
                s_g2[:, k:k + 1]), inc=(ev, 1))
        s.do("sync", lambda e: e.dma_start(out=sW1gT_d[:, :], in_=s_w1g[:]),
             inc=(st, 16), waits=[ev])
        s.do("sync", lambda e: e.dma_start(out=sW2T_d[:, :], in_=s_sc2T[:]),
             inc=(st, 16))
        s.do("sync", lambda e: e.dma_start(out=ipT_d[:, :], in_=s_ipTu[:]),
             inc=(st, 16))
        for h8 in range(8):
            s.do("tensor", lambda e, h=h8: e.transpose(
                p_t[:, h:h + 1], s_ipbrow[0:1, h * 128:(h + 1) * 128],
                s_identf[0:1, 0:1]),
                inc=(mm, 1) if h8 == 7 else None, waits=[ev] if h8 == 0 else ())
        s.do("vector", lambda e: e.tensor_add(s_tmpc[:, 0:4], s_convw[:, 0:16:4],
                                              s_convw[:, 1:16:4]), inc=(ve, 1), waits=[mm])
        s.do("vector", lambda e: e.tensor_add(s_tmpc[:, 4:8], s_convw[:, 2:16:4],
                                              s_convw[:, 3:16:4]), inc=(ve, 1))
        s.do("vector", lambda e: e.tensor_add(s_tmpc[:, 8:12], s_tmpc[:, 0:4],
                                              s_tmpc[:, 4:8]), inc=(ve, 1), waits=[ve])
        s.do("vector", lambda e: e.tensor_mul(s_tmpc[:, 12:16], p_t[:, 0:4],
                                              s_tmpc[:, 8:12]), inc=(ve, 1), waits=[ve])
        s.do("vector", lambda e: e.tensor_add(s_cbeff[:, 0:4], s_cbeff[:, 0:4],
                                              s_tmpc[:, 12:16]), inc=(ve, 1), waits=[ve])
        s.do("vector", lambda e: e.tensor_copy(s_ipbz[:, 0:4], p_t[:, 4:8]), inc=(ve, 1))

    def make_ffn(in_dram, out_dram, tag, in_ext=False, out_ext=False):
        def ld_ap(r0):
            if in_ext:
                return in_dram[r0:r0 + 512, :].rearrange("(a p) m -> p a m", p=128)
            return in_dram[:, r0 * 8:r0 * 8 + 4096]

        def st_ap(r0):
            if out_ext:
                return out_dram[r0:r0 + 512, :].rearrange("(a p) m -> p a m", p=128)
            return out_dram[:, r0 * 8:r0 * 8 + 4096]
        def phase(s, pctx):
            psb = lambda n, sh, dt=F32: pctx.enter_context(nc.sbuf_tensor(tag + n, list(sh), dt))
            pps = lambda n, sh: pctx.enter_context(nc.psum_tensor(tag + n, list(sh), F32))
            s_x = psb("f_x", (128, 4 * D))
            s_scr = psb("f_scr", (128, 2 * D))
            s_stats = psb("f_stats", (128, 128))
            s_xnT = psb("f_xnT", (128, 8 * TPH), BF16)
            s_sc = psb("f_sc", (128, NSUB * 16))
            s_es = psb("f_es", (128, NSUB * 16))
            s_msk = psb("f_msk", (128, NSUB * 16))
            s_cw = psb("f_cw", (128, NSUB * 16))
            s_cwT = psb("f_cwT", (16, TPH), BF16)
            s_w1e = [psb(f"f_w1e{i}", (128, 8 * 1024), BF16) for i in range(2)]
            s_w2e = [psb("f_w2e0", (128, 8 * 1024), BF16),
                     psb("f_w2e1", (128, 4 * 1024), BF16)]
            s_hu = psb("f_hu", (128, 2 * 8 * 512), BF16)
            s_sg = psb("f_sg", (128, 2 * 8 * 512), BF16)
            s_hgT = psb("f_hgT", (128, 2 * 8 * 512), BF16)
            s_cwb = psb("f_cwb", (128, 2 * 512), BF16)
            s_zacc = psb("f_zacc", (128, NSUB * 1024))
            p_up = [pps(f"f_pup{i}", (128, 512)) for i in range(3)]
            p_cw = pps("f_pcw", (128, 512))
            p_z = [pps(f"f_pz{i}", (128, 1024)) for i in range(2)]

            up_slot_ev = {}
            z_slot = {}
            cw_ev = [0]
            trx = {}
            trc = {}
            upg = [0]

            def up_pass(ci, wbuf, dst, first_waits):
                for mt in range(8):
                    slot = upg[0] % 3
                    upg[0] += 1
                    w = list(first_waits) if mt == 0 else []
                    if slot in up_slot_ev:
                        w.append((ev, up_slot_ev[slot]))
                    for kt in range(8):
                        s.do("tensor", lambda e, ci=ci, mt=mt, kt=kt, slot=slot, wbuf=wbuf: e.matmul(
                            p_up[slot][:, :],
                            wbuf[:, kt * 1024 + mt * 128:kt * 1024 + (mt + 1) * 128],
                            s_xnT[:, kt * TPH + ci * 512:kt * TPH + (ci + 1) * 512],
                            start=(kt == 0), stop=(kt == 7)),
                            inc=(mm, 1) if kt == 7 else None, waits=w)
                    kind, ap = dst(mt)
                    if kind == "copy":
                        s.do("scalar", lambda e, ap=ap, slot=slot: e.copy(
                            ap, p_up[slot][:, :]), inc=(ev, 1), waits=[mm])
                    else:
                        s.do("scalar", lambda e, ap=ap, slot=slot: e.activation(
                            ap, p_up[slot][:, :], AF.Silu), inc=(ev, 1), waits=[mm])
                    up_slot_ev[slot] = ev.vn

            def down_pass(ci, cpar, w2buf, nkt, zcopy):
                for sub in range(4):
                    slot = sub % 2
                    w = [(ve, ve.vn)] if sub == 0 else []
                    if slot in z_slot:
                        w.append(z_slot[slot])
                    first = True
                    for nh in range(2):
                        for kt in range(nkt):
                            s.do("tensor", lambda e, sub=sub, nh=nh, kt=kt, slot=slot, w2buf=w2buf, cpar=cpar: e.matmul(
                                p_z[slot][:, nh * 512:(nh + 1) * 512],
                                s_hgT[:, cpar * 4096 + kt * 512 + sub * 128:
                                      cpar * 4096 + kt * 512 + (sub + 1) * 128],
                                w2buf[:, kt * 1024 + nh * 512:kt * 1024 + (nh + 1) * 512],
                                start=(kt == 0), stop=(kt == nkt - 1)),
                                inc=(mm, 1) if (nh == 1 and kt == nkt - 1) else None,
                                waits=w if first else ())
                            first = False
                    zc = (ci * 4 + sub) * 1024
                    if zcopy:
                        s.do("vector", lambda e, zc=zc, slot=slot: e.tensor_copy(
                            s_zacc[:, zc:zc + 1024], p_z[slot][:, :]),
                            inc=(ve, 1), waits=[mm, st])
                    else:
                        s.do("vector", lambda e, zc=zc, slot=slot: e.tensor_add(
                            s_zacc[:, zc:zc + 1024], s_zacc[:, zc:zc + 1024],
                            p_z[slot][:, :]),
                            inc=(ve, 1), waits=[mm])
                    z_slot[slot] = (ve, ve.vn)

            def gate_pass(ci, cpar, n, with_cw):
                for mt in range(n):
                    w = [ev, mm] if mt == 0 else []
                    s.do("vector", lambda e, mt=mt, cpar=cpar: e.tensor_mul(
                        s_hgT[:, cpar * 4096 + mt * 512:cpar * 4096 + (mt + 1) * 512],
                        s_hu[:, cpar * 4096 + mt * 512:cpar * 4096 + (mt + 1) * 512],
                        s_sg[:, cpar * 4096 + mt * 512:cpar * 4096 + (mt + 1) * 512]),
                        inc=(ve, 1), waits=w)
                if with_cw:
                    for mt in range(n):
                        s.do("vector", lambda e, mt=mt, cpar=cpar: e.tensor_mul(
                            s_hgT[:, cpar * 4096 + mt * 512:cpar * 4096 + (mt + 1) * 512],
                            s_hgT[:, cpar * 4096 + mt * 512:cpar * 4096 + (mt + 1) * 512],
                            s_cwb[:, cpar * 512:(cpar + 1) * 512]),
                            inc=(ve, 1), waits=[ve] if mt == 0 else ())

            def cwrep(ci, cpar, ei):
                s.do("tensor", lambda e, ci=ci, ei=ei: e.matmul(
                    p_cw[:, :], s_sel16[:, ei * 128:(ei + 1) * 128],
                    s_cwT[0:16, ci * 512:(ci + 1) * 512],
                    start=True, stop=True),
                    inc=(mm, 1), waits=[(ev, cw_ev[0])])
                s.do("scalar", lambda e, cpar=cpar: e.copy(
                    s_cwb[:, cpar * 512:(cpar + 1) * 512], p_cw[:, :]),
                    inc=(ev, 1), waits=[mm, (ve, ve.vn)])
                cw_ev[0] = ev.vn

            dbg = FFN_DBG or {}
            nhv = dbg.get("nh", NH)
            nexp = dbg.get("ne", NE)
            do_shared = dbg.get("sh", True)
            do_epi = dbg.get("epi", True)
            do_rout = dbg.get("rout", True)
            do_exp = dbg.get("exp", True)
            hv_prev_mm = 0
            for hv in range(nhv):
                mmafter = {}
                tok0 = hv * TPH
                for ci in range(SPAN):
                    r0 = tok0 + ci * 512
                    s.do("sync", lambda e, r0=r0: e.dma_start(
                        out=(s_x[:].rearrange("p (a m) -> p a m", a=4) if in_ext
                             else s_x[:]),
                        in_=ld_ap(r0)),
                        inc=(ld, 16), waits=[ve, st, (mm, mm.vn)])
                    if dbg.get("ln", True):
                        emit_ln(s, s_x, s_stats, s_scr,
                                lambda sub: s_x[:, sub * D:(sub + 1) * D], ld)
                    if dbg.get("tr", True):
                        emit_tr(s, lambda sub, kt: s_x[:, sub * D + kt * 128:sub * D + (kt + 1) * 128],
                                lambda sub, kt, ci=ci: s_xnT[
                                    :, kt * TPH + ci * 512 + sub * 128:
                                    kt * TPH + ci * 512 + (sub + 1) * 128],
                                p_z[0], trx, first_wait=(ve, ve.vn),
                                evac_first_wait=(mm, mm.vn))
                if not do_rout:
                    continue
                for g in range(NSUB):
                    ci, sub = g // 4, g % 4
                    for kt in range(8):
                        w = [ev, (ev, cw_ev[0])] if (g == 0 and kt == 0) else []
                        s.do("tensor", lambda e, ci=ci, sub=sub, g=g, kt=kt: e.matmul(
                            p_cw[:, g * 16:(g + 1) * 16],
                            s_xnT[:, kt * TPH + ci * 512 + sub * 128:
                                  kt * TPH + ci * 512 + (sub + 1) * 128],
                            s_WgT[:, kt * 16:(kt + 1) * 16],
                            start=(kt == 0), stop=(kt == 7)),
                            inc=(mm, 1) if kt == 7 else None, waits=w)
                NCOL = NSUB * 16
                s.do("scalar", lambda e: e.copy(s_sc[:, 0:NCOL], p_cw[:, 0:NCOL]),
                     inc=(ev, 1), waits=[mm])
                cw_ev[0] = ev.vn
                for g in range(NSUB):
                    s.do("vector", lambda e, g=g: e.reduce_max(
                        s_stats[:, g:g + 1], s_sc[:, g * 16:(g + 1) * 16], axis=AX.X),
                        inc=(ve, 1), waits=[ev] if g == 0 else ())
                s.do("vector", lambda e: e.tensor_scalar_mul(
                    s_stats[:, 16:16 + NSUB], s_stats[:, 0:NSUB], -1.0),
                    inc=(ve, 1), waits=[ve])
                for g in range(NSUB):
                    s.do("scalar", lambda e, g=g: e.activation(
                        s_es[:, g * 16:(g + 1) * 16], s_sc[:, g * 16:(g + 1) * 16],
                        AF.Exp, bias=s_stats[:, 16 + g:17 + g],
                        accum_out=s_stats[:, 32 + g:33 + g]),
                        inc=(ev, 1), waits=[ve] if g == 0 else ())
                s.do("vector", lambda e: e.reciprocal(
                    s_stats[:, 48:48 + NSUB], s_stats[:, 32:32 + NSUB]),
                    inc=(ve, 1), waits=[ev])
                for g in range(NSUB):
                    s.do("vector", lambda e, g=g: e.tensor_single_scalar(
                        s_msk[:, g * 16:(g + 1) * 16], s_sc[:, g * 16:(g + 1) * 16],
                        s_stats[:, g:g + 1], ALU.is_ge),
                        inc=(ve, 1), waits=[ve] if g == 0 else ())
                s.do("vector", lambda e: e.scalar_tensor_tensor(
                    s_cw[:, 0:NCOL], s_msk[:, 0:NCOL], -1e30, s_sc[:, 0:NCOL],
                    ALU.mult, ALU.add), inc=(ve, 1), waits=[ve])
                for g in range(NSUB):
                    s.do("vector", lambda e, g=g: e.reduce_max(
                        s_stats[:, 64 + g:65 + g], s_cw[:, g * 16:(g + 1) * 16], axis=AX.X),
                        inc=(ve, 1), waits=[ve] if g == 0 else ())
                for g in range(NSUB):
                    s.do("vector", lambda e, g=g: e.tensor_single_scalar(
                        s_cw[:, g * 16:(g + 1) * 16], s_sc[:, g * 16:(g + 1) * 16],
                        s_stats[:, 64 + g:65 + g], ALU.is_ge),
                        inc=(ve, 1), waits=[ve] if g == 0 else ())
                s.do("vector", lambda e: e.tensor_max(
                    s_msk[:, 0:NCOL], s_msk[:, 0:NCOL], s_cw[:, 0:NCOL]),
                    inc=(ve, 1), waits=[ve])
                s.do("vector", lambda e: e.tensor_mul(
                    s_cw[:, 0:NCOL], s_es[:, 0:NCOL], s_msk[:, 0:NCOL]),
                    inc=(ve, 1), waits=[ve])
                for g in range(NSUB):
                    s.do("vector", lambda e, g=g: e.tensor_single_scalar(
                        s_cw[:, g * 16:(g + 1) * 16], s_cw[:, g * 16:(g + 1) * 16],
                        s_stats[:, 48 + g:49 + g], ALU.mult),
                        inc=(ve, 1), waits=[ve] if g == 0 else ())
                for g in range(NSUB):
                    slot = g % 2
                    w = [ve] if g == 0 else []
                    if slot in trc:
                        w.append((ev, trc[slot]))
                    s.do("tensor", lambda e, g=g, slot=slot: e.transpose(
                        p_z[1][0:16, slot * 512:slot * 512 + 128],
                        s_cw[:, g * 16:(g + 1) * 16], s_identf[:, :]),
                        inc=(mm, 1), waits=w)
                    s.do("scalar", lambda e, g=g, slot=slot: e.copy(
                        s_cwT[0:16, g * 128:(g + 1) * 128],
                        p_z[1][0:16, slot * 512:slot * 512 + 128]),
                        inc=(ev, 1), waits=[mm])
                    trc[slot] = ev.vn
                cw_ev[0] = ev.vn
                z_slot[0] = (ev, ev.vn)
                z_slot[1] = (ev, ev.vn)

                if not do_exp:
                    continue
                for ei in range(nexp):
                    par = ei % 2
                    bfree = (mm, mmafter.get(ei - 2, hv_prev_mm))
                    s.do("scalar", lambda e, ei=ei, par=par: e.dma_start(
                        out=s_w1e[par][:],
                        in_=w1bf_d[:, ei * 8192:(ei + 1) * 8192]),
                        inc=(wls, 16), waits=[bfree])
                    s.do("scalar", lambda e, ei=ei, par=par: e.dma_start(
                        out=s_w2e[par][:, 0:4 * 1024],
                        in_=w2bf_d[:, ei * 4096:(ei + 1) * 4096]),
                        inc=(wls, 16))
                    wneed = (wls, wls.vn)
                    for ci in range(SPAN):
                        cpar = ci % 2
                        dstx = (lambda mt, cpar=cpar:
                                ("copy", s_hu[:, cpar * 4096 + mt * 512:cpar * 4096 + (mt + 1) * 512])
                                if mt < 4 else
                                ("silu", s_sg[:, cpar * 4096 + (mt - 4) * 512:cpar * 4096 + (mt - 3) * 512]))
                        up_pass(ci, s_w1e[par], dstx, [wneed] if ci == 0 else [])
                        cwrep(ci, cpar, ei)
                        gate_pass(ci, cpar, 4, True)
                        if ci >= 1:
                            down_pass(ci - 1, (ci - 1) % 2, s_w2e[par], 4, (ei == 0))
                    down_pass(SPAN - 1, (SPAN - 1) % 2, s_w2e[par], 4, (ei == 0))
                    mmafter[ei] = mm.vn
                if not do_shared:
                    continue
                s.do("scalar", lambda e: e.dma_start(
                    out=s_w1e[0][:], in_=ws1a_d[:, :]),
                    inc=(wls, 16), waits=[(mm, mmafter.get(nexp - 2, hv_prev_mm))])
                s.do("scalar", lambda e: e.dma_start(
                    out=s_w2e[0][:], in_=ws1b_d[:, :]),
                    inc=(wls, 16))
                s.do("scalar", lambda e: e.dma_start(
                    out=s_w1e[1][:], in_=ws2_d[:, :]),
                    inc=(wls, 16), waits=[(mm, mmafter.get(nexp - 1, hv_prev_mm))])
                wneed = (wls, wls.vn)
                for ci in range(SPAN):
                    cpar = ci % 2
                    up_pass(ci, s_w1e[0],
                            lambda mt, cpar=cpar: ("copy", s_hu[:, cpar * 4096 + mt * 512:cpar * 4096 + (mt + 1) * 512]),
                            [wneed] if ci == 0 else [])
                    up_pass(ci, s_w2e[0],
                            lambda mt, cpar=cpar: ("silu", s_sg[:, cpar * 4096 + mt * 512:cpar * 4096 + (mt + 1) * 512]),
                            [])
                    gate_pass(ci, cpar, 8, False)
                    if ci >= 1:
                        down_pass(ci - 1, (ci - 1) % 2, s_w1e[1], 8, False)
                down_pass(SPAN - 1, (SPAN - 1) % 2, s_w1e[1], 8, False)
                hv_prev_mm = mm.vn
                if not do_epi:
                    continue
                for ci in range(SPAN):
                    r0 = tok0 + ci * 512
                    s.do("sync", lambda e, r0=r0: e.dma_start(
                        out=(s_x[:].rearrange("p (a m) -> p a m", a=4) if in_ext
                             else s_x[:]),
                        in_=ld_ap(r0)),
                        inc=(ld, 16), waits=[ve, st])
                    for sub in range(4):
                        zc = (ci * 4 + sub) * 1024
                        s.do("vector", lambda e, sub=sub, zc=zc: e.scalar_tensor_tensor(
                            s_x[:, sub * D:(sub + 1) * D], s_zacc[:, zc:zc + 1024],
                            0.5, s_x[:, sub * D:(sub + 1) * D], ALU.mult, ALU.add),
                            inc=(ve, 1), waits=[ld, ve] if sub == 0 else ())
                    s.do("sync", lambda e, r0=r0: e.dma_start(
                        out=st_ap(r0),
                        in_=(s_x[:].rearrange("p (a m) -> p a m", a=4) if out_ext
                             else s_x[:])),
                        inc=(st, 16), waits=[ve])
        return phase

    def phaseMLP(s, pctx):
        psb = lambda n, sh, dt=F32: pctx.enter_context(nc.sbuf_tensor(n, list(sh), dt))
        pps = lambda n, sh: pctx.enter_context(nc.psum_tensor(n, list(sh), F32))
        s_x = psb("m_x", (128, 4 * D))
        s_scr = psb("m_scr", (128, 2 * D))
        s_stats = psb("m_stats", (128, 128))
        s_ain = psb("m_ain", (128, NSUB * D))
        s_ainT = psb("m_ainT", (128, 8 * TPH), BF16)
        s_w1 = psb("m_w1", (128, 8 * DIN), BF16)
        s_w2 = psb("m_w2", (128, 16 * D), BF16)
        s_gatT = psb("m_gatT", (128, 16 * 512), BF16)
        p_up = [pps(f"m_pup{i}", (128, 512)) for i in range(3)]
        p_z = [pps(f"m_pz{i}", (128, 1024)) for i in range(2)]
        trx = {}
        up_slot_ve = {}
        z_slot = {}
        upg = [0]

        s.do("gpsimd", lambda e: e.dma_start(out=s_w1[:], in_=sW1gT_d[:, :]),
             inc=(wld, 16))
        s.do("gpsimd", lambda e: e.dma_start(out=s_w2[:], in_=sW2T_d[:, :]),
             inc=(wld, 16))
        wneed = (wld, wld.vn)
        gidx = [0]
        for hv in range(NH):
            tok0 = hv * TPH
            for ci in range(SPAN):
                r0 = tok0 + ci * 512
                s.do("sync", lambda e, r0=r0: e.dma_start(
                    out=s_x[:], in_=x1_d[:, r0 * 8:r0 * 8 + 4096]),
                    inc=(ld, 16), waits=[ve, st, (mm, mm.vn)])
                emit_ln(s, s_x, s_stats, s_scr,
                        lambda sub, ci=ci: s_ain[:, (ci * 4 + sub) * D:(ci * 4 + sub + 1) * D], ld)
                emit_tr(s, lambda sub, kt, ci=ci: s_ain[
                            :, (ci * 4 + sub) * D + kt * 128:(ci * 4 + sub) * D + (kt + 1) * 128],
                        lambda sub, kt, ci=ci: s_ainT[
                            :, kt * TPH + ci * 512 + sub * 128:
                            kt * TPH + ci * 512 + (sub + 1) * 128],
                        p_z[0], trx, first_wait=(ve, ve.vn),
                        evac_first_wait=(mm, mm.vn))
            z_slot[0] = (ev, ev.vn)
            z_slot[1] = (ev, ev.vn)
            for ci in range(SPAN):
                r0 = tok0 + ci * 512
                for mt in range(16):
                    slot = upg[0] % 3
                    upg[0] += 1
                    w = [ev] if (ci == 0 and mt == 0) else []
                    if hv == 0 and ci == 0 and mt == 0:
                        w.append(wneed)
                    if slot in up_slot_ve:
                        w.append((ve, up_slot_ve[slot]))
                    for kt in range(8):
                        s.do("tensor", lambda e, ci=ci, mt=mt, kt=kt, slot=slot: e.matmul(
                            p_up[slot][:, :],
                            s_w1[:, kt * DIN + mt * 128:kt * DIN + (mt + 1) * 128],
                            s_ainT[:, kt * TPH + ci * 512:kt * TPH + (ci + 1) * 512],
                            start=(kt == 0), stop=False),
                            inc=None, waits=w)
                    s.do("tensor", lambda e, mt=mt, slot=slot: e.matmul(
                        p_up[slot][:, :], s_b1eff[0:1, mt * 128:(mt + 1) * 128],
                        s_ones[0:1, 0:512], start=False, stop=True),
                        inc=(mm, 1))
                    s.do("vector", lambda e, mt=mt, slot=slot: e.tensor_scalar(
                        s_gatT[:, mt * 512:(mt + 1) * 512], p_up[slot][:, :],
                        -1.0, 1.0, ALU.max, ALU.min),
                        inc=(ve, 1), waits=[mm])
                    up_slot_ve[slot] = ve.vn
                for sub in range(4):
                    slot = sub % 2
                    w = [ve]
                    if slot in z_slot:
                        w.append(z_slot[slot])
                    first = True
                    for nh in range(2):
                        for kt in range(16):
                            s.do("tensor", lambda e, sub=sub, nh=nh, kt=kt, slot=slot: e.matmul(
                                p_z[slot][:, nh * 512:(nh + 1) * 512],
                                s_gatT[:, kt * 512 + sub * 128:kt * 512 + (sub + 1) * 128],
                                s_w2[:, kt * 1024 + nh * 512:kt * 1024 + (nh + 1) * 512],
                                start=(kt == 0), stop=False),
                                inc=None, waits=w if first else ())
                            first = False
                        s.do("tensor", lambda e, nh=nh, slot=slot: e.matmul(
                            p_z[slot][:, nh * 512:(nh + 1) * 512],
                            s_ones[0:1, 0:128],
                            s_b2row[0:1, nh * 512:(nh + 1) * 512],
                            start=False, stop=True),
                            inc=(mm, 1) if nh == 1 else None)
                    col = (ci * 4 + sub)
                    s.do("vector", lambda e, sub=sub, col=col, slot=slot: e.tensor_add(
                        s_x[:, sub * D:(sub + 1) * D], p_z[slot][:, :],
                        s_ain[:, col * D:(col + 1) * D]),
                        inc=(ve, 1),
                        waits=[mm, (st, st.vn)] if sub == 0 else [mm])
                    gc = gidx[0]
                    gidx[0] += 1
                    s.do("vector", lambda e, sub=sub, gc=gc: e.reduce_max(
                        s_gcols[:, gc:gc + 1], s_x[:, sub * D:(sub + 1) * D],
                        axis=AX.X, apply_absolute_value=True),
                        inc=(ve, 1), waits=[ve])
                    z_slot[slot] = (ve, ve.vn)
                s.do("sync", lambda e, r0=r0: e.dma_start(
                    out=u1_d[:, r0 * 8:r0 * 8 + 4096], in_=s_x[:]),
                    inc=(st, 16), waits=[ve])
        NG = gidx[0]
        s.do("vector", lambda e: e.memset(s_scr[:, 0:128], 0.0), inc=(ve, 1))
        s.do("vector", lambda e, NG=NG: e.reduce_max(
            s_scr[:, 0:1], s_gcols[:, 0:NG], axis=AX.X), inc=(ve, 1), waits=[ve])
        s.do("tensor", lambda e: e.transpose(
            p_up[0][:, 0:128], s_scr[:, 0:128], s_identf[:, :]),
            inc=(mm, 1), waits=[ve])
        s.do("vector", lambda e: e.reduce_max(
            s_stats[0:1, 0:1], p_up[0][0:1, 0:128], axis=AX.X),
            inc=(ve, 1), waits=[mm])
        s.do("vector", lambda e: e.tensor_single_scalar(
            s_gsc[0:1, 0:16], s_gmaskb[0:1, 0:16], s_stats[0:1, 0:1], ALU.mult),
            inc=(ve, 1), waits=[ve])
        s.do("sync", lambda e: e.dma_start(out=gam_in[:, :], in_=s_gsc[0:1, :]),
             inc=(st, 16), waits=[ve])

    def phaseC(kind, op, groups, src, dst, sem):
        def phase(s, pctx):
            s.do("gpsimd", lambda e: e.collective_compute(
                kind, op, replica_groups=groups,
                ins=[src.ap().opt()], outs=[dst.ap().opt()]),
                inc=(sem, "cc"))
            s.do("gpsimd", lambda e: e.nop(), waits=[sem])
        return phase

    def phaseP2(s, pctx):
        psb = lambda n, sh, dt=F32: pctx.enter_context(nc.sbuf_tensor(n, list(sh), dt))
        s_u1 = psb("q_u1", (128, 4 * D))
        s_x1 = psb("q_x1", (128, 4 * D))
        s_scr = psb("q_scr", (128, 2 * D))
        s_stats = psb("q_stats", (128, 128))
        s_gall = psb("q_gall", (128, 16))
        s_ulnT = psb("q_ulnT", (128, 8 * 512), BF16)
        p_t = pctx.enter_context(nc.psum_tensor("q_pt", [128, 1024], F32))
        trx = {}

        s.do("sync", lambda e: e.dma_start(
            out=s_gall[:], in_=bass.AP(gam_out, 0, [[0, 128], [1, 16]])),
            inc=(ld, 16))
        s.do("vector", lambda e: e.tensor_mul(s_gall[:], s_gall[:], s_gmaskb[:]),
             inc=(ve, 1), waits=[ld])
        s.do("vector", lambda e: e.reduce_sum(s_qscale[:], s_gall[:], axis=AX.X),
             inc=(ve, 1), waits=[ve])
        s.do("vector", lambda e: e.tensor_scalar_add(s_qscale[:], s_qscale[:], EPS),
             inc=(ve, 1), waits=[ve])
        s.do("vector", lambda e: e.reciprocal(s_qscale[:], s_qscale[:]),
             inc=(ve, 1), waits=[ve])
        s.do("vector", lambda e: e.tensor_scalar_mul(s_qscale[:], s_qscale[:], QB),
             inc=(ve, 1), waits=[ve])
        for ci in range(NCH1):
            r0 = ci * 512
            s.do("sync", lambda e, r0=r0: e.dma_start(
                out=s_u1[:], in_=u1_d[:, r0 * 8:r0 * 8 + 4096]),
                inc=(ld, 16), waits=[ve, st, (mm, mm.vn)])
            s.do("sync", lambda e, r0=r0: e.dma_start(
                out=s_x1[:], in_=x1_d[:, r0 * 8:r0 * 8 + 4096]),
                inc=(ld, 16))
            s.do("vector", lambda e: e.tensor_scalar(
                s_u1[:], s_u1[:], s_qscale[:, 0:1], -(QB - EPS), ALU.mult, ALU.max),
                inc=(ve, 1), waits=[ld, ve])
            s.do("vector", lambda e: e.tensor_scalar_min(s_u1[:], s_u1[:], QB - EPS),
                 inc=(ve, 1), waits=[ve])
            s.do("vector", lambda e: e.tensor_add(s_x1[:], s_x1[:], s_u1[:]),
                 inc=(ve, 1), waits=[ve])
            s.do("sync", lambda e, r0=r0: e.dma_start(
                out=x2p_d[:, r0 * 8:r0 * 8 + 4096], in_=s_x1[:]),
                inc=(st, 16), waits=[ve])
            emit_ln(s, s_u1, s_stats, s_scr,
                    lambda sub: s_u1[:, sub * D:(sub + 1) * D], ve)
            emit_tr(s, lambda sub, kt: s_u1[:, sub * D + kt * 128:sub * D + (kt + 1) * 128],
                    lambda sub, kt: s_ulnT[:, kt * 512 + sub * 128:kt * 512 + (sub + 1) * 128],
                    p_t, trx, first_wait=(ve, ve.vn),
                    evac_first_wait=(st, st.vn))
            s.do("sync", lambda e, r0=r0: e.dma_start(
                out=ulnag_in[:, r0 * 8:r0 * 8 + 4096], in_=s_ulnT[:]),
                inc=(st, 16), waits=[ev])

    def phaseP3(s, pctx):
        psb = lambda n, sh, dt=F32: pctx.enter_context(nc.sbuf_tensor(n, list(sh), dt))
        pps = lambda n, sh: pctx.enter_context(nc.psum_tensor(n, list(sh), F32))
        s_ipw = psb("p3_ipw", (128, 8 * 1024), BF16)
        s_uln = [psb(f"p3_uln{i}", (128, 8 * 512), BF16) for i in range(2)]
        s_rng = psb("p3_rng", (128, 4 * 516))
        s_xt = psb("p3_xt", (128, 4 * 512))
        s_u = psb("p3_u", (128, 4 * 512))
        s_ub = psb("p3_ub", (128, 4 * 512), BF16)
        s_sz = psb("p3_sz", (128, 4 * 512))
        s_xda = psb("p3_xda", (96, L))
        p_up = [pps(f"p3_pup{i}", (128, 512)) for i in range(3)]
        p_xp = pps("p3_pxp", (96, 512))
        up_slot_ev = {}
        upg = [0]
        xp_ev = [0]

        s.do("gpsimd", lambda e: e.dma_start(out=s_ipw[:], in_=ipT_d[:, :]),
             inc=(wld, 16))
        wneed = (wld, wld.vn)
        NB = T_loc // 512
        for gi in range(NCH3):
            r, ci3 = gi // NB, gi % NB
            buf = s_uln[gi % 2]
            s.do("sync", lambda e, r=r, ci3=ci3, buf=buf: e.dma_start(
                out=buf[:],
                in_=ulnag_out[r * 128:(r + 1) * 128,
                              ci3 * 4096:(ci3 + 1) * 4096]),
                inc=(ld, 16), waits=[(mm, mm.vn)])
            lneed = (ld, ld.vn)
            for mt in range(8):
                slot = upg[0] % 3
                upg[0] += 1
                w = [wneed, lneed] if mt == 0 else []
                if slot in up_slot_ev:
                    w.append((ev, up_slot_ev[slot]))
                for kt in range(8):
                    s.do("tensor", lambda e, mt=mt, kt=kt, slot=slot, buf=buf: e.matmul(
                        p_up[slot][:, :],
                        s_ipw[:, kt * 1024 + mt * 128:kt * 1024 + (mt + 1) * 128],
                        buf[:, kt * 512:(kt + 1) * 512],
                        start=(kt == 0), stop=(kt == 7)),
                        inc=(mm, 1) if kt == 7 else None, waits=w)
                if mt < 4:
                    s.do("scalar", lambda e, mt=mt, slot=slot: e.copy(
                        s_rng[:, mt * 516 + 3:mt * 516 + 515], p_up[slot][:, :]),
                        inc=(ev, 1), waits=[mm, (ve, ve.vn)])
                else:
                    ct = mt - 4
                    s.do("scalar", lambda e, ct=ct, slot=slot: e.activation(
                        s_sz[:, ct * 512:(ct + 1) * 512], p_up[slot][:, :],
                        AF.Silu, bias=s_ipbz[:, ct:ct + 1]),
                        inc=(ev, 1), waits=[mm, (st, st.vn)])
                up_slot_ev[slot] = ev.vn
            if gi == 0:
                for ct in range(4):
                    s.do("vector", lambda e, ct=ct: e.memset(
                        s_rng[:, ct * 516:ct * 516 + 3], 0.0), inc=(ve, 1))
            for ct in range(4):
                base = ct * 516
                s.do("vector", lambda e, ct=ct, base=base: e.tensor_single_scalar(
                    s_xt[:, ct * 512:(ct + 1) * 512], s_rng[:, base:base + 512],
                    s_convw[:, ct * 4:ct * 4 + 1], ALU.mult),
                    inc=(ve, 1), waits=[ev, ve] if ct == 0 else ())
                for k in range(1, 4):
                    s.do("vector", lambda e, ct=ct, base=base, k=k: e.scalar_tensor_tensor(
                        s_xt[:, ct * 512:(ct + 1) * 512], s_rng[:, base + k:base + k + 512],
                        s_convw[:, ct * 4 + k:ct * 4 + k + 1],
                        s_xt[:, ct * 512:(ct + 1) * 512], ALU.mult, ALU.add),
                        inc=(ve, 1), waits=[ve])
            for ct in range(4):
                s.do("scalar", lambda e, ct=ct: e.copy(
                    s_rng[:, ct * 516:ct * 516 + 3],
                    s_rng[:, ct * 516 + 512:ct * 516 + 515]),
                    inc=(ev, 1), waits=[(ve, ve.vn)] if ct == 0 else ())
            for ct in range(4):
                s.do("scalar", lambda e, ct=ct: e.activation(
                    s_u[:, ct * 512:(ct + 1) * 512], s_xt[:, ct * 512:(ct + 1) * 512],
                    AF.Silu, bias=s_cbeff[:, ct:ct + 1]),
                    inc=(ev, 1), waits=[(ve, ve.vn), (st, st.vn)] if ct == 0 else ())
            for ct in range(4):
                s.do("scalar", lambda e, ct=ct: e.copy(
                    s_ub[:, ct * 512:(ct + 1) * 512], s_u[:, ct * 512:(ct + 1) * 512]),
                    inc=(ev, 1), waits=[(mm, mm.vn), ev] if ct == 0 else ())
            s.do("sync", lambda e, gi=gi: e.dma_start(
                out=u_d.ap().rearrange("(a p) t -> p a t", p=128)[:, :, gi * 512:(gi + 1) * 512],
                in_=s_u[:].rearrange("p (a m) -> p a m", a=4)),
                inc=(st, 16), waits=[ev])
            s.do("sync", lambda e, gi=gi: e.dma_start(
                out=sz_d.ap().rearrange("(a p) t -> p a t", p=128)[:, :, gi * 512:(gi + 1) * 512],
                in_=s_sz[:].rearrange("p (a m) -> p a m", a=4)),
                inc=(st, 16))
            for kt in range(4):
                w = [ev] if kt == 0 else []
                if kt == 0 and xp_ev[0]:
                    w.append((ev, xp_ev[0]))
                s.do("tensor", lambda e, kt=kt: e.matmul(
                    p_xp[:, :], s_xpT[:, kt * 96:(kt + 1) * 96],
                    s_ub[:, kt * 512:(kt + 1) * 512],
                    start=(kt == 0), stop=(kt == 3)),
                    inc=(mm, 1) if kt == 3 else None, waits=w)
            s.do("scalar", lambda e, gi=gi: e.copy(
                s_xda[:, gi * 512:(gi + 1) * 512], p_xp[0:96, :]),
                inc=(ev, 1), waits=[mm])
            xp_ev[0] = ev.vn
        s.do("sync", lambda e: e.dma_start(out=xdbl_in[:, :], in_=s_xda[:]),
             inc=(st, 16), waits=[ev])

    def phaseP4(s, pctx):
        psb = lambda n, sh, dt=F32: pctx.enter_context(nc.sbuf_tensor(n, list(sh), dt))
        pps = lambda n, sh: pctx.enter_context(nc.psum_tensor(n, list(sh), F32))
        s_selb = psb("p4_selb", (32, 32 * 128))
        s_u4 = [psb(f"p4_u{i}", (128, 4 * CH4)) for i in range(2)]
        s_dsz = psb("p4_dsz", (128, 4 * CH4))
        s_y4 = psb("p4_y4", (128, 4 * CH4))
        s_a4 = psb("p4_a4", (128, 4 * CH4))
        s_b4 = psb("p4_b4", (128, 4 * CH4))
        s_h4 = [psb(f"p4_h{i}", (128, CH4)) for i in range(4)]
        s_ym = [psb(f"p4_ym{i}", (128, CH4)) for i in range(4)]
        s_ygb = psb("p4_ygb", (128, 4 * CH4), BF16)
        s_xd4 = [psb(f"p4_xd{i}", (96, CH4)) for i in range(2)]
        s_ab = psb("p4_ab", (128, 2 * 512))
        s_bc = psb("p4_bc", (32, CH4))
        s_crb = psb("p4_crb", (128, CH4))
        s_dtin = psb("p4_dtin", (64, CH4), BF16)
        p_br = [pps(f"p4_pbr{i}", (128, CH4)) for i in range(2)]
        p_cr = [pps(f"p4_pcr{i}", (128, CH4)) for i in range(2)]
        NQ = CH4 // 512

        s.do("sync", lambda e: e.dma_start(out=s_selb[:], in_=selbc[:, :]),
             inc=(ld, 16))
        hinit_done = [0]
        for c4 in range(NCH4):
            ubuf, xdb = s_u4[c4 % 2], s_xd4[c4 % 2]
            s.do("sync", lambda e, c4=c4, ubuf=ubuf: e.dma_start(
                out=ubuf[:].rearrange("p (a m) -> p a m", a=4),
                in_=u_d.ap().rearrange("(a p) t -> p a t", p=128)[
                    :, :, c4 * CH4:(c4 + 1) * CH4]),
                inc=(ld, 16), waits=[(ve, ve.vn)])
            s.do("sync", lambda e, c4=c4, xdb=xdb: e.dma_start(
                out=xdb[:], in_=xdbl_out[:, c4 * CH4:(c4 + 1) * CH4]),
                inc=(ld, 16), waits=[(mm, mm.vn), (ev, ev.vn)])
            lneed = (ld, ld.vn)
            s.do("scalar", lambda e, xdb=xdb: e.copy(s_dtin[:, :], xdb[0:64, :]),
                 inc=(ev, 1), waits=[lneed, (mm, mm.vn)])
            dtin_done = (ev, ev.vn)
            regions = [p_br[0][:, 0:512], p_br[0][:, 512:1024],
                       p_cr[0][:, 0:512], p_cr[0][:, 512:1024]]
            dt_reg_ev = {}
            for mt in range(4):
                for nq in range(NQ):
                    gidx = mt * NQ + nq
                    reg = gidx % 4
                    w = []
                    if gidx == 0:
                        w += [dtin_done, (ve, ve.vn)]
                    if reg in dt_reg_ev:
                        w.append((ev, dt_reg_ev[reg]))
                    s.do("tensor", lambda e, mt=mt, nq=nq, reg=reg: e.matmul(
                        regions[reg], s_dtpT[:, mt * 128:(mt + 1) * 128],
                        s_dtin[:, nq * 512:(nq + 1) * 512],
                        start=True, stop=True),
                        inc=(mm, 1), waits=w)
                    cc0 = mt * CH4 + nq * 512
                    s.do("scalar", lambda e, mt=mt, cc0=cc0, reg=reg: e.activation(
                        s_dsz[:, cc0:cc0 + 512], regions[reg], AF.Relu,
                        bias=s_dtpb[:, mt:mt + 1]),
                        inc=(ev, 1),
                        waits=[mm, (ve, ve.vn), (st, st.vn)] if gidx == 0 else [mm])
                    s.do("scalar", lambda e, mt=mt, reg=reg: e.activation(
                        s_ab[:, 0:512], regions[reg], AF.Abs,
                        bias=s_dtpb[:, mt:mt + 1]),
                        inc=(ev, 1), waits=[(ve, ve.vn)])
                    s.do("scalar", lambda e: e.activation(
                        s_ab[:, 512:1024], s_ab[:, 0:512], AF.Exp, scale=-1.0),
                        inc=(ev, 1), waits=[ev])
                    s.do("scalar", lambda e: e.activation(
                        s_ab[:, 0:512], s_ab[:, 512:1024], AF.Ln, bias=1.0),
                        inc=(ev, 1), waits=[ev])
                    dt_reg_ev[reg] = ev.vn
                    s.do("vector", lambda e, cc0=cc0: e.tensor_add(
                        s_dsz[:, cc0:cc0 + 512], s_dsz[:, cc0:cc0 + 512],
                        s_ab[:, 0:512]),
                        inc=(ve, 1), waits=[ev])
            dt_done_ve = (ve, ve.vn)
            s.do("scalar", lambda e, xdb=xdb: e.copy(s_bc[:, :], xdb[64:96, :]),
                 inc=(ev, 1), waits=[lneed, (mm, mm.vn)])
            bc_done = (ev, ev.vn)
            prev_hinit = hinit_done[0]
            b_done = {}
            ym_done = {}
            scan_done = {}
            ymv = {}
            ya_done = {}
            cr_evd = {}
            for sv in range(S):
                for nq in range(NQ):
                    w = []
                    if sv <= 1 and nq == 0:
                        w += [bc_done, (ev, ev.vn)]
                    if sv >= 2 and nq == 0:
                        w.append((ve, b_done[sv - 2]))
                    s.do("tensor", lambda e, sv=sv, nq=nq: e.matmul(
                        p_br[sv % 2][:, nq * 512:(nq + 1) * 512],
                        s_selb[:, sv * 128:(sv + 1) * 128],
                        s_bc[0:32, nq * 512:(nq + 1) * 512],
                        start=True, stop=True),
                        inc=(mm, 1), waits=w)
                br_mm = (mm, mm.vn)
                for ct in range(4):
                    w = []
                    if (ct, sv - 1) in scan_done:
                        w.append((ve, scan_done[(ct, sv - 1)]))
                    if sv == 0 and ct == 0:
                        w.append(dt_done_ve)
                    s.do("scalar", lambda e, ct=ct, sv=sv: e.activation(
                        s_a4[:, ct * CH4:(ct + 1) * CH4],
                        s_dsz[:, ct * CH4:(ct + 1) * CH4],
                        AF.Exp, scale=s_negA[:, ct * 16 + sv:ct * 16 + sv + 1]),
                        inc=(ev, 1), waits=w)
                a_done = (ev, ev.vn)
                for ct in range(4):
                    w = [br_mm] if ct == 0 else []
                    if sv == 0 and ct == 0:
                        w.append(lneed)
                    s.do("vector", lambda e, ct=ct, sv=sv, ubuf=ubuf: e.tensor_mul(
                        s_b4[:, ct * CH4:(ct + 1) * CH4],
                        ubuf[:, ct * CH4:(ct + 1) * CH4], p_br[sv % 2][:, :]),
                        inc=(ve, 1), waits=w)
                b_done[sv] = ve.vn
                for ct in range(4):
                    col = ct * 16 + sv
                    init = 0.0 if c4 == 0 else s_hinit[:, col:col + 1]
                    w = [a_done] if ct == 0 else []
                    if c4 > 0 and sv == 0 and ct == 0:
                        w.append((ev, prev_hinit))
                    if sv >= 1 and (sv - 1, ct) in ymv:
                        w.append(((ve if ct == 0 else gp), ymv[(sv - 1, ct)]))
                    if ct == 0:
                        w.append(ve)
                    s.do("vector", lambda e, ct=ct, init=init: e.tensor_tensor_scan(
                        s_h4[ct][:, :], s_a4[:, ct * CH4:(ct + 1) * CH4],
                        s_b4[:, ct * CH4:(ct + 1) * CH4], init,
                        ALU.mult, ALU.add),
                        inc=(ve, 1), waits=w)
                    scan_done[(ct, sv)] = ve.vn
                    s.do("scalar", lambda e, ct=ct, col=col: e.copy(
                        s_hinit[:, col:col + 1], s_h4[ct][:, CH4 - 1:CH4]),
                        inc=(ev, 1), waits=[(ve, scan_done[(ct, sv)])])
                for nq in range(NQ):
                    w = []
                    if nq == 0:
                        w.append((ev, cr_evd[sv - 2]) if sv >= 2 else bc_done)
                    s.do("tensor", lambda e, sv=sv, nq=nq: e.matmul(
                        p_cr[sv % 2][:, nq * 512:(nq + 1) * 512],
                        s_selb[:, (16 + sv) * 128:(17 + sv) * 128],
                        s_bc[0:32, nq * 512:(nq + 1) * 512],
                        start=True, stop=True),
                        inc=(mm, 1), waits=w)
                cr_mm = (mm, mm.vn)
                wev = [cr_mm, ev]
                if sv >= 1:
                    wev += [(ve, ymv.get((sv - 1, 0), 0)), (gp, ymv.get((sv - 1, 3), 0))]
                s.do("scalar", lambda e, sv=sv: e.copy(s_crb[:, :], p_cr[sv % 2][:, :]),
                     inc=(ev, 1), waits=wev)
                cr_ev = (ev, ev.vn)
                cr_evd[sv] = ev.vn
                if sv == 0:
                    for ct in range(4):
                        eng = "vector" if ct == 0 else "gpsimd"
                        w = [cr_ev, ve, (gp, gp.vn)] if ct <= 1 else ()
                        s.do(eng, lambda e, ct=ct: e.tensor_mul(
                            s_y4[:, ct * CH4:(ct + 1) * CH4], s_h4[ct][:, :],
                            s_crb[:, :]),
                            inc=(ve, 1) if ct == 0 else (gp, 1), waits=w)
                        ymv[(sv, ct)] = ve.vn if ct == 0 else gp.vn
                else:
                    for ct in range(4):
                        eng = "vector" if ct == 0 else "gpsimd"
                        w = [cr_ev] if ct <= 1 else []
                        if ct == 0:
                            w.append(ve)
                        w.append((gp, ya_done.get((sv - 1, ct), 0)))
                        s.do(eng, lambda e, ct=ct: e.tensor_mul(
                            s_ym[ct][:, :], s_h4[ct][:, :], s_crb[:, :]),
                            inc=(ve, 1) if ct == 0 else (gp, 1), waits=w)
                        ymv[(sv, ct)] = ve.vn if ct == 0 else gp.vn
                    for ct in range(4):
                        sem = ve if ct == 0 else gp
                        s.do("gpsimd", lambda e, ct=ct: e.tensor_add(
                            s_y4[:, ct * CH4:(ct + 1) * CH4],
                            s_y4[:, ct * CH4:(ct + 1) * CH4],
                            s_ym[ct][:, :]),
                            inc=(gp, 1), waits=[(sem, ymv[(sv, ct)]), gp])
                        ya_done[(sv, ct)] = gp.vn
                ym_done[sv] = ve.vn
            hinit_done[0] = ev.vn
            s.do("sync", lambda e, c4=c4: e.dma_start(
                out=s_dsz[:].rearrange("p (a m) -> p a m", a=4),
                in_=sz_d.ap().rearrange("(a p) t -> p a t", p=128)[
                    :, :, c4 * CH4:(c4 + 1) * CH4]),
                inc=(ld, 16), waits=[(ve, ve.vn), (ev, ev.vn)])
            szneed = (ld, ld.vn)
            for ct in range(4):
                s.do("vector", lambda e, ct=ct, ubuf=ubuf: e.scalar_tensor_tensor(
                    s_y4[:, ct * CH4:(ct + 1) * CH4],
                    ubuf[:, ct * CH4:(ct + 1) * CH4],
                    s_Dcs[:, ct:ct + 1], s_y4[:, ct * CH4:(ct + 1) * CH4],
                    ALU.mult, ALU.add),
                    inc=(ve, 1), waits=[ve, gp] if ct == 0 else ())
            for ct in range(4):
                s.do("vector", lambda e, ct=ct: e.tensor_mul(
                    s_ygb[:, ct * CH4:(ct + 1) * CH4],
                    s_y4[:, ct * CH4:(ct + 1) * CH4],
                    s_dsz[:, ct * CH4:(ct + 1) * CH4]),
                    inc=(ve, 1), waits=[szneed, (st, st.vn), ve] if ct == 0 else ())
            s.do("sync", lambda e, c4=c4: e.dma_start(
                out=ygag_in.ap().rearrange("(a p) t -> p a t", p=128)[
                    :, :, c4 * CH4:(c4 + 1) * CH4],
                in_=s_ygb[:].rearrange("p (a m) -> p a m", a=4)),
                inc=(st, 16), waits=[ve])

    def phaseP5a(s, pctx):
        psb = lambda n, sh, dt=F32: pctx.enter_context(nc.sbuf_tensor(n, list(sh), dt))
        pps = lambda n, sh: pctx.enter_context(nc.psum_tensor(n, list(sh), F32))
        s_opw = psb("p5_opw", (128, 16 * 1024), BF16)
        s_yg = [psb(f"p5_yg{i}", (128, 16 * 512), BF16) for i in range(2)]
        s_x2p = psb("p5_x2p", (128, 4 * D))
        p_z = [pps(f"p5_pz{i}", (128, 1024)) for i in range(2)]
        z_slot = {}

        s.do("scalar", lambda e: e.dma_start(
            out=s_opw[:], in_=opbf_d[:, :]),
            inc=(wls, 16))
        wneed = (wls, wls.vn)
        qcell = []

        def ygload(e, ci, buf):
            if not qcell:
                qcell.append(e.snap(nc.partition_id() % 4 * T_loc))
            return e.dma_start(
                out=buf[:].rearrange("p (a m) -> p a m", a=16),
                in_=ygag_out.ap().rearrange("(a p) t -> p a t", p=128)[
                    :, :, bass.ds(qcell[0] + ci * 512, 512)])

        for ci in range(NCH1):
            buf = s_yg[ci % 2]
            s.do("sync", lambda e, ci=ci, buf=buf: ygload(e, ci, buf),
                 inc=(ld, 16), waits=[(mm, mm.vn)])
            s.do("sync", lambda e, ci=ci: e.dma_start(
                out=s_x2p[:], in_=x2p_d[:, ci * 4096:(ci + 1) * 4096]),
                inc=(ld, 16), waits=[(ve, ve.vn), (st, st.vn)])
            lneed = (ld, ld.vn)
            for sub in range(4):
                slot = sub % 2
                w = [lneed, wneed] if sub == 0 else []
                if slot in z_slot:
                    w.append(z_slot[slot])
                first = True
                for nh in range(2):
                    for kt in range(16):
                        s.do("tensor", lambda e, sub=sub, nh=nh, kt=kt, slot=slot, buf=buf: e.matmul(
                            p_z[slot][:, nh * 512:(nh + 1) * 512],
                            buf[:, kt * 512 + sub * 128:kt * 512 + (sub + 1) * 128],
                            s_opw[:, kt * 1024 + nh * 512:kt * 1024 + (nh + 1) * 512],
                            start=(kt == 0), stop=(kt == 15)),
                            inc=(mm, 1) if (nh == 1 and kt == 15) else None,
                            waits=w if first else ())
                        first = False
                s.do("vector", lambda e, sub=sub, slot=slot: e.tensor_add(
                    s_x2p[:, sub * D:(sub + 1) * D], s_x2p[:, sub * D:(sub + 1) * D],
                    p_z[slot][:, :]),
                    inc=(ve, 1), waits=[mm])
                z_slot[slot] = (ve, ve.vn)
            s.do("sync", lambda e, ci=ci: e.dma_start(
                out=x2_d[:, ci * 4096:(ci + 1) * 4096], in_=s_x2p[:]),
                inc=(st, 16), waits=[ve])

    phases = [
        phaseW,
        make_ffn(x_sh, x1_d.ap(), "fa", in_ext=True),
        phaseMLP,
        phaseC("AllReduce", ALU.max, G8, gam_in, gam_out, cc1),
        phaseP2,
        phaseC("AllGather", ALU.bypass, G4, ulnag_in, ulnag_out, cc2),
        phaseP3,
        phaseC("AllReduce", ALU.add, G4, xdbl_in, xdbl_out, cc3),
        phaseP4,
        phaseC("AllGather", ALU.bypass, G4, ygag_in, ygag_out, cc4),
        phaseP5a,
        make_ffn(x2_d.ap(), out_sh, "fb", out_ext=True),
    ]
    np_ = NPHASES if NPHASES is not None else len(phases)
    for ph in phases[:np_]:
        run_phase(ph)

    for sem in SEM.values():
        assert sem.vn < 60000, (sem.name, sem.vn)
    ctx.close()
    return nc


def host_inputs(inputs, L):
    T_loc = L // 4
    g = lambda k: np.asarray(inputs[k])
    x = g("L0_attn_x")
    co = np.ascontiguousarray

    def cs(v, nk):
        return co(v.reshape(nk, 128).T)

    W1 = g("L0_attn_W1")
    W2 = g("L0_attn_W2")

    def tiles(wT, nk):
        K, M = wT.shape
        assert K == nk * 128
        return co(wT.reshape(nk, 128, M).transpose(1, 0, 2).reshape(128, nk * M))

    Ws1 = g("L0_ffn_Ws1")
    base = dict(
        ffn_W1=co(np.stack([tiles(g("L0_ffn_W1")[e], 8) for e in range(NE)])),
        ffn_W2=co(np.stack([tiles(g("L0_ffn_W2")[e], 4) for e in range(NE)])),
        ffn_Ws1a=tiles(Ws1[:, 0:HS], 8),
        ffn_Ws1b=tiles(Ws1[:, HS:2 * HS], 8),
        ffn_Ws2=tiles(g("L0_ffn_Ws2"), 8),
        ffn_WgT=tiles(co(g("L0_ffn_Wg").T), 8),
        aW1T=tiles(co(W1.T), 8),
        aW2T=tiles(co(W2.T), 16),
        b1_row=co(g("L0_attn_b1").reshape(1, DIN)),
        b2_row=co(g("L0_attn_b2").reshape(1, D)),
        b1ln_cs=cs(g("L0_attn_ln1_b"), 8).astype(BF),
        g1_cs=cs(g("L0_attn_ln1_g"), 8),
        b2ln_cs=cs(g("L0_attn_ln2_b"), 8).astype(BF),
        g2_cs=cs(g("L0_attn_ln2_g"), 8),
        opT=tiles(co(g("L0_attn_out_proj_w").T), 16),
        identf=np.eye(128, dtype=np.float32),
        ones_row=np.ones((1, 512), dtype=BF),
    )
    sel16 = np.zeros((16, 16 * 128), dtype=BF)
    for e in range(16):
        sel16[e, e * 128:(e + 1) * 128] = 1.0
    base["sel16"] = sel16
    selbc = np.zeros((32, 32 * 128), dtype=np.float32)
    for j in range(32):
        selbc[j, j * 128:(j + 1) * 128] = 1.0
    base["selbc"] = selbc

    ipw = g("L0_attn_in_proj_w")
    xpw = g("L0_attn_x_proj_w")
    dtw = g("L0_attn_dt_proj_w")
    dtb = g("L0_attn_dt_proj_b")
    cw = g("L0_attn_conv_w")[:, 0, :]
    cb = g("L0_attn_conv_b")
    Al = g("L0_attn_A_log")
    Dp = g("L0_attn_D_param")

    maps = []
    for c in range(NCORES):
        b, q = c // 4, c % 4
        sl = slice(q * 512, (q + 1) * 512)
        ip_sl = np.concatenate([ipw[sl], ipw[DIN + q * 512:DIN + (q + 1) * 512]], axis=0)
        gm = np.zeros((1, 16), dtype=np.float32)
        gm[0, q] = 1.0
        m = dict(base)
        m.update(
            x_sh=co(x[b, q * T_loc:(q + 1) * T_loc]),
            ipT_sl=tiles(co(ip_sl.T), 8),
            xpT_sl=tiles(co(xpw[:, sl].T), 4),
            dtpT_sl=co(dtw[sl].T),
            dtpb_cs=cs(dtb[sl], 4),
            convw_cs=co(cw[sl].reshape(4, 128, 4).transpose(1, 0, 2).reshape(128, 16)),
            convb_cs=cs(cb[sl], 4),
            A_cs=co(Al[sl].reshape(4, 128, 16).transpose(1, 0, 2).reshape(128, 64)),
            D_cs=cs(Dp[sl], 4),
            gmask=gm,
        )
        maps.append(m)
    return maps


def kernel(**inputs):
    L = L_FULL
    nc = build_program(L)
    maps = host_inputs(inputs, L)
    res = run_bass_kernel_spmd(nc, maps, core_ids=list(range(NCORES)))
    T_loc = L // 4
    out = np.empty((2, L, D), dtype=np.float32)
    for c in range(NCORES):
        b, q = c // 4, c % 4
        out[b, q * T_loc:(q + 1) * T_loc] = res.results[c]["out_sh"]
    return out


# version 07
# speedup vs baseline: 1.0031x; 1.0031x over previous
import numpy as np
import ml_dtypes
import contextlib

import concourse.bass as bass
import concourse.mybir as mybir
from concourse.bass_utils import run_bass_kernel_spmd

F32 = mybir.dt.float32
BF16 = mybir.dt.bfloat16
AF = mybir.ActivationFunctionType
ALU = mybir.AluOpType
AX = mybir.AxisListType

D = 1024
DIN = 2048
S = 16
RK = 64
NE = 16
H = 512
HS = 1024
EPS = 1e-5
QB = 128.0
NCORES = 8
BF = ml_dtypes.bfloat16
ENGS = ("sync", "tensor", "vector", "scalar", "gpsimd")
L_FULL = 16384
NPHASES = None
FFN_DBG = None


class DSem:
    def __init__(self, nc, ctx, name):
        self.h = ctx.enter_context(nc.semaphore(name))
        self.vn = 0
        self.name = name


class Script:

    def __init__(self):
        self.q = {e: [] for e in ENGS}

    def do(self, eng, fn, inc=None, waits=(), tag=""):
        wl = []
        for w in waits:
            if isinstance(w, DSem):
                wl.append((w, w.vn))
            else:
                wl.append(w)
        target = None
        if inc is not None:
            sem, k = inc
            sem.vn += 1 if k == "cc" else k
            target = (sem, k)

        def run(e, wl=wl, fn=fn, target=target, tag=tag):
            for (ws, wn) in wl:
                if wn > 0:
                    e.wait_ge(ws.h, wn)
            inst = fn(e)
            if target is not None:
                sem, k = target
                assert inst is not None, f"op with inc returned None ({tag})"
                if k == "cc":
                    inst.then_inc(sem.h)
                else:
                    inst.then_inc(sem.h, k)

        self.q[eng].append(run)

    def emit(self, blk):
        for name in ENGS:
            ops = self.q[name]
            if not ops:
                continue

            def body(e, ops=ops):
                for op in ops:
                    op(e)

            getattr(blk, name)(body)


def build_program(L):
    T_loc = L // 4
    NCH1 = T_loc // 512
    SPAN = min(2, NCH1)
    NH = NCH1 // SPAN
    NCH3 = L // 512
    CH4 = 1024
    NCH4 = L // CH4
    DCH = 512
    TPH = SPAN * 512
    NSUB = SPAN * 4

    nc = bass.Bass("TRN2", target_bir_lowering=False)

    def inp(name, shape, dt=F32):
        return nc.dram_tensor(name, list(shape), dt, kind="ExternalInput").ap()

    x_sh = inp("x_sh", (T_loc, D))
    ffn_W1 = inp("ffn_W1", (NE, 128, 8 * 1024))
    ffn_W2 = inp("ffn_W2", (NE, 128, 4 * 1024))
    ffn_Ws1a = inp("ffn_Ws1a", (128, 8 * 1024))
    ffn_Ws1b = inp("ffn_Ws1b", (128, 8 * 1024))
    ffn_Ws2 = inp("ffn_Ws2", (128, 8 * 1024))
    ffn_WgT = inp("ffn_WgT", (128, 8 * 16))
    aW1T = inp("aW1T", (128, 8 * DIN))
    aW2T = inp("aW2T", (128, 16 * D))
    b1_row = inp("b1_row", (1, DIN))
    b2_row = inp("b2_row", (1, D))
    b1ln_cs = inp("b1ln_cs", (128, 8), BF16)
    g1_cs = inp("g1_cs", (128, 8))
    b2ln_cs = inp("b2ln_cs", (128, 8), BF16)
    g2_cs = inp("g2_cs", (128, 8))
    ipT_sl = inp("ipT_sl", (128, 8 * 1024))
    opT = inp("opT", (128, 16 * 1024))
    xpT_sl = inp("xpT_sl", (128, 4 * 96))
    dtpT_sl = inp("dtpT_sl", (RK, DCH))
    dtpb_cs = inp("dtpb_cs", (128, 4))
    convw_cs = inp("convw_cs", (128, 16))
    convb_cs = inp("convb_cs", (128, 4))
    A_cs = inp("A_cs", (128, 64))
    D_cs = inp("D_cs", (128, 4))
    identf = inp("identf", (128, 128))
    ones_row = inp("ones_row", (1, 512), BF16)
    sel16 = inp("sel16", (16, 16 * 128), BF16)
    selbc = inp("selbc", (32, 32 * 128))
    gmask = inp("gmask", (1, 16))

    out_sh = nc.dram_tensor("out_sh", [T_loc, D], F32, kind="ExternalOutput").ap()

    def scr(name, shape, dt=F32):
        return nc.dram_tensor(name, list(shape), dt)

    x1_d = scr("x1_d", (128, 8 * T_loc))
    u1_d = scr("u1_d", (128, 8 * T_loc))
    x2p_d = scr("x2p_d", (128, 8 * T_loc))
    x2_d = scr("x2_d", (128, 8 * T_loc))
    u_d = scr("u_d", (DCH, L))
    sz_d = scr("sz_d", (DCH, L))
    sW1gT_d = scr("sW1gT_d", (128, 8 * DIN), BF16)
    w1bf_d = scr("w1bf_d", (128, NE * 8 * 1024), BF16)
    w2bf_d = scr("w2bf_d", (128, NE * 4 * 1024), BF16)
    ws1a_d = scr("ws1a_d", (128, 8 * 1024), BF16)
    ws1b_d = scr("ws1b_d", (128, 8 * 1024), BF16)
    ws2_d = scr("ws2_d", (128, 8 * 1024), BF16)
    opbf_d = scr("opbf_d", (128, 16 * 1024), BF16)
    sW2T_d = scr("sW2T_d", (128, 16 * D), BF16)
    ipT_d = scr("ipT_d", (128, 8 * 1024), BF16)
    gam_in = scr("gam_in", (1, 16))
    gam_out = scr("gam_out", (1, 16))
    ulnag_in = scr("ulnag_in", (128, 8 * T_loc), BF16)
    ulnag_out = scr("ulnag_out", (4 * 128, 8 * T_loc), BF16)
    xdbl_in = scr("xdbl_in", (96, L))
    xdbl_out = scr("xdbl_out", (96, L))
    ygag_in = scr("ygag_in", (DCH, L), BF16)
    ygag_out = scr("ygag_out", (DIN, L), BF16)

    G4 = [[0, 1, 2, 3], [4, 5, 6, 7]]
    G8 = [list(range(NCORES))]

    ctx = contextlib.ExitStack()
    sb = lambda name, shape, dt=F32: ctx.enter_context(nc.sbuf_tensor(name, list(shape), dt))

    s_identf = sb("s_identf", (128, 128))
    s_ones = sb("s_ones", (1, 512), BF16)
    s_sel16 = sb("s_sel16", (16, 16 * 128), BF16)
    s_b1eff = sb("s_b1eff", (1, DIN), BF16)
    s_b2row = sb("s_b2row", (1, D), BF16)
    s_WgT = sb("s_WgT", (128, 8 * 16), BF16)
    s_qscale = sb("s_qscale", (128, 1))
    s_gmaskb = sb("s_gmaskb", (128, 16))
    s_negA = sb("s_negA", (128, 64))
    s_Dcs = sb("s_Dcs", (128, 4))
    s_cbeff = sb("s_cbeff", (128, 4))
    s_ipbz = sb("s_ipbz", (128, 4))
    s_dtpb = sb("s_dtpb", (128, 4))
    s_convw = sb("s_convw", (128, 16))
    s_xpT = sb("s_xpT", (128, 4 * 96), BF16)
    s_dtpT = sb("s_dtpT", (64, 4 * 128), BF16)
    s_hinit = sb("s_hinit", (128, 64))
    s_gsc = sb("s_gsc", (1, 16))
    s_gcols = sb("s_gcols", (128, 64))

    SEM = {}
    for name in ["ld", "wld", "wls", "st", "mm", "ev", "ve", "gp", "cc1", "cc2", "cc3", "cc4"]:
        SEM[name] = DSem(nc, ctx, name)
    ld, wld, st = SEM["ld"], SEM["wld"], SEM["st"]
    wls = SEM["wls"]
    gp = SEM["gp"]
    mm, ev, ve = SEM["mm"], SEM["ev"], SEM["ve"]
    cc1, cc2, cc3, cc4 = SEM["cc1"], SEM["cc2"], SEM["cc3"], SEM["cc4"]

    def run_phase(builder):
        with nc.Block() as blk, contextlib.ExitStack() as pctx:
            s = Script()
            builder(s, pctx)
            s.do("sync", lambda e: e.nop(), waits=[st, wld, wls])
            s.emit(blk)
        nc.all_engine_barrier()

    def emit_ln(s, s_x, s_stats, s_scr, norm_out, wait_loaded):
        for sub in range(4):
            s.do("scalar", lambda e, sub=sub: e.activation(
                s_scr[:, 0:D], s_x[:, sub * D:(sub + 1) * D], AF.Square,
                accum_out=s_stats[:, 16 + sub:17 + sub]),
                inc=(ev, 1), waits=[wait_loaded, ev] if sub == 0 else [ev])
            s.do("scalar", lambda e, sub=sub: e.activation(
                s_scr[:, D:2 * D], s_x[:, sub * D:(sub + 1) * D], AF.Copy,
                accum_out=s_stats[:, sub:sub + 1]),
                inc=(ev, 1), waits=[ev])
        ST = s_stats
        s.do("vector", lambda e: e.tensor_scalar_mul(ST[:, 32:36], ST[:, 0:4], 1.0 / D),
             inc=(ve, 1), waits=[ev])
        s.do("vector", lambda e: e.scalar_tensor_tensor(
            ST[:, 48:52], ST[:, 32:36], -1.0, ST[:, 32:36], ALU.mult, ALU.mult),
            inc=(ve, 1), waits=[ve])
        s.do("vector", lambda e: e.scalar_tensor_tensor(
            ST[:, 64:68], ST[:, 16:20], 1.0 / D, ST[:, 48:52], ALU.mult, ALU.add),
            inc=(ve, 1), waits=[ve])
        s.do("vector", lambda e: e.tensor_scalar_add(ST[:, 80:84], ST[:, 64:68], EPS),
             inc=(ve, 1), waits=[ve])
        s.do("scalar", lambda e: e.activation(ST[:, 64:68], ST[:, 80:84], AF.Ln),
             inc=(ev, 1), waits=[ve])
        s.do("scalar", lambda e: e.activation(ST[:, 96:100], ST[:, 64:68], AF.Exp,
                                              scale=-0.5), inc=(ev, 1), waits=[ev])
        s.do("vector", lambda e: e.scalar_tensor_tensor(
            ST[:, 112:116], ST[:, 32:36], -1.0, ST[:, 96:100], ALU.mult, ALU.mult),
            inc=(ve, 1), waits=[ev])
        for sub in range(4):
            s.do("vector", lambda e, sub=sub: e.tensor_scalar(
                norm_out(sub), s_x[:, sub * D:(sub + 1) * D],
                ST[:, 96 + sub:97 + sub], ST[:, 112 + sub:113 + sub],
                ALU.mult, ALU.add),
                inc=(ve, 1), waits=[ve] if sub == 0 else ())

    def emit_tr(s, src_ap, dst_col, p4, slots, first_wait=None, evac_first_wait=None):
        for sub in range(4):
            for kt in range(8):
                g = sub * 8 + kt
                slot = g % 2
                w = []
                if first_wait is not None and g == 0:
                    w.append(first_wait)
                if slot in slots:
                    w.append((ev, slots[slot]))
                s.do("tensor", lambda e, sub=sub, kt=kt, slot=slot: e.transpose(
                    p4[:, slot * 512:slot * 512 + 128],
                    src_ap(sub, kt), s_identf[:, :]),
                    inc=(mm, 1), waits=w)
                ew = [mm]
                if evac_first_wait is not None and g == 0:
                    ew.append(evac_first_wait)
                s.do("scalar", lambda e, sub=sub, kt=kt, slot=slot: e.copy(
                    dst_col(sub, kt), p4[:, slot * 512:slot * 512 + 128]),
                    inc=(ev, 1), waits=ew)
                slots[slot] = ev.vn

    def phaseW(s, pctx):
        psb = lambda n, sh, dt=F32: pctx.enter_context(nc.sbuf_tensor(n, list(sh), dt))
        s_w1u = psb("w_w1u", (128, 8 * DIN), BF16)
        s_ipTu = psb("w_ipTu", (128, 8 * 1024), BF16)
        s_sc2T = psb("w_sc2T", (128, 16 * D), BF16)
        s_b1c = psb("w_b1c", (128, 8), BF16)
        s_b2c = psb("w_b2c", (128, 8), BF16)
        s_g1 = psb("w_g1", (128, 8))
        s_g2 = psb("w_g2", (128, 8))
        s_bacc = psb("w_bacc", (1, DIN))
        s_b1raw = psb("w_b1raw", (1, DIN))
        s_ipbrow = psb("w_ipbrow", (1, 1024))
        s_tmpc = psb("w_tmpc", (128, 16))
        s_w1g = psb("w_w1g", (128, 8 * DIN), BF16)
        p_b = pctx.enter_context(nc.psum_tensor("w_pb", [1, 512], F32))
        p_t = pctx.enter_context(nc.psum_tensor("w_pt", [128, 8], F32))

        for o, i in [
            (s_identf[:], identf[:, :]), (s_ones[:], ones_row[:, :]),
            (s_sel16[:], sel16[:, :]),
            (s_g1[:], g1_cs[:, :]), (s_g2[:], g2_cs[:, :]),
            (s_negA[:], A_cs[:, :]), (s_Dcs[:], D_cs[:, :]),
            (s_dtpb[:], dtpb_cs[:, :]), (s_convw[:], convw_cs[:, :]),
            (s_cbeff[:], convb_cs[:, :]), (s_b1raw[:], b1_row[:, :]),
            (s_gmaskb[:], bass.AP(gmask.tensor, 0, [[0, 128], [1, 16]])),
            (s_b1c[:], b1ln_cs[:, :]), (s_b2c[:], b2ln_cs[:, :]),
        ]:
            s.do("sync", lambda e, o=o, i=i: e.dma_start(out=o, in_=i), inc=(ld, 16))
        for o, i in [
            (s_w1u[:], aW1T[:, :]),
            (s_sc2T[:], aW2T[:, :]),
            (s_ipTu[:], ipT_sl[:, :]),
            (s_WgT[:], ffn_WgT[:, :]),
            (s_xpT[:], xpT_sl[:, :]),
            (s_dtpT[:], dtpT_sl[:, :]),
            (s_b2row[:], b2_row[:, :]),
        ]:
            s.do("gpsimd", lambda e, o=o, i=i: e.dma_start(out=o, in_=i), inc=(wld, 16))
        for o, i in [
            (w1bf_d.ap().rearrange("p (e m) -> p e m", e=NE),
             ffn_W1.rearrange("e p m -> p e m")),
            (w2bf_d.ap().rearrange("p (e m) -> p e m", e=NE),
             ffn_W2.rearrange("e p m -> p e m")),
            (ws1a_d.ap(), ffn_Ws1a[:, :]),
            (ws1b_d.ap(), ffn_Ws1b[:, :]),
            (ws2_d.ap(), ffn_Ws2[:, :]),
            (opbf_d.ap(), opT[:, :]),
        ]:
            s.do("gpsimd", lambda e, o=o, i=i: e.dma_start(out=o, in_=i), inc=(wld, 16))

        s.do("scalar", lambda e: e.activation(s_w1u[:], s_w1u[:], AF.Sign),
             inc=(ev, 1), waits=[ld, wld])
        s.do("scalar", lambda e: e.activation(s_sc2T[:], s_sc2T[:], AF.Sign),
             inc=(ev, 1))
        s.do("scalar", lambda e: e.activation(s_negA[:], s_negA[:], AF.Exp),
             inc=(ev, 1))
        s.do("vector", lambda e: e.tensor_scalar_mul(s_negA[:], s_negA[:], -1.0),
             inc=(ve, 1), waits=[ev])
        for c in range(4):
            if c > 0:
                s.do("tensor", lambda e: e.nop(), waits=[ev])
            for k in range(8):
                s.do("tensor", lambda e, c=c, k=k: e.matmul(
                    p_b[:, :], s_b1c[:, k:k + 1],
                    s_w1u[:, k * DIN + c * 512:k * DIN + (c + 1) * 512],
                    start=(k == 0), stop=(k == 7)),
                    inc=(mm, 1) if k == 7 else None,
                    waits=[ev, ve] if (c == 0 and k == 0) else ())
            s.do("scalar", lambda e, c=c: e.copy(
                s_bacc[0:1, c * 512:(c + 1) * 512], p_b[0:1, :]), inc=(ev, 1), waits=[mm])
        for c in range(2):
            s.do("tensor", lambda e: e.nop(), waits=[ev])
            for k in range(8):
                s.do("tensor", lambda e, c=c, k=k: e.matmul(
                    p_b[:, :], s_b2c[:, k:k + 1],
                    s_ipTu[:, k * 1024 + c * 512:k * 1024 + (c + 1) * 512],
                    start=(k == 0), stop=(k == 7)),
                    inc=(mm, 1) if k == 7 else None)
            s.do("scalar", lambda e, c=c: e.copy(
                s_ipbrow[0:1, c * 512:(c + 1) * 512], p_b[0:1, :]), inc=(ev, 1), waits=[mm])
        s.do("vector", lambda e: e.tensor_add(s_bacc[0:1, :], s_bacc[0:1, :],
                                              s_b1raw[0:1, :]), inc=(ve, 1), waits=[ev])
        s.do("vector", lambda e: e.tensor_copy(s_b1eff[0:1, :], s_bacc[0:1, :]),
             inc=(ve, 1), waits=[ve])
        for k in range(8):
            s.do("scalar", lambda e, k=k: e.mul(
                s_w1g[:, k * DIN:(k + 1) * DIN], s_w1u[:, k * DIN:(k + 1) * DIN],
                s_g1[:, k:k + 1]), inc=(ev, 1), waits=[ev] if k == 0 else ())
        for k in range(8):
            s.do("scalar", lambda e, k=k: e.mul(
                s_ipTu[:, k * 1024:(k + 1) * 1024], s_ipTu[:, k * 1024:(k + 1) * 1024],
                s_g2[:, k:k + 1]), inc=(ev, 1))
        s.do("sync", lambda e: e.dma_start(out=sW1gT_d[:, :], in_=s_w1g[:]),
             inc=(st, 16), waits=[ev])
        s.do("sync", lambda e: e.dma_start(out=sW2T_d[:, :], in_=s_sc2T[:]),
             inc=(st, 16))
        s.do("sync", lambda e: e.dma_start(out=ipT_d[:, :], in_=s_ipTu[:]),
             inc=(st, 16))
        for h8 in range(8):
            s.do("tensor", lambda e, h=h8: e.transpose(
                p_t[:, h:h + 1], s_ipbrow[0:1, h * 128:(h + 1) * 128],
                s_identf[0:1, 0:1]),
                inc=(mm, 1) if h8 == 7 else None, waits=[ev] if h8 == 0 else ())
        s.do("vector", lambda e: e.tensor_add(s_tmpc[:, 0:4], s_convw[:, 0:16:4],
                                              s_convw[:, 1:16:4]), inc=(ve, 1), waits=[mm])
        s.do("vector", lambda e: e.tensor_add(s_tmpc[:, 4:8], s_convw[:, 2:16:4],
                                              s_convw[:, 3:16:4]), inc=(ve, 1))
        s.do("vector", lambda e: e.tensor_add(s_tmpc[:, 8:12], s_tmpc[:, 0:4],
                                              s_tmpc[:, 4:8]), inc=(ve, 1), waits=[ve])
        s.do("vector", lambda e: e.tensor_mul(s_tmpc[:, 12:16], p_t[:, 0:4],
                                              s_tmpc[:, 8:12]), inc=(ve, 1), waits=[ve])
        s.do("vector", lambda e: e.tensor_add(s_cbeff[:, 0:4], s_cbeff[:, 0:4],
                                              s_tmpc[:, 12:16]), inc=(ve, 1), waits=[ve])
        s.do("vector", lambda e: e.tensor_copy(s_ipbz[:, 0:4], p_t[:, 4:8]), inc=(ve, 1))

    def make_ffn(in_dram, out_dram, tag, in_ext=False, out_ext=False):
        def ld_ap(r0):
            if in_ext:
                return in_dram[r0:r0 + 512, :].rearrange("(a p) m -> p a m", p=128)
            return in_dram[:, r0 * 8:r0 * 8 + 4096]

        def st_ap(r0):
            if out_ext:
                return out_dram[r0:r0 + 512, :].rearrange("(a p) m -> p a m", p=128)
            return out_dram[:, r0 * 8:r0 * 8 + 4096]
        def phase(s, pctx):
            psb = lambda n, sh, dt=F32: pctx.enter_context(nc.sbuf_tensor(tag + n, list(sh), dt))
            pps = lambda n, sh: pctx.enter_context(nc.psum_tensor(tag + n, list(sh), F32))
            s_x = psb("f_x", (128, 4 * D))
            s_scr = psb("f_scr", (128, 2 * D))
            s_stats = psb("f_stats", (128, 128))
            s_xnT = psb("f_xnT", (128, 8 * TPH), BF16)
            s_sc = psb("f_sc", (128, NSUB * 16))
            s_es = psb("f_es", (128, NSUB * 16))
            s_msk = psb("f_msk", (128, NSUB * 16))
            s_cw = psb("f_cw", (128, NSUB * 16))
            s_cwT = psb("f_cwT", (16, TPH), BF16)
            s_w1e = [psb(f"f_w1e{i}", (128, 8 * 1024), BF16) for i in range(2)]
            s_w2e = [psb("f_w2e0", (128, 8 * 1024), BF16),
                     psb("f_w2e1", (128, 4 * 1024), BF16)]
            s_hu = psb("f_hu", (128, 2 * 8 * 512), BF16)
            s_sg = psb("f_sg", (128, 2 * 8 * 512), BF16)
            s_hgT = psb("f_hgT", (128, 2 * 8 * 512), BF16)
            s_cwb = psb("f_cwb", (128, 2 * 512), BF16)
            s_zacc = psb("f_zacc", (128, NSUB * 1024))
            p_up = [pps(f"f_pup{i}", (128, 512)) for i in range(3)]
            p_cw = pps("f_pcw", (128, 512))
            p_z = [pps(f"f_pz{i}", (128, 1024)) for i in range(2)]

            up_slot_ev = {}
            z_slot = {}
            cw_ev = [0]
            trx = {}
            trc = {}
            upg = [0]

            def up_pass(ci, wbuf, dst, first_waits):
                for mt in range(8):
                    slot = upg[0] % 3
                    upg[0] += 1
                    w = list(first_waits) if mt == 0 else []
                    if slot in up_slot_ev:
                        w.append((ev, up_slot_ev[slot]))
                    for kt in range(8):
                        s.do("tensor", lambda e, ci=ci, mt=mt, kt=kt, slot=slot, wbuf=wbuf: e.matmul(
                            p_up[slot][:, :],
                            wbuf[:, kt * 1024 + mt * 128:kt * 1024 + (mt + 1) * 128],
                            s_xnT[:, kt * TPH + ci * 512:kt * TPH + (ci + 1) * 512],
                            start=(kt == 0), stop=(kt == 7)),
                            inc=(mm, 1) if kt == 7 else None, waits=w)
                    kind, ap = dst(mt)
                    if kind == "copy":
                        s.do("scalar", lambda e, ap=ap, slot=slot: e.copy(
                            ap, p_up[slot][:, :]), inc=(ev, 1), waits=[mm])
                    else:
                        s.do("scalar", lambda e, ap=ap, slot=slot: e.activation(
                            ap, p_up[slot][:, :], AF.Silu), inc=(ev, 1), waits=[mm])
                    up_slot_ev[slot] = ev.vn

            def down_pass(ci, cpar, w2buf, nkt, zcopy):
                for sub in range(4):
                    slot = sub % 2
                    w = [(ve, ve.vn)] if sub == 0 else []
                    if slot in z_slot:
                        w.append(z_slot[slot])
                    first = True
                    for nh in range(2):
                        for kt in range(nkt):
                            s.do("tensor", lambda e, sub=sub, nh=nh, kt=kt, slot=slot, w2buf=w2buf, cpar=cpar: e.matmul(
                                p_z[slot][:, nh * 512:(nh + 1) * 512],
                                s_hgT[:, cpar * 4096 + kt * 512 + sub * 128:
                                      cpar * 4096 + kt * 512 + (sub + 1) * 128],
                                w2buf[:, kt * 1024 + nh * 512:kt * 1024 + (nh + 1) * 512],
                                start=(kt == 0), stop=(kt == nkt - 1)),
                                inc=(mm, 1) if (nh == 1 and kt == nkt - 1) else None,
                                waits=w if first else ())
                            first = False
                    zc = (ci * 4 + sub) * 1024
                    if zcopy:
                        s.do("vector", lambda e, zc=zc, slot=slot: e.tensor_copy(
                            s_zacc[:, zc:zc + 1024], p_z[slot][:, :]),
                            inc=(ve, 1), waits=[mm, st])
                    else:
                        s.do("vector", lambda e, zc=zc, slot=slot: e.tensor_add(
                            s_zacc[:, zc:zc + 1024], s_zacc[:, zc:zc + 1024],
                            p_z[slot][:, :]),
                            inc=(ve, 1), waits=[mm])
                    z_slot[slot] = (ve, ve.vn)

            def gate_pass(ci, cpar, n, with_cw):
                for mt in range(n):
                    w = [ev, mm] if mt == 0 else []
                    s.do("vector", lambda e, mt=mt, cpar=cpar: e.tensor_mul(
                        s_hgT[:, cpar * 4096 + mt * 512:cpar * 4096 + (mt + 1) * 512],
                        s_hu[:, cpar * 4096 + mt * 512:cpar * 4096 + (mt + 1) * 512],
                        s_sg[:, cpar * 4096 + mt * 512:cpar * 4096 + (mt + 1) * 512]),
                        inc=(ve, 1), waits=w)
                if with_cw:
                    for mt in range(n):
                        s.do("vector", lambda e, mt=mt, cpar=cpar: e.tensor_mul(
                            s_hgT[:, cpar * 4096 + mt * 512:cpar * 4096 + (mt + 1) * 512],
                            s_hgT[:, cpar * 4096 + mt * 512:cpar * 4096 + (mt + 1) * 512],
                            s_cwb[:, cpar * 512:(cpar + 1) * 512]),
                            inc=(ve, 1), waits=[ve] if mt == 0 else ())

            def cwrep(ci, cpar, ei):
                s.do("tensor", lambda e, ci=ci, ei=ei: e.matmul(
                    p_cw[:, :], s_sel16[:, ei * 128:(ei + 1) * 128],
                    s_cwT[0:16, ci * 512:(ci + 1) * 512],
                    start=True, stop=True),
                    inc=(mm, 1), waits=[(ev, cw_ev[0])])
                s.do("scalar", lambda e, cpar=cpar: e.copy(
                    s_cwb[:, cpar * 512:(cpar + 1) * 512], p_cw[:, :]),
                    inc=(ev, 1), waits=[mm, (ve, ve.vn)])
                cw_ev[0] = ev.vn

            dbg = FFN_DBG or {}
            nhv = dbg.get("nh", NH)
            nexp = dbg.get("ne", NE)
            do_shared = dbg.get("sh", True)
            do_epi = dbg.get("epi", True)
            do_rout = dbg.get("rout", True)
            do_exp = dbg.get("exp", True)
            hv_prev_mm = 0
            for hv in range(nhv):
                mmafter = {}
                tok0 = hv * TPH
                for ci in range(SPAN):
                    r0 = tok0 + ci * 512
                    s.do("sync", lambda e, r0=r0: e.dma_start(
                        out=(s_x[:].rearrange("p (a m) -> p a m", a=4) if in_ext
                             else s_x[:]),
                        in_=ld_ap(r0)),
                        inc=(ld, 16), waits=[ve, st, (mm, mm.vn)])
                    if dbg.get("ln", True):
                        emit_ln(s, s_x, s_stats, s_scr,
                                lambda sub: s_x[:, sub * D:(sub + 1) * D], ld)
                    if dbg.get("tr", True):
                        emit_tr(s, lambda sub, kt: s_x[:, sub * D + kt * 128:sub * D + (kt + 1) * 128],
                                lambda sub, kt, ci=ci: s_xnT[
                                    :, kt * TPH + ci * 512 + sub * 128:
                                    kt * TPH + ci * 512 + (sub + 1) * 128],
                                p_z[0], trx, first_wait=(ve, ve.vn),
                                evac_first_wait=(mm, mm.vn))
                if not do_rout:
                    continue
                for g in range(NSUB):
                    ci, sub = g // 4, g % 4
                    for kt in range(8):
                        w = [ev, (ev, cw_ev[0])] if (g == 0 and kt == 0) else []
                        s.do("tensor", lambda e, ci=ci, sub=sub, g=g, kt=kt: e.matmul(
                            p_cw[:, g * 16:(g + 1) * 16],
                            s_xnT[:, kt * TPH + ci * 512 + sub * 128:
                                  kt * TPH + ci * 512 + (sub + 1) * 128],
                            s_WgT[:, kt * 16:(kt + 1) * 16],
                            start=(kt == 0), stop=(kt == 7)),
                            inc=(mm, 1) if kt == 7 else None, waits=w)
                NCOL = NSUB * 16
                s.do("scalar", lambda e: e.copy(s_sc[:, 0:NCOL], p_cw[:, 0:NCOL]),
                     inc=(ev, 1), waits=[mm])
                cw_ev[0] = ev.vn
                for g in range(NSUB):
                    s.do("vector", lambda e, g=g: e.reduce_max(
                        s_stats[:, g:g + 1], s_sc[:, g * 16:(g + 1) * 16], axis=AX.X),
                        inc=(ve, 1), waits=[ev] if g == 0 else ())
                s.do("vector", lambda e: e.tensor_scalar_mul(
                    s_stats[:, 16:16 + NSUB], s_stats[:, 0:NSUB], -1.0),
                    inc=(ve, 1), waits=[ve])
                for g in range(NSUB):
                    s.do("scalar", lambda e, g=g: e.activation(
                        s_es[:, g * 16:(g + 1) * 16], s_sc[:, g * 16:(g + 1) * 16],
                        AF.Exp, bias=s_stats[:, 16 + g:17 + g],
                        accum_out=s_stats[:, 32 + g:33 + g]),
                        inc=(ev, 1), waits=[ve] if g == 0 else ())
                s.do("vector", lambda e: e.reciprocal(
                    s_stats[:, 48:48 + NSUB], s_stats[:, 32:32 + NSUB]),
                    inc=(ve, 1), waits=[ev])
                for g in range(NSUB):
                    s.do("vector", lambda e, g=g: e.tensor_single_scalar(
                        s_msk[:, g * 16:(g + 1) * 16], s_sc[:, g * 16:(g + 1) * 16],
                        s_stats[:, g:g + 1], ALU.is_ge),
                        inc=(ve, 1), waits=[ve] if g == 0 else ())
                s.do("vector", lambda e: e.scalar_tensor_tensor(
                    s_cw[:, 0:NCOL], s_msk[:, 0:NCOL], -1e30, s_sc[:, 0:NCOL],
                    ALU.mult, ALU.add), inc=(ve, 1), waits=[ve])
                for g in range(NSUB):
                    s.do("vector", lambda e, g=g: e.reduce_max(
                        s_stats[:, 64 + g:65 + g], s_cw[:, g * 16:(g + 1) * 16], axis=AX.X),
                        inc=(ve, 1), waits=[ve] if g == 0 else ())
                for g in range(NSUB):
                    s.do("vector", lambda e, g=g: e.tensor_single_scalar(
                        s_cw[:, g * 16:(g + 1) * 16], s_sc[:, g * 16:(g + 1) * 16],
                        s_stats[:, 64 + g:65 + g], ALU.is_ge),
                        inc=(ve, 1), waits=[ve] if g == 0 else ())
                s.do("vector", lambda e: e.tensor_max(
                    s_msk[:, 0:NCOL], s_msk[:, 0:NCOL], s_cw[:, 0:NCOL]),
                    inc=(ve, 1), waits=[ve])
                s.do("vector", lambda e: e.tensor_mul(
                    s_cw[:, 0:NCOL], s_es[:, 0:NCOL], s_msk[:, 0:NCOL]),
                    inc=(ve, 1), waits=[ve])
                for g in range(NSUB):
                    s.do("vector", lambda e, g=g: e.tensor_single_scalar(
                        s_cw[:, g * 16:(g + 1) * 16], s_cw[:, g * 16:(g + 1) * 16],
                        s_stats[:, 48 + g:49 + g], ALU.mult),
                        inc=(ve, 1), waits=[ve] if g == 0 else ())
                for g in range(NSUB):
                    slot = g % 2
                    w = [ve] if g == 0 else []
                    if slot in trc:
                        w.append((ev, trc[slot]))
                    s.do("tensor", lambda e, g=g, slot=slot: e.transpose(
                        p_z[1][0:16, slot * 512:slot * 512 + 128],
                        s_cw[:, g * 16:(g + 1) * 16], s_identf[:, :]),
                        inc=(mm, 1), waits=w)
                    s.do("scalar", lambda e, g=g, slot=slot: e.copy(
                        s_cwT[0:16, g * 128:(g + 1) * 128],
                        p_z[1][0:16, slot * 512:slot * 512 + 128]),
                        inc=(ev, 1), waits=[mm])
                    trc[slot] = ev.vn
                cw_ev[0] = ev.vn
                z_slot[0] = (ev, ev.vn)
                z_slot[1] = (ev, ev.vn)

                if not do_exp:
                    continue
                for ei in range(nexp):
                    par = ei % 2
                    bfree = (mm, mmafter.get(ei - 2, hv_prev_mm))
                    s.do("scalar", lambda e, ei=ei, par=par: e.dma_start(
                        out=s_w1e[par][:],
                        in_=w1bf_d[:, ei * 8192:(ei + 1) * 8192]),
                        inc=(wls, 16), waits=[bfree])
                    s.do("scalar", lambda e, ei=ei, par=par: e.dma_start(
                        out=s_w2e[par][:, 0:4 * 1024],
                        in_=w2bf_d[:, ei * 4096:(ei + 1) * 4096]),
                        inc=(wls, 16))
                    wneed = (wls, wls.vn)
                    for ci in range(SPAN):
                        cpar = ci % 2
                        dstx = (lambda mt, cpar=cpar:
                                ("copy", s_hu[:, cpar * 4096 + mt * 512:cpar * 4096 + (mt + 1) * 512])
                                if mt < 4 else
                                ("silu", s_sg[:, cpar * 4096 + (mt - 4) * 512:cpar * 4096 + (mt - 3) * 512]))
                        up_pass(ci, s_w1e[par], dstx, [wneed] if ci == 0 else [])
                        cwrep(ci, cpar, ei)
                        gate_pass(ci, cpar, 4, True)
                        if ci >= 1:
                            down_pass(ci - 1, (ci - 1) % 2, s_w2e[par], 4, (ei == 0))
                    down_pass(SPAN - 1, (SPAN - 1) % 2, s_w2e[par], 4, (ei == 0))
                    mmafter[ei] = mm.vn
                if not do_shared:
                    continue
                s.do("scalar", lambda e: e.dma_start(
                    out=s_w1e[0][:], in_=ws1a_d[:, :]),
                    inc=(wls, 16), waits=[(mm, mmafter.get(nexp - 2, hv_prev_mm))])
                s.do("scalar", lambda e: e.dma_start(
                    out=s_w2e[0][:], in_=ws1b_d[:, :]),
                    inc=(wls, 16))
                s.do("scalar", lambda e: e.dma_start(
                    out=s_w1e[1][:], in_=ws2_d[:, :]),
                    inc=(wls, 16), waits=[(mm, mmafter.get(nexp - 1, hv_prev_mm))])
                wneed = (wls, wls.vn)
                for ci in range(SPAN):
                    cpar = ci % 2
                    up_pass(ci, s_w1e[0],
                            lambda mt, cpar=cpar: ("copy", s_hu[:, cpar * 4096 + mt * 512:cpar * 4096 + (mt + 1) * 512]),
                            [wneed] if ci == 0 else [])
                    up_pass(ci, s_w2e[0],
                            lambda mt, cpar=cpar: ("silu", s_sg[:, cpar * 4096 + mt * 512:cpar * 4096 + (mt + 1) * 512]),
                            [])
                    gate_pass(ci, cpar, 8, False)
                    if ci >= 1:
                        down_pass(ci - 1, (ci - 1) % 2, s_w1e[1], 8, False)
                down_pass(SPAN - 1, (SPAN - 1) % 2, s_w1e[1], 8, False)
                hv_prev_mm = mm.vn
                if not do_epi:
                    continue
                for ci in range(SPAN):
                    r0 = tok0 + ci * 512
                    s.do("sync", lambda e, r0=r0: e.dma_start(
                        out=(s_x[:].rearrange("p (a m) -> p a m", a=4) if in_ext
                             else s_x[:]),
                        in_=ld_ap(r0)),
                        inc=(ld, 16), waits=[ve, st])
                    for sub in range(4):
                        zc = (ci * 4 + sub) * 1024
                        s.do("vector", lambda e, sub=sub, zc=zc: e.scalar_tensor_tensor(
                            s_x[:, sub * D:(sub + 1) * D], s_zacc[:, zc:zc + 1024],
                            0.5, s_x[:, sub * D:(sub + 1) * D], ALU.mult, ALU.add),
                            inc=(ve, 1), waits=[ld, ve] if sub == 0 else ())
                    s.do("sync", lambda e, r0=r0: e.dma_start(
                        out=st_ap(r0),
                        in_=(s_x[:].rearrange("p (a m) -> p a m", a=4) if out_ext
                             else s_x[:])),
                        inc=(st, 16), waits=[ve])
        return phase

    def phaseMLP(s, pctx):
        psb = lambda n, sh, dt=F32: pctx.enter_context(nc.sbuf_tensor(n, list(sh), dt))
        pps = lambda n, sh: pctx.enter_context(nc.psum_tensor(n, list(sh), F32))
        s_x = psb("m_x", (128, 4 * D))
        s_scr = psb("m_scr", (128, 2 * D))
        s_stats = psb("m_stats", (128, 128))
        s_ain = psb("m_ain", (128, NSUB * D))
        s_ainT = psb("m_ainT", (128, 8 * TPH), BF16)
        s_w1 = psb("m_w1", (128, 8 * DIN), BF16)
        s_w2 = psb("m_w2", (128, 16 * D), BF16)
        s_gatT = psb("m_gatT", (128, 16 * 512), BF16)
        p_up = [pps(f"m_pup{i}", (128, 512)) for i in range(3)]
        p_z = [pps(f"m_pz{i}", (128, 1024)) for i in range(2)]
        trx = {}
        up_slot_ve = {}
        z_slot = {}
        upg = [0]

        s.do("gpsimd", lambda e: e.dma_start(out=s_w1[:], in_=sW1gT_d[:, :]),
             inc=(wld, 16))
        s.do("gpsimd", lambda e: e.dma_start(out=s_w2[:], in_=sW2T_d[:, :]),
             inc=(wld, 16))
        wneed = (wld, wld.vn)
        gidx = [0]
        for hv in range(NH):
            tok0 = hv * TPH
            for ci in range(SPAN):
                r0 = tok0 + ci * 512
                s.do("sync", lambda e, r0=r0: e.dma_start(
                    out=s_x[:], in_=x1_d[:, r0 * 8:r0 * 8 + 4096]),
                    inc=(ld, 16), waits=[ve, st, (mm, mm.vn)])
                emit_ln(s, s_x, s_stats, s_scr,
                        lambda sub, ci=ci: s_ain[:, (ci * 4 + sub) * D:(ci * 4 + sub + 1) * D], ld)
                emit_tr(s, lambda sub, kt, ci=ci: s_ain[
                            :, (ci * 4 + sub) * D + kt * 128:(ci * 4 + sub) * D + (kt + 1) * 128],
                        lambda sub, kt, ci=ci: s_ainT[
                            :, kt * TPH + ci * 512 + sub * 128:
                            kt * TPH + ci * 512 + (sub + 1) * 128],
                        p_z[0], trx, first_wait=(ve, ve.vn),
                        evac_first_wait=(mm, mm.vn))
            z_slot[0] = (ev, ev.vn)
            z_slot[1] = (ev, ev.vn)
            for ci in range(SPAN):
                r0 = tok0 + ci * 512
                for mt in range(16):
                    slot = upg[0] % 3
                    upg[0] += 1
                    w = [ev] if (ci == 0 and mt == 0) else []
                    if hv == 0 and ci == 0 and mt == 0:
                        w.append(wneed)
                    if slot in up_slot_ve:
                        w.append((ve, up_slot_ve[slot]))
                    for kt in range(8):
                        s.do("tensor", lambda e, ci=ci, mt=mt, kt=kt, slot=slot: e.matmul(
                            p_up[slot][:, :],
                            s_w1[:, kt * DIN + mt * 128:kt * DIN + (mt + 1) * 128],
                            s_ainT[:, kt * TPH + ci * 512:kt * TPH + (ci + 1) * 512],
                            start=(kt == 0), stop=False),
                            inc=None, waits=w)
                    s.do("tensor", lambda e, mt=mt, slot=slot: e.matmul(
                        p_up[slot][:, :], s_b1eff[0:1, mt * 128:(mt + 1) * 128],
                        s_ones[0:1, 0:512], start=False, stop=True),
                        inc=(mm, 1))
                    s.do("vector", lambda e, mt=mt, slot=slot: e.tensor_scalar(
                        s_gatT[:, mt * 512:(mt + 1) * 512], p_up[slot][:, :],
                        -1.0, 1.0, ALU.max, ALU.min),
                        inc=(ve, 1), waits=[mm])
                    up_slot_ve[slot] = ve.vn
                for sub in range(4):
                    slot = sub % 2
                    w = [ve]
                    if slot in z_slot:
                        w.append(z_slot[slot])
                    first = True
                    for nh in range(2):
                        for kt in range(16):
                            s.do("tensor", lambda e, sub=sub, nh=nh, kt=kt, slot=slot: e.matmul(
                                p_z[slot][:, nh * 512:(nh + 1) * 512],
                                s_gatT[:, kt * 512 + sub * 128:kt * 512 + (sub + 1) * 128],
                                s_w2[:, kt * 1024 + nh * 512:kt * 1024 + (nh + 1) * 512],
                                start=(kt == 0), stop=False),
                                inc=None, waits=w if first else ())
                            first = False
                        s.do("tensor", lambda e, nh=nh, slot=slot: e.matmul(
                            p_z[slot][:, nh * 512:(nh + 1) * 512],
                            s_ones[0:1, 0:128],
                            s_b2row[0:1, nh * 512:(nh + 1) * 512],
                            start=False, stop=True),
                            inc=(mm, 1) if nh == 1 else None)
                    col = (ci * 4 + sub)
                    s.do("vector", lambda e, sub=sub, col=col, slot=slot: e.tensor_add(
                        s_x[:, sub * D:(sub + 1) * D], p_z[slot][:, :],
                        s_ain[:, col * D:(col + 1) * D]),
                        inc=(ve, 1),
                        waits=[mm, (st, st.vn)] if sub == 0 else [mm])
                    gc = gidx[0]
                    gidx[0] += 1
                    s.do("vector", lambda e, sub=sub, gc=gc: e.reduce_max(
                        s_gcols[:, gc:gc + 1], s_x[:, sub * D:(sub + 1) * D],
                        axis=AX.X, apply_absolute_value=True),
                        inc=(ve, 1), waits=[ve])
                    z_slot[slot] = (ve, ve.vn)
                s.do("sync", lambda e, r0=r0: e.dma_start(
                    out=u1_d[:, r0 * 8:r0 * 8 + 4096], in_=s_x[:]),
                    inc=(st, 16), waits=[ve])
        NG = gidx[0]
        s.do("vector", lambda e: e.memset(s_scr[:, 0:128], 0.0), inc=(ve, 1))
        s.do("vector", lambda e, NG=NG: e.reduce_max(
            s_scr[:, 0:1], s_gcols[:, 0:NG], axis=AX.X), inc=(ve, 1), waits=[ve])
        s.do("tensor", lambda e: e.transpose(
            p_up[0][:, 0:128], s_scr[:, 0:128], s_identf[:, :]),
            inc=(mm, 1), waits=[ve])
        s.do("vector", lambda e: e.reduce_max(
            s_stats[0:1, 0:1], p_up[0][0:1, 0:128], axis=AX.X),
            inc=(ve, 1), waits=[mm])
        s.do("vector", lambda e: e.tensor_single_scalar(
            s_gsc[0:1, 0:16], s_gmaskb[0:1, 0:16], s_stats[0:1, 0:1], ALU.mult),
            inc=(ve, 1), waits=[ve])
        s.do("sync", lambda e: e.dma_start(out=gam_in[:, :], in_=s_gsc[0:1, :]),
             inc=(st, 16), waits=[ve])

    def phaseC(kind, op, groups, src, dst, sem):
        def phase(s, pctx):
            s.do("gpsimd", lambda e: e.collective_compute(
                kind, op, replica_groups=groups,
                ins=[src.ap().opt()], outs=[dst.ap().opt()]),
                inc=(sem, "cc"))
            s.do("gpsimd", lambda e: e.nop(), waits=[sem])
        return phase

    def phaseP2(s, pctx):
        psb = lambda n, sh, dt=F32: pctx.enter_context(nc.sbuf_tensor(n, list(sh), dt))
        s_u1 = psb("q_u1", (128, 4 * D))
        s_x1 = psb("q_x1", (128, 4 * D))
        s_scr = psb("q_scr", (128, 2 * D))
        s_stats = psb("q_stats", (128, 128))
        s_gall = psb("q_gall", (128, 16))
        s_ulnT = psb("q_ulnT", (128, 8 * 512), BF16)
        p_t = pctx.enter_context(nc.psum_tensor("q_pt", [128, 1024], F32))
        trx = {}

        s.do("sync", lambda e: e.dma_start(
            out=s_gall[:], in_=bass.AP(gam_out, 0, [[0, 128], [1, 16]])),
            inc=(ld, 16))
        s.do("vector", lambda e: e.tensor_mul(s_gall[:], s_gall[:], s_gmaskb[:]),
             inc=(ve, 1), waits=[ld])
        s.do("vector", lambda e: e.reduce_sum(s_qscale[:], s_gall[:], axis=AX.X),
             inc=(ve, 1), waits=[ve])
        s.do("vector", lambda e: e.tensor_scalar_add(s_qscale[:], s_qscale[:], EPS),
             inc=(ve, 1), waits=[ve])
        s.do("vector", lambda e: e.reciprocal(s_qscale[:], s_qscale[:]),
             inc=(ve, 1), waits=[ve])
        s.do("vector", lambda e: e.tensor_scalar_mul(s_qscale[:], s_qscale[:], QB),
             inc=(ve, 1), waits=[ve])
        for ci in range(NCH1):
            r0 = ci * 512
            s.do("sync", lambda e, r0=r0: e.dma_start(
                out=s_u1[:], in_=u1_d[:, r0 * 8:r0 * 8 + 4096]),
                inc=(ld, 16), waits=[ve, st, (mm, mm.vn)])
            s.do("sync", lambda e, r0=r0: e.dma_start(
                out=s_x1[:], in_=x1_d[:, r0 * 8:r0 * 8 + 4096]),
                inc=(ld, 16))
            s.do("vector", lambda e: e.tensor_scalar(
                s_u1[:], s_u1[:], s_qscale[:, 0:1], -(QB - EPS), ALU.mult, ALU.max),
                inc=(ve, 1), waits=[ld, ve])
            s.do("vector", lambda e: e.tensor_scalar_min(s_u1[:], s_u1[:], QB - EPS),
                 inc=(ve, 1), waits=[ve])
            s.do("vector", lambda e: e.tensor_add(s_x1[:], s_x1[:], s_u1[:]),
                 inc=(ve, 1), waits=[ve])
            s.do("sync", lambda e, r0=r0: e.dma_start(
                out=x2p_d[:, r0 * 8:r0 * 8 + 4096], in_=s_x1[:]),
                inc=(st, 16), waits=[ve])
            emit_ln(s, s_u1, s_stats, s_scr,
                    lambda sub: s_u1[:, sub * D:(sub + 1) * D], ve)
            emit_tr(s, lambda sub, kt: s_u1[:, sub * D + kt * 128:sub * D + (kt + 1) * 128],
                    lambda sub, kt: s_ulnT[:, kt * 512 + sub * 128:kt * 512 + (sub + 1) * 128],
                    p_t, trx, first_wait=(ve, ve.vn),
                    evac_first_wait=(st, st.vn))
            s.do("sync", lambda e, r0=r0: e.dma_start(
                out=ulnag_in[:, r0 * 8:r0 * 8 + 4096], in_=s_ulnT[:]),
                inc=(st, 16), waits=[ev])

    def phaseP3(s, pctx):
        psb = lambda n, sh, dt=F32: pctx.enter_context(nc.sbuf_tensor(n, list(sh), dt))
        pps = lambda n, sh: pctx.enter_context(nc.psum_tensor(n, list(sh), F32))
        s_ipw = psb("p3_ipw", (128, 8 * 1024), BF16)
        s_uln = [psb(f"p3_uln{i}", (128, 8 * 512), BF16) for i in range(2)]
        s_rng = psb("p3_rng", (128, 4 * 516))
        s_xt = psb("p3_xt", (128, 4 * 512))
        s_u = psb("p3_u", (128, 4 * 512))
        s_ub = psb("p3_ub", (128, 4 * 512), BF16)
        s_sz = psb("p3_sz", (128, 4 * 512))
        s_xda = psb("p3_xda", (96, L))
        p_up = [pps(f"p3_pup{i}", (128, 512)) for i in range(3)]
        p_xp = pps("p3_pxp", (96, 512))
        up_slot_ev = {}
        upg = [0]
        xp_ev = [0]

        s.do("gpsimd", lambda e: e.dma_start(out=s_ipw[:], in_=ipT_d[:, :]),
             inc=(wld, 16))
        wneed = (wld, wld.vn)
        NB = T_loc // 512
        for gi in range(NCH3):
            r, ci3 = gi // NB, gi % NB
            buf = s_uln[gi % 2]
            s.do("sync", lambda e, r=r, ci3=ci3, buf=buf: e.dma_start(
                out=buf[:],
                in_=ulnag_out[r * 128:(r + 1) * 128,
                              ci3 * 4096:(ci3 + 1) * 4096]),
                inc=(ld, 16), waits=[(mm, mm.vn)])
            lneed = (ld, ld.vn)
            for mt in range(8):
                slot = upg[0] % 3
                upg[0] += 1
                w = [wneed, lneed] if mt == 0 else []
                if slot in up_slot_ev:
                    w.append((ev, up_slot_ev[slot]))
                for kt in range(8):
                    s.do("tensor", lambda e, mt=mt, kt=kt, slot=slot, buf=buf: e.matmul(
                        p_up[slot][:, :],
                        s_ipw[:, kt * 1024 + mt * 128:kt * 1024 + (mt + 1) * 128],
                        buf[:, kt * 512:(kt + 1) * 512],
                        start=(kt == 0), stop=(kt == 7)),
                        inc=(mm, 1) if kt == 7 else None, waits=w)
                if mt < 4:
                    s.do("scalar", lambda e, mt=mt, slot=slot: e.copy(
                        s_rng[:, mt * 516 + 3:mt * 516 + 515], p_up[slot][:, :]),
                        inc=(ev, 1), waits=[mm, (ve, ve.vn)])
                else:
                    ct = mt - 4
                    s.do("scalar", lambda e, ct=ct, slot=slot: e.activation(
                        s_sz[:, ct * 512:(ct + 1) * 512], p_up[slot][:, :],
                        AF.Silu, bias=s_ipbz[:, ct:ct + 1]),
                        inc=(ev, 1), waits=[mm, (st, st.vn)])
                up_slot_ev[slot] = ev.vn
            if gi == 0:
                for ct in range(4):
                    s.do("vector", lambda e, ct=ct: e.memset(
                        s_rng[:, ct * 516:ct * 516 + 3], 0.0), inc=(ve, 1))
            for ct in range(4):
                base = ct * 516
                s.do("vector", lambda e, ct=ct, base=base: e.tensor_single_scalar(
                    s_xt[:, ct * 512:(ct + 1) * 512], s_rng[:, base:base + 512],
                    s_convw[:, ct * 4:ct * 4 + 1], ALU.mult),
                    inc=(ve, 1), waits=[ev, ve] if ct == 0 else ())
                for k in range(1, 4):
                    s.do("vector", lambda e, ct=ct, base=base, k=k: e.scalar_tensor_tensor(
                        s_xt[:, ct * 512:(ct + 1) * 512], s_rng[:, base + k:base + k + 512],
                        s_convw[:, ct * 4 + k:ct * 4 + k + 1],
                        s_xt[:, ct * 512:(ct + 1) * 512], ALU.mult, ALU.add),
                        inc=(ve, 1), waits=[ve])
            for ct in range(4):
                s.do("scalar", lambda e, ct=ct: e.copy(
                    s_rng[:, ct * 516:ct * 516 + 3],
                    s_rng[:, ct * 516 + 512:ct * 516 + 515]),
                    inc=(ev, 1), waits=[(ve, ve.vn)] if ct == 0 else ())
            for ct in range(4):
                s.do("scalar", lambda e, ct=ct: e.activation(
                    s_u[:, ct * 512:(ct + 1) * 512], s_xt[:, ct * 512:(ct + 1) * 512],
                    AF.Silu, bias=s_cbeff[:, ct:ct + 1]),
                    inc=(ev, 1), waits=[(ve, ve.vn), (st, st.vn)] if ct == 0 else ())
            for ct in range(4):
                s.do("scalar", lambda e, ct=ct: e.copy(
                    s_ub[:, ct * 512:(ct + 1) * 512], s_u[:, ct * 512:(ct + 1) * 512]),
                    inc=(ev, 1), waits=[(mm, mm.vn), ev] if ct == 0 else ())
            s.do("sync", lambda e, gi=gi: e.dma_start(
                out=u_d.ap().rearrange("(a p) t -> p a t", p=128)[:, :, gi * 512:(gi + 1) * 512],
                in_=s_u[:].rearrange("p (a m) -> p a m", a=4)),
                inc=(st, 16), waits=[ev])
            s.do("sync", lambda e, gi=gi: e.dma_start(
                out=sz_d.ap().rearrange("(a p) t -> p a t", p=128)[:, :, gi * 512:(gi + 1) * 512],
                in_=s_sz[:].rearrange("p (a m) -> p a m", a=4)),
                inc=(st, 16))
            for kt in range(4):
                w = [ev] if kt == 0 else []
                if kt == 0 and xp_ev[0]:
                    w.append((ev, xp_ev[0]))
                s.do("tensor", lambda e, kt=kt: e.matmul(
                    p_xp[:, :], s_xpT[:, kt * 96:(kt + 1) * 96],
                    s_ub[:, kt * 512:(kt + 1) * 512],
                    start=(kt == 0), stop=(kt == 3)),
                    inc=(mm, 1) if kt == 3 else None, waits=w)
            s.do("scalar", lambda e, gi=gi: e.copy(
                s_xda[:, gi * 512:(gi + 1) * 512], p_xp[0:96, :]),
                inc=(ev, 1), waits=[mm])
            xp_ev[0] = ev.vn
        s.do("sync", lambda e: e.dma_start(out=xdbl_in[:, :], in_=s_xda[:]),
             inc=(st, 16), waits=[ev])

    def phaseP4(s, pctx):
        psb = lambda n, sh, dt=F32: pctx.enter_context(nc.sbuf_tensor(n, list(sh), dt))
        pps = lambda n, sh: pctx.enter_context(nc.psum_tensor(n, list(sh), F32))
        s_selb = psb("p4_selb", (32, 32 * 128))
        s_u4 = [psb(f"p4_u{i}", (128, 4 * CH4)) for i in range(2)]
        s_dsz = psb("p4_dsz", (128, 4 * CH4))
        s_y4 = psb("p4_y4", (128, 4 * CH4))
        s_a4 = psb("p4_a4", (128, 4 * CH4))
        s_b4 = psb("p4_b4", (128, 4 * CH4))
        s_h4 = [psb(f"p4_h{i}", (128, CH4)) for i in range(4)]
        s_ym = [psb(f"p4_ym{i}", (128, CH4)) for i in range(4)]
        s_ygb = psb("p4_ygb", (128, 4 * CH4), BF16)
        s_xd4 = [psb(f"p4_xd{i}", (96, CH4)) for i in range(2)]
        s_ab = psb("p4_ab", (128, 2 * 512))
        s_bc = psb("p4_bc", (32, CH4))
        s_crb = psb("p4_crb", (128, CH4))
        s_dtin = psb("p4_dtin", (64, CH4), BF16)
        p_br = [pps(f"p4_pbr{i}", (128, CH4)) for i in range(2)]
        p_cr = [pps(f"p4_pcr{i}", (128, CH4)) for i in range(2)]
        NQ = CH4 // 512

        s.do("sync", lambda e: e.dma_start(out=s_selb[:], in_=selbc[:, :]),
             inc=(ld, 16))
        hinit_done = [0]
        for c4 in range(NCH4):
            ubuf, xdb = s_u4[c4 % 2], s_xd4[c4 % 2]
            s.do("sync", lambda e, c4=c4, ubuf=ubuf: e.dma_start(
                out=ubuf[:].rearrange("p (a m) -> p a m", a=4),
                in_=u_d.ap().rearrange("(a p) t -> p a t", p=128)[
                    :, :, c4 * CH4:(c4 + 1) * CH4]),
                inc=(ld, 16), waits=[(ve, ve.vn)])
            s.do("sync", lambda e, c4=c4, xdb=xdb: e.dma_start(
                out=xdb[:], in_=xdbl_out[:, c4 * CH4:(c4 + 1) * CH4]),
                inc=(ld, 16), waits=[(mm, mm.vn), (ev, ev.vn)])
            lneed = (ld, ld.vn)
            s.do("scalar", lambda e, xdb=xdb: e.copy(s_dtin[:, :], xdb[0:64, :]),
                 inc=(ev, 1), waits=[lneed, (mm, mm.vn)])
            dtin_done = (ev, ev.vn)
            regions = [p_br[0][:, 0:512], p_br[0][:, 512:1024],
                       p_cr[0][:, 0:512], p_cr[0][:, 512:1024]]
            dt_reg_ev = {}
            for mt in range(4):
                for nq in range(NQ):
                    gidx = mt * NQ + nq
                    reg = gidx % 4
                    w = []
                    if gidx == 0:
                        w += [dtin_done, (ve, ve.vn)]
                    if reg in dt_reg_ev:
                        w.append((ev, dt_reg_ev[reg]))
                    s.do("tensor", lambda e, mt=mt, nq=nq, reg=reg: e.matmul(
                        regions[reg], s_dtpT[:, mt * 128:(mt + 1) * 128],
                        s_dtin[:, nq * 512:(nq + 1) * 512],
                        start=True, stop=True),
                        inc=(mm, 1), waits=w)
                    cc0 = mt * CH4 + nq * 512
                    s.do("scalar", lambda e, mt=mt, cc0=cc0, reg=reg: e.activation(
                        s_dsz[:, cc0:cc0 + 512], regions[reg], AF.Relu,
                        bias=s_dtpb[:, mt:mt + 1]),
                        inc=(ev, 1),
                        waits=[mm, (ve, ve.vn), (st, st.vn)] if gidx == 0 else [mm])
                    s.do("scalar", lambda e, mt=mt, reg=reg: e.activation(
                        s_ab[:, 0:512], regions[reg], AF.Abs,
                        bias=s_dtpb[:, mt:mt + 1]),
                        inc=(ev, 1), waits=[(ve, ve.vn)])
                    s.do("scalar", lambda e: e.activation(
                        s_ab[:, 512:1024], s_ab[:, 0:512], AF.Exp, scale=-1.0),
                        inc=(ev, 1), waits=[ev])
                    s.do("scalar", lambda e: e.activation(
                        s_ab[:, 0:512], s_ab[:, 512:1024], AF.Ln, bias=1.0),
                        inc=(ev, 1), waits=[ev])
                    dt_reg_ev[reg] = ev.vn
                    s.do("vector", lambda e, cc0=cc0: e.tensor_add(
                        s_dsz[:, cc0:cc0 + 512], s_dsz[:, cc0:cc0 + 512],
                        s_ab[:, 0:512]),
                        inc=(ve, 1), waits=[ev])
            dt_done_ve = (ve, ve.vn)
            s.do("scalar", lambda e, xdb=xdb: e.copy(s_bc[:, :], xdb[64:96, :]),
                 inc=(ev, 1), waits=[lneed, (mm, mm.vn)])
            bc_done = (ev, ev.vn)
            prev_hinit = hinit_done[0]
            b_done = {}
            ym_done = {}
            scan_done = {}
            ymv = {}
            ya_done = {}
            cr_evd = {}
            for sv in range(S):
                for nq in range(NQ):
                    w = []
                    if sv <= 1 and nq == 0:
                        w += [bc_done, (ev, ev.vn)]
                    if sv >= 2 and nq == 0:
                        w.append((ve, b_done[sv - 2]))
                    s.do("tensor", lambda e, sv=sv, nq=nq: e.matmul(
                        p_br[sv % 2][:, nq * 512:(nq + 1) * 512],
                        s_selb[:, sv * 128:(sv + 1) * 128],
                        s_bc[0:32, nq * 512:(nq + 1) * 512],
                        start=True, stop=True),
                        inc=(mm, 1), waits=w)
                br_mm = (mm, mm.vn)
                for ct in range(4):
                    w = []
                    if (ct, sv - 1) in scan_done:
                        w.append((ve, scan_done[(ct, sv - 1)]))
                    if sv == 0 and ct == 0:
                        w.append(dt_done_ve)
                    s.do("scalar", lambda e, ct=ct, sv=sv: e.activation(
                        s_a4[:, ct * CH4:(ct + 1) * CH4],
                        s_dsz[:, ct * CH4:(ct + 1) * CH4],
                        AF.Exp, scale=s_negA[:, ct * 16 + sv:ct * 16 + sv + 1]),
                        inc=(ev, 1), waits=w)
                a_done = (ev, ev.vn)
                for ct in range(4):
                    w = [br_mm] if ct == 0 else []
                    if sv == 0 and ct == 0:
                        w.append(lneed)
                    s.do("vector", lambda e, ct=ct, sv=sv, ubuf=ubuf: e.tensor_mul(
                        s_b4[:, ct * CH4:(ct + 1) * CH4],
                        ubuf[:, ct * CH4:(ct + 1) * CH4], p_br[sv % 2][:, :]),
                        inc=(ve, 1), waits=w)
                b_done[sv] = ve.vn
                for ct in range(4):
                    col = ct * 16 + sv
                    init = 0.0 if c4 == 0 else s_hinit[:, col:col + 1]
                    w = [a_done] if ct == 0 else []
                    if c4 > 0 and sv == 0 and ct == 0:
                        w.append((ev, prev_hinit))
                    if sv >= 1 and (sv - 1, ct) in ymv:
                        w.append((ve, ymv[(sv - 1, ct)]))
                    if ct == 0:
                        w.append(ve)
                    s.do("vector", lambda e, ct=ct, init=init: e.tensor_tensor_scan(
                        s_h4[ct][:, :], s_a4[:, ct * CH4:(ct + 1) * CH4],
                        s_b4[:, ct * CH4:(ct + 1) * CH4], init,
                        ALU.mult, ALU.add),
                        inc=(ve, 1), waits=w)
                    scan_done[(ct, sv)] = ve.vn
                    s.do("scalar", lambda e, ct=ct, col=col: e.copy(
                        s_hinit[:, col:col + 1], s_h4[ct][:, CH4 - 1:CH4]),
                        inc=(ev, 1), waits=[(ve, scan_done[(ct, sv)])])
                for nq in range(NQ):
                    w = []
                    if nq == 0:
                        w.append((ev, cr_evd[sv - 2]) if sv >= 2 else bc_done)
                    s.do("tensor", lambda e, sv=sv, nq=nq: e.matmul(
                        p_cr[sv % 2][:, nq * 512:(nq + 1) * 512],
                        s_selb[:, (16 + sv) * 128:(17 + sv) * 128],
                        s_bc[0:32, nq * 512:(nq + 1) * 512],
                        start=True, stop=True),
                        inc=(mm, 1), waits=w)
                cr_mm = (mm, mm.vn)
                wev = [cr_mm, ev]
                if sv >= 1:
                    wev.append((ve, ymv.get((sv - 1, 3), 0)))
                s.do("scalar", lambda e, sv=sv: e.copy(s_crb[:, :], p_cr[sv % 2][:, :]),
                     inc=(ev, 1), waits=wev)
                cr_ev = (ev, ev.vn)
                cr_evd[sv] = ev.vn
                if sv == 0:
                    for ct in range(4):
                        w = [cr_ev, ve, (gp, gp.vn)] if ct == 0 else ()
                        s.do("vector", lambda e, ct=ct: e.tensor_mul(
                            s_y4[:, ct * CH4:(ct + 1) * CH4], s_h4[ct][:, :],
                            s_crb[:, :]),
                            inc=(ve, 1), waits=w)
                        ymv[(sv, ct)] = ve.vn
                else:
                    for ct in range(4):
                        w = [cr_ev, ve] if ct == 0 else []
                        w.append((gp, ya_done.get((sv - 1, ct), 0)))
                        s.do("vector", lambda e, ct=ct: e.tensor_mul(
                            s_ym[ct][:, :], s_h4[ct][:, :], s_crb[:, :]),
                            inc=(ve, 1), waits=w)
                        ymv[(sv, ct)] = ve.vn
                    for ct in range(4):
                        s.do("gpsimd", lambda e, ct=ct: e.tensor_add(
                            s_y4[:, ct * CH4:(ct + 1) * CH4],
                            s_y4[:, ct * CH4:(ct + 1) * CH4],
                            s_ym[ct][:, :]),
                            inc=(gp, 1), waits=[(ve, ymv[(sv, ct)]), gp])
                        ya_done[(sv, ct)] = gp.vn
                ym_done[sv] = ve.vn
            hinit_done[0] = ev.vn
            s.do("sync", lambda e, c4=c4: e.dma_start(
                out=s_dsz[:].rearrange("p (a m) -> p a m", a=4),
                in_=sz_d.ap().rearrange("(a p) t -> p a t", p=128)[
                    :, :, c4 * CH4:(c4 + 1) * CH4]),
                inc=(ld, 16), waits=[(ve, ve.vn), (ev, ev.vn)])
            szneed = (ld, ld.vn)
            for ct in range(4):
                s.do("vector", lambda e, ct=ct, ubuf=ubuf: e.scalar_tensor_tensor(
                    s_y4[:, ct * CH4:(ct + 1) * CH4],
                    ubuf[:, ct * CH4:(ct + 1) * CH4],
                    s_Dcs[:, ct:ct + 1], s_y4[:, ct * CH4:(ct + 1) * CH4],
                    ALU.mult, ALU.add),
                    inc=(ve, 1), waits=[ve, gp] if ct == 0 else ())
            for ct in range(4):
                s.do("vector", lambda e, ct=ct: e.tensor_mul(
                    s_ygb[:, ct * CH4:(ct + 1) * CH4],
                    s_y4[:, ct * CH4:(ct + 1) * CH4],
                    s_dsz[:, ct * CH4:(ct + 1) * CH4]),
                    inc=(ve, 1), waits=[szneed, (st, st.vn), ve] if ct == 0 else ())
            s.do("sync", lambda e, c4=c4: e.dma_start(
                out=ygag_in.ap().rearrange("(a p) t -> p a t", p=128)[
                    :, :, c4 * CH4:(c4 + 1) * CH4],
                in_=s_ygb[:].rearrange("p (a m) -> p a m", a=4)),
                inc=(st, 16), waits=[ve])

    def phaseP5a(s, pctx):
        psb = lambda n, sh, dt=F32: pctx.enter_context(nc.sbuf_tensor(n, list(sh), dt))
        pps = lambda n, sh: pctx.enter_context(nc.psum_tensor(n, list(sh), F32))
        s_opw = psb("p5_opw", (128, 16 * 1024), BF16)
        s_yg = [psb(f"p5_yg{i}", (128, 16 * 512), BF16) for i in range(2)]
        s_x2p = psb("p5_x2p", (128, 4 * D))
        p_z = [pps(f"p5_pz{i}", (128, 1024)) for i in range(2)]
        z_slot = {}

        s.do("scalar", lambda e: e.dma_start(
            out=s_opw[:], in_=opbf_d[:, :]),
            inc=(wls, 16))
        wneed = (wls, wls.vn)
        qcell = []

        def ygload(e, ci, buf):
            if not qcell:
                qcell.append(e.snap(nc.partition_id() % 4 * T_loc))
            return e.dma_start(
                out=buf[:].rearrange("p (a m) -> p a m", a=16),
                in_=ygag_out.ap().rearrange("(a p) t -> p a t", p=128)[
                    :, :, bass.ds(qcell[0] + ci * 512, 512)])

        for ci in range(NCH1):
            buf = s_yg[ci % 2]
            s.do("sync", lambda e, ci=ci, buf=buf: ygload(e, ci, buf),
                 inc=(ld, 16), waits=[(mm, mm.vn)])
            s.do("sync", lambda e, ci=ci: e.dma_start(
                out=s_x2p[:], in_=x2p_d[:, ci * 4096:(ci + 1) * 4096]),
                inc=(ld, 16), waits=[(ve, ve.vn), (st, st.vn)])
            lneed = (ld, ld.vn)
            for sub in range(4):
                slot = sub % 2
                w = [lneed, wneed] if sub == 0 else []
                if slot in z_slot:
                    w.append(z_slot[slot])
                first = True
                for nh in range(2):
                    for kt in range(16):
                        s.do("tensor", lambda e, sub=sub, nh=nh, kt=kt, slot=slot, buf=buf: e.matmul(
                            p_z[slot][:, nh * 512:(nh + 1) * 512],
                            buf[:, kt * 512 + sub * 128:kt * 512 + (sub + 1) * 128],
                            s_opw[:, kt * 1024 + nh * 512:kt * 1024 + (nh + 1) * 512],
                            start=(kt == 0), stop=(kt == 15)),
                            inc=(mm, 1) if (nh == 1 and kt == 15) else None,
                            waits=w if first else ())
                        first = False
                s.do("vector", lambda e, sub=sub, slot=slot: e.tensor_add(
                    s_x2p[:, sub * D:(sub + 1) * D], s_x2p[:, sub * D:(sub + 1) * D],
                    p_z[slot][:, :]),
                    inc=(ve, 1), waits=[mm])
                z_slot[slot] = (ve, ve.vn)
            s.do("sync", lambda e, ci=ci: e.dma_start(
                out=x2_d[:, ci * 4096:(ci + 1) * 4096], in_=s_x2p[:]),
                inc=(st, 16), waits=[ve])

    phases = [
        phaseW,
        make_ffn(x_sh, x1_d.ap(), "fa", in_ext=True),
        phaseMLP,
        phaseC("AllReduce", ALU.max, G8, gam_in, gam_out, cc1),
        phaseP2,
        phaseC("AllGather", ALU.bypass, G4, ulnag_in, ulnag_out, cc2),
        phaseP3,
        phaseC("AllReduce", ALU.add, G4, xdbl_in, xdbl_out, cc3),
        phaseP4,
        phaseC("AllGather", ALU.bypass, G4, ygag_in, ygag_out, cc4),
        phaseP5a,
        make_ffn(x2_d.ap(), out_sh, "fb", out_ext=True),
    ]
    np_ = NPHASES if NPHASES is not None else len(phases)
    for ph in phases[:np_]:
        run_phase(ph)

    for sem in SEM.values():
        assert sem.vn < 60000, (sem.name, sem.vn)
    ctx.close()
    return nc


def host_inputs(inputs, L):
    T_loc = L // 4
    g = lambda k: np.asarray(inputs[k])
    x = g("L0_attn_x")
    co = np.ascontiguousarray

    def cs(v, nk):
        return co(v.reshape(nk, 128).T)

    W1 = g("L0_attn_W1")
    W2 = g("L0_attn_W2")

    def tiles(wT, nk):
        K, M = wT.shape
        assert K == nk * 128
        return co(wT.reshape(nk, 128, M).transpose(1, 0, 2).reshape(128, nk * M))

    Ws1 = g("L0_ffn_Ws1")
    base = dict(
        ffn_W1=co(np.stack([tiles(g("L0_ffn_W1")[e], 8) for e in range(NE)])),
        ffn_W2=co(np.stack([tiles(g("L0_ffn_W2")[e], 4) for e in range(NE)])),
        ffn_Ws1a=tiles(Ws1[:, 0:HS], 8),
        ffn_Ws1b=tiles(Ws1[:, HS:2 * HS], 8),
        ffn_Ws2=tiles(g("L0_ffn_Ws2"), 8),
        ffn_WgT=tiles(co(g("L0_ffn_Wg").T), 8),
        aW1T=tiles(co(W1.T), 8),
        aW2T=tiles(co(W2.T), 16),
        b1_row=co(g("L0_attn_b1").reshape(1, DIN)),
        b2_row=co(g("L0_attn_b2").reshape(1, D)),
        b1ln_cs=cs(g("L0_attn_ln1_b"), 8).astype(BF),
        g1_cs=cs(g("L0_attn_ln1_g"), 8),
        b2ln_cs=cs(g("L0_attn_ln2_b"), 8).astype(BF),
        g2_cs=cs(g("L0_attn_ln2_g"), 8),
        opT=tiles(co(g("L0_attn_out_proj_w").T), 16),
        identf=np.eye(128, dtype=np.float32),
        ones_row=np.ones((1, 512), dtype=BF),
    )
    sel16 = np.zeros((16, 16 * 128), dtype=BF)
    for e in range(16):
        sel16[e, e * 128:(e + 1) * 128] = 1.0
    base["sel16"] = sel16
    selbc = np.zeros((32, 32 * 128), dtype=np.float32)
    for j in range(32):
        selbc[j, j * 128:(j + 1) * 128] = 1.0
    base["selbc"] = selbc

    ipw = g("L0_attn_in_proj_w")
    xpw = g("L0_attn_x_proj_w")
    dtw = g("L0_attn_dt_proj_w")
    dtb = g("L0_attn_dt_proj_b")
    cw = g("L0_attn_conv_w")[:, 0, :]
    cb = g("L0_attn_conv_b")
    Al = g("L0_attn_A_log")
    Dp = g("L0_attn_D_param")

    maps = []
    for c in range(NCORES):
        b, q = c // 4, c % 4
        sl = slice(q * 512, (q + 1) * 512)
        ip_sl = np.concatenate([ipw[sl], ipw[DIN + q * 512:DIN + (q + 1) * 512]], axis=0)
        gm = np.zeros((1, 16), dtype=np.float32)
        gm[0, q] = 1.0
        m = dict(base)
        m.update(
            x_sh=co(x[b, q * T_loc:(q + 1) * T_loc]),
            ipT_sl=tiles(co(ip_sl.T), 8),
            xpT_sl=tiles(co(xpw[:, sl].T), 4),
            dtpT_sl=co(dtw[sl].T),
            dtpb_cs=cs(dtb[sl], 4),
            convw_cs=co(cw[sl].reshape(4, 128, 4).transpose(1, 0, 2).reshape(128, 16)),
            convb_cs=cs(cb[sl], 4),
            A_cs=co(Al[sl].reshape(4, 128, 16).transpose(1, 0, 2).reshape(128, 64)),
            D_cs=cs(Dp[sl], 4),
            gmask=gm,
        )
        maps.append(m)
    return maps


def kernel(**inputs):
    L = L_FULL
    nc = build_program(L)
    maps = host_inputs(inputs, L)
    res = run_bass_kernel_spmd(nc, maps, core_ids=list(range(NCORES)))
    T_loc = L // 4
    out = np.empty((2, L, D), dtype=np.float32)
    for c in range(NCORES):
        b, q = c // 4, c % 4
        out[b, q * T_loc:(q + 1) * T_loc] = res.results[c]["out_sh"]
    return out


# version 10
# speedup vs baseline: 1.0616x; 1.0583x over previous
import numpy as np
import ml_dtypes
import contextlib

import concourse.bass as bass
import concourse.mybir as mybir
from concourse.bass_utils import run_bass_kernel_spmd

F32 = mybir.dt.float32
BF16 = mybir.dt.bfloat16
AF = mybir.ActivationFunctionType
ALU = mybir.AluOpType
AX = mybir.AxisListType

D = 1024
DIN = 2048
S = 16
RK = 64
NE = 16
H = 512
HS = 1024
EPS = 1e-5
QB = 128.0
NCORES = 8
BF = ml_dtypes.bfloat16
ENGS = ("sync", "tensor", "vector", "scalar", "gpsimd")
L_FULL = 16384
NPHASES = None
FFN_DBG = None


class DSem:
    def __init__(self, nc, ctx, name):
        self.h = ctx.enter_context(nc.semaphore(name))
        self.vn = 0
        self.name = name


class Script:

    def __init__(self):
        self.q = {e: [] for e in ENGS}

    def do(self, eng, fn, inc=None, waits=(), tag=""):
        wl = []
        for w in waits:
            if isinstance(w, DSem):
                wl.append((w, w.vn))
            else:
                wl.append(w)
        target = None
        if inc is not None:
            sem, k = inc
            sem.vn += 1 if k == "cc" else k
            target = (sem, k)

        def run(e, wl=wl, fn=fn, target=target, tag=tag):
            for (ws, wn) in wl:
                if wn > 0:
                    e.wait_ge(ws.h, wn)
            inst = fn(e)
            if target is not None:
                sem, k = target
                assert inst is not None, f"op with inc returned None ({tag})"
                if k == "cc":
                    inst.then_inc(sem.h)
                else:
                    inst.then_inc(sem.h, k)

        self.q[eng].append(run)

    def emit(self, blk):
        for name in ENGS:
            ops = self.q[name]
            if not ops:
                continue

            def body(e, ops=ops):
                for op in ops:
                    op(e)

            getattr(blk, name)(body)


def build_program(L):
    T_loc = L // 4
    NCH1 = T_loc // 512
    SPAN = min(2, NCH1)
    NH = NCH1 // SPAN
    NCH3 = L // 512
    CH4 = 1024
    NCH4 = L // CH4
    DCH = 512
    TPH = SPAN * 512
    NSUB = SPAN * 4

    nc = bass.Bass("TRN2", target_bir_lowering=False)

    def inp(name, shape, dt=F32):
        return nc.dram_tensor(name, list(shape), dt, kind="ExternalInput").ap()

    x_sh = inp("x_sh", (T_loc, D))
    ffn_W1 = inp("ffn_W1", (NE, 128, 8 * 1024))
    ffn_W2 = inp("ffn_W2", (NE, 128, 4 * 1024))
    ffn_Ws1a = inp("ffn_Ws1a", (128, 8 * 1024))
    ffn_Ws1b = inp("ffn_Ws1b", (128, 8 * 1024))
    ffn_Ws2 = inp("ffn_Ws2", (128, 8 * 1024))
    ffn_WgT = inp("ffn_WgT", (128, 8 * 16))
    aW1T = inp("aW1T", (128, 8 * DIN))
    aW2T = inp("aW2T", (128, 16 * D))
    b1_row = inp("b1_row", (1, DIN))
    b2_row = inp("b2_row", (1, D))
    b1ln_cs = inp("b1ln_cs", (128, 8), BF16)
    g1_cs = inp("g1_cs", (128, 8))
    b2ln_cs = inp("b2ln_cs", (128, 8), BF16)
    g2_cs = inp("g2_cs", (128, 8))
    ipT_sl = inp("ipT_sl", (128, 8 * 1024))
    opT = inp("opT", (128, 16 * 1024))
    xpT_sl = inp("xpT_sl", (128, 4 * 96))
    dtpT_sl = inp("dtpT_sl", (RK, DCH))
    dtpb_cs = inp("dtpb_cs", (128, 4))
    convw_cs = inp("convw_cs", (128, 16))
    convb_cs = inp("convb_cs", (128, 4))
    A_cs = inp("A_cs", (128, 64))
    D_cs = inp("D_cs", (128, 4))
    identf = inp("identf", (128, 128))
    ones_row = inp("ones_row", (1, 512), BF16)
    sel16 = inp("sel16", (16, 16 * 128), BF16)
    selbc = inp("selbc", (32, 32 * 128), BF16)
    gmask = inp("gmask", (1, 16))

    out_sh = nc.dram_tensor("out_sh", [T_loc, D], F32, kind="ExternalOutput").ap()

    def scr(name, shape, dt=F32):
        return nc.dram_tensor(name, list(shape), dt)

    x1_d = scr("x1_d", (128, 8 * T_loc))
    u1_d = scr("u1_d", (128, 8 * T_loc))
    x2p_d = scr("x2p_d", (128, 8 * T_loc))
    x2_d = scr("x2_d", (128, 8 * T_loc))
    u_d = scr("u_d", (DCH, L))
    sz_d = scr("sz_d", (DCH, L))
    sW1gT_d = scr("sW1gT_d", (128, 8 * DIN), BF16)
    w1bf_d = scr("w1bf_d", (128, NE * 8 * 1024), BF16)
    w2bf_d = scr("w2bf_d", (128, NE * 4 * 1024), BF16)
    ws1a_d = scr("ws1a_d", (128, 8 * 1024), BF16)
    ws1b_d = scr("ws1b_d", (128, 8 * 1024), BF16)
    ws2_d = scr("ws2_d", (128, 8 * 1024), BF16)
    opbf_d = scr("opbf_d", (128, 16 * 1024), BF16)
    sW2T_d = scr("sW2T_d", (128, 16 * D), BF16)
    ipT_d = scr("ipT_d", (128, 8 * 1024), BF16)
    gam_in = scr("gam_in", (1, 16))
    gam_out = scr("gam_out", (1, 16))
    ulnag_in = scr("ulnag_in", (128, 8 * T_loc), BF16)
    ulnag_out = scr("ulnag_out", (4 * 128, 8 * T_loc), BF16)
    xdbl_in = scr("xdbl_in", (L // 1024 * 96, 1024))
    xdbl_out = scr("xdbl_out", (L // 1024 * 96, 1024))
    ygag_in = scr("ygag_in", (DCH, L), BF16)
    ygag_out = scr("ygag_out", (DIN, L), BF16)

    G4 = [[0, 1, 2, 3], [4, 5, 6, 7]]
    G8 = [list(range(NCORES))]

    ctx = contextlib.ExitStack()
    sb = lambda name, shape, dt=F32: ctx.enter_context(nc.sbuf_tensor(name, list(shape), dt))

    s_identf = sb("s_identf", (128, 128))
    s_ones = sb("s_ones", (1, 512), BF16)
    s_sel16 = sb("s_sel16", (16, 16 * 128), BF16)
    s_b1eff = sb("s_b1eff", (1, DIN), BF16)
    s_b2row = sb("s_b2row", (1, D), BF16)
    s_WgT = sb("s_WgT", (128, 8 * 16), BF16)
    s_qscale = sb("s_qscale", (128, 1))
    s_gmaskb = sb("s_gmaskb", (128, 16))
    s_negA = sb("s_negA", (128, 64))
    s_Dcs = sb("s_Dcs", (128, 4))
    s_cbeff = sb("s_cbeff", (128, 4))
    s_ipbz = sb("s_ipbz", (128, 4))
    s_dtpb = sb("s_dtpb", (128, 4))
    s_convw = sb("s_convw", (128, 16))
    s_xpT = sb("s_xpT", (128, 4 * 96), BF16)
    s_dtpT = sb("s_dtpT", (64, 4 * 128), BF16)
    s_hinit = sb("s_hinit", (128, 64))
    s_gsc = sb("s_gsc", (1, 16))
    s_gcols = sb("s_gcols", (128, 64))

    SEM = {}
    for name in ["ld", "wld", "wls", "st", "mm", "ev", "ve", "gp", "cc1", "cc2", "cc3", "cc4"]:
        SEM[name] = DSem(nc, ctx, name)
    ld, wld, st = SEM["ld"], SEM["wld"], SEM["st"]
    wls = SEM["wls"]
    gp = SEM["gp"]
    mm, ev, ve = SEM["mm"], SEM["ev"], SEM["ve"]
    cc1, cc2, cc3, cc4 = SEM["cc1"], SEM["cc2"], SEM["cc3"], SEM["cc4"]

    def run_phase(builder):
        with nc.Block() as blk, contextlib.ExitStack() as pctx:
            s = Script()
            builder(s, pctx)
            s.do("sync", lambda e: e.nop(), waits=[st, wld, wls])
            s.emit(blk)
        nc.all_engine_barrier()

    def emit_ln(s, s_x, s_stats, s_scr, norm_out, wait_loaded):
        for sub in range(4):
            s.do("scalar", lambda e, sub=sub: e.activation(
                s_scr[:, 0:D], s_x[:, sub * D:(sub + 1) * D], AF.Square,
                accum_out=s_stats[:, 16 + sub:17 + sub]),
                inc=(ev, 1), waits=[wait_loaded, ev] if sub == 0 else [ev])
            s.do("scalar", lambda e, sub=sub: e.activation(
                s_scr[:, D:2 * D], s_x[:, sub * D:(sub + 1) * D], AF.Copy,
                accum_out=s_stats[:, sub:sub + 1]),
                inc=(ev, 1), waits=[ev])
        ST = s_stats
        s.do("vector", lambda e: e.tensor_scalar_mul(ST[:, 32:36], ST[:, 0:4], 1.0 / D),
             inc=(ve, 1), waits=[ev])
        s.do("vector", lambda e: e.scalar_tensor_tensor(
            ST[:, 48:52], ST[:, 32:36], -1.0, ST[:, 32:36], ALU.mult, ALU.mult),
            inc=(ve, 1), waits=[ve])
        s.do("vector", lambda e: e.scalar_tensor_tensor(
            ST[:, 64:68], ST[:, 16:20], 1.0 / D, ST[:, 48:52], ALU.mult, ALU.add),
            inc=(ve, 1), waits=[ve])
        s.do("vector", lambda e: e.tensor_scalar_add(ST[:, 80:84], ST[:, 64:68], EPS),
             inc=(ve, 1), waits=[ve])
        s.do("scalar", lambda e: e.activation(ST[:, 64:68], ST[:, 80:84], AF.Ln),
             inc=(ev, 1), waits=[ve])
        s.do("scalar", lambda e: e.activation(ST[:, 96:100], ST[:, 64:68], AF.Exp,
                                              scale=-0.5), inc=(ev, 1), waits=[ev])
        s.do("vector", lambda e: e.scalar_tensor_tensor(
            ST[:, 112:116], ST[:, 32:36], -1.0, ST[:, 96:100], ALU.mult, ALU.mult),
            inc=(ve, 1), waits=[ev])
        for sub in range(4):
            s.do("vector", lambda e, sub=sub: e.tensor_scalar(
                norm_out(sub), s_x[:, sub * D:(sub + 1) * D],
                ST[:, 96 + sub:97 + sub], ST[:, 112 + sub:113 + sub],
                ALU.mult, ALU.add),
                inc=(ve, 1), waits=[ve] if sub == 0 else ())

    def emit_tr(s, src_ap, dst_col, p4, slots, first_wait=None, evac_first_wait=None):
        for sub in range(4):
            for kt in range(8):
                g = sub * 8 + kt
                slot = g % 2
                w = []
                if first_wait is not None and g == 0:
                    w.append(first_wait)
                if slot in slots:
                    w.append((ev, slots[slot]))
                s.do("tensor", lambda e, sub=sub, kt=kt, slot=slot: e.transpose(
                    p4[:, slot * 512:slot * 512 + 128],
                    src_ap(sub, kt), s_identf[:, :]),
                    inc=(mm, 1), waits=w)
                ew = [mm]
                if evac_first_wait is not None and g == 0:
                    ew.append(evac_first_wait)
                s.do("scalar", lambda e, sub=sub, kt=kt, slot=slot: e.copy(
                    dst_col(sub, kt), p4[:, slot * 512:slot * 512 + 128]),
                    inc=(ev, 1), waits=ew)
                slots[slot] = ev.vn

    def phaseW(s, pctx):
        psb = lambda n, sh, dt=F32: pctx.enter_context(nc.sbuf_tensor(n, list(sh), dt))
        s_w1u = psb("w_w1u", (128, 8 * DIN), BF16)
        s_ipTu = psb("w_ipTu", (128, 8 * 1024), BF16)
        s_sc2T = psb("w_sc2T", (128, 16 * D), BF16)
        s_b1c = psb("w_b1c", (128, 8), BF16)
        s_b2c = psb("w_b2c", (128, 8), BF16)
        s_g1 = psb("w_g1", (128, 8))
        s_g2 = psb("w_g2", (128, 8))
        s_bacc = psb("w_bacc", (1, DIN))
        s_b1raw = psb("w_b1raw", (1, DIN))
        s_ipbrow = psb("w_ipbrow", (1, 1024))
        s_tmpc = psb("w_tmpc", (128, 16))
        s_w1g = psb("w_w1g", (128, 8 * DIN), BF16)
        p_b = pctx.enter_context(nc.psum_tensor("w_pb", [1, 512], F32))
        p_t = pctx.enter_context(nc.psum_tensor("w_pt", [128, 8], F32))

        for o, i in [
            (s_identf[:], identf[:, :]), (s_ones[:], ones_row[:, :]),
            (s_sel16[:], sel16[:, :]),
            (s_g1[:], g1_cs[:, :]), (s_g2[:], g2_cs[:, :]),
            (s_negA[:], A_cs[:, :]), (s_Dcs[:], D_cs[:, :]),
            (s_dtpb[:], dtpb_cs[:, :]), (s_convw[:], convw_cs[:, :]),
            (s_cbeff[:], convb_cs[:, :]), (s_b1raw[:], b1_row[:, :]),
            (s_gmaskb[:], bass.AP(gmask.tensor, 0, [[0, 128], [1, 16]])),
            (s_b1c[:], b1ln_cs[:, :]), (s_b2c[:], b2ln_cs[:, :]),
        ]:
            s.do("sync", lambda e, o=o, i=i: e.dma_start(out=o, in_=i), inc=(ld, 16))
        for o, i in [
            (s_w1u[:], aW1T[:, :]),
            (s_sc2T[:], aW2T[:, :]),
            (s_ipTu[:], ipT_sl[:, :]),
            (s_WgT[:], ffn_WgT[:, :]),
            (s_xpT[:], xpT_sl[:, :]),
            (s_dtpT[:], dtpT_sl[:, :]),
            (s_b2row[:], b2_row[:, :]),
        ]:
            s.do("gpsimd", lambda e, o=o, i=i: e.dma_start(out=o, in_=i), inc=(wld, 16))
        for o, i in [
            (w1bf_d.ap().rearrange("p (e m) -> p e m", e=NE),
             ffn_W1.rearrange("e p m -> p e m")),
            (w2bf_d.ap().rearrange("p (e m) -> p e m", e=NE),
             ffn_W2.rearrange("e p m -> p e m")),
            (ws1a_d.ap(), ffn_Ws1a[:, :]),
            (ws1b_d.ap(), ffn_Ws1b[:, :]),
            (ws2_d.ap(), ffn_Ws2[:, :]),
            (opbf_d.ap(), opT[:, :]),
        ]:
            s.do("gpsimd", lambda e, o=o, i=i: e.dma_start(out=o, in_=i), inc=(wld, 16))

        s.do("scalar", lambda e: e.activation(s_w1u[:], s_w1u[:], AF.Sign),
             inc=(ev, 1), waits=[ld, wld])
        s.do("scalar", lambda e: e.activation(s_sc2T[:], s_sc2T[:], AF.Sign),
             inc=(ev, 1))
        s.do("scalar", lambda e: e.activation(s_negA[:], s_negA[:], AF.Exp),
             inc=(ev, 1))
        s.do("vector", lambda e: e.tensor_scalar_mul(s_negA[:], s_negA[:], -1.0),
             inc=(ve, 1), waits=[ev])
        for c in range(4):
            if c > 0:
                s.do("tensor", lambda e: e.nop(), waits=[ev])
            for k in range(8):
                s.do("tensor", lambda e, c=c, k=k: e.matmul(
                    p_b[:, :], s_b1c[:, k:k + 1],
                    s_w1u[:, k * DIN + c * 512:k * DIN + (c + 1) * 512],
                    start=(k == 0), stop=(k == 7)),
                    inc=(mm, 1) if k == 7 else None,
                    waits=[ev, ve] if (c == 0 and k == 0) else ())
            s.do("scalar", lambda e, c=c: e.copy(
                s_bacc[0:1, c * 512:(c + 1) * 512], p_b[0:1, :]), inc=(ev, 1), waits=[mm])
        for c in range(2):
            s.do("tensor", lambda e: e.nop(), waits=[ev])
            for k in range(8):
                s.do("tensor", lambda e, c=c, k=k: e.matmul(
                    p_b[:, :], s_b2c[:, k:k + 1],
                    s_ipTu[:, k * 1024 + c * 512:k * 1024 + (c + 1) * 512],
                    start=(k == 0), stop=(k == 7)),
                    inc=(mm, 1) if k == 7 else None)
            s.do("scalar", lambda e, c=c: e.copy(
                s_ipbrow[0:1, c * 512:(c + 1) * 512], p_b[0:1, :]), inc=(ev, 1), waits=[mm])
        s.do("vector", lambda e: e.tensor_add(s_bacc[0:1, :], s_bacc[0:1, :],
                                              s_b1raw[0:1, :]), inc=(ve, 1), waits=[ev])
        s.do("vector", lambda e: e.tensor_copy(s_b1eff[0:1, :], s_bacc[0:1, :]),
             inc=(ve, 1), waits=[ve])
        for k in range(8):
            s.do("scalar", lambda e, k=k: e.mul(
                s_w1g[:, k * DIN:(k + 1) * DIN], s_w1u[:, k * DIN:(k + 1) * DIN],
                s_g1[:, k:k + 1]), inc=(ev, 1), waits=[ev] if k == 0 else ())
        for k in range(8):
            s.do("scalar", lambda e, k=k: e.mul(
                s_ipTu[:, k * 1024:(k + 1) * 1024], s_ipTu[:, k * 1024:(k + 1) * 1024],
                s_g2[:, k:k + 1]), inc=(ev, 1))
        s.do("sync", lambda e: e.dma_start(out=sW1gT_d[:, :], in_=s_w1g[:]),
             inc=(st, 16), waits=[ev])
        s.do("sync", lambda e: e.dma_start(out=sW2T_d[:, :], in_=s_sc2T[:]),
             inc=(st, 16))
        s.do("sync", lambda e: e.dma_start(out=ipT_d[:, :], in_=s_ipTu[:]),
             inc=(st, 16))
        for h8 in range(8):
            s.do("tensor", lambda e, h=h8: e.transpose(
                p_t[:, h:h + 1], s_ipbrow[0:1, h * 128:(h + 1) * 128],
                s_identf[0:1, 0:1]),
                inc=(mm, 1) if h8 == 7 else None, waits=[ev] if h8 == 0 else ())
        s.do("vector", lambda e: e.tensor_add(s_tmpc[:, 0:4], s_convw[:, 0:16:4],
                                              s_convw[:, 1:16:4]), inc=(ve, 1), waits=[mm])
        s.do("vector", lambda e: e.tensor_add(s_tmpc[:, 4:8], s_convw[:, 2:16:4],
                                              s_convw[:, 3:16:4]), inc=(ve, 1))
        s.do("vector", lambda e: e.tensor_add(s_tmpc[:, 8:12], s_tmpc[:, 0:4],
                                              s_tmpc[:, 4:8]), inc=(ve, 1), waits=[ve])
        s.do("vector", lambda e: e.tensor_mul(s_tmpc[:, 12:16], p_t[:, 0:4],
                                              s_tmpc[:, 8:12]), inc=(ve, 1), waits=[ve])
        s.do("vector", lambda e: e.tensor_add(s_cbeff[:, 0:4], s_cbeff[:, 0:4],
                                              s_tmpc[:, 12:16]), inc=(ve, 1), waits=[ve])
        s.do("vector", lambda e: e.tensor_copy(s_ipbz[:, 0:4], p_t[:, 4:8]), inc=(ve, 1))

    def make_ffn(in_dram, out_dram, tag, in_ext=False, out_ext=False):
        def ld_ap(r0):
            if in_ext:
                return in_dram[r0:r0 + 512, :].rearrange("(a p) m -> p a m", p=128)
            return in_dram[:, r0 * 8:r0 * 8 + 4096]

        def st_ap(r0):
            if out_ext:
                return out_dram[r0:r0 + 512, :].rearrange("(a p) m -> p a m", p=128)
            return out_dram[:, r0 * 8:r0 * 8 + 4096]
        def phase(s, pctx):
            psb = lambda n, sh, dt=F32: pctx.enter_context(nc.sbuf_tensor(tag + n, list(sh), dt))
            pps = lambda n, sh: pctx.enter_context(nc.psum_tensor(tag + n, list(sh), F32))
            s_x = psb("f_x", (128, 4 * D))
            s_scr = psb("f_scr", (128, 2 * D))
            s_stats = psb("f_stats", (128, 128))
            s_xnT = psb("f_xnT", (128, 8 * TPH), BF16)
            s_sc = psb("f_sc", (128, NSUB * 16))
            s_es = psb("f_es", (128, NSUB * 16))
            s_msk = psb("f_msk", (128, NSUB * 16))
            s_cw = psb("f_cw", (128, NSUB * 16))
            s_cwT = psb("f_cwT", (16, TPH), BF16)
            s_w1e = [psb(f"f_w1e{i}", (128, 8 * 1024), BF16) for i in range(2)]
            s_w2e = [psb("f_w2e0", (128, 8 * 1024), BF16),
                     psb("f_w2e1", (128, 4 * 1024), BF16)]
            s_hu = psb("f_hu", (128, 2 * 8 * 512), BF16)
            s_sg = psb("f_sg", (128, 2 * 8 * 512), BF16)
            s_hgT = psb("f_hgT", (128, 2 * 8 * 512), BF16)
            s_cwb = psb("f_cwb", (128, 2 * 512), BF16)
            s_zacc = psb("f_zacc", (128, NSUB * 1024))
            p_up = [pps(f"f_pup{i}", (128, 512)) for i in range(3)]
            p_cw = pps("f_pcw", (128, 512))
            p_z = [pps(f"f_pz{i}", (128, 1024)) for i in range(2)]

            up_slot_ev = {}
            z_slot = {}
            cw_ev = [0]
            trx = {}
            trc = {}
            upg = [0]

            def up_pass(ci, wbuf, dst, first_waits):
                for mt in range(8):
                    slot = upg[0] % 3
                    upg[0] += 1
                    w = list(first_waits) if mt == 0 else []
                    if slot in up_slot_ev:
                        w.append((ev, up_slot_ev[slot]))
                    for kt in range(8):
                        s.do("tensor", lambda e, ci=ci, mt=mt, kt=kt, slot=slot, wbuf=wbuf: e.matmul(
                            p_up[slot][:, :],
                            wbuf[:, kt * 1024 + mt * 128:kt * 1024 + (mt + 1) * 128],
                            s_xnT[:, kt * TPH + ci * 512:kt * TPH + (ci + 1) * 512],
                            start=(kt == 0), stop=(kt == 7)),
                            inc=(mm, 1) if kt == 7 else None, waits=w)
                    kind, ap = dst(mt)
                    if kind == "copy":
                        s.do("scalar", lambda e, ap=ap, slot=slot: e.copy(
                            ap, p_up[slot][:, :]), inc=(ev, 1), waits=[mm])
                    else:
                        s.do("scalar", lambda e, ap=ap, slot=slot: e.activation(
                            ap, p_up[slot][:, :], AF.Silu), inc=(ev, 1), waits=[mm])
                    up_slot_ev[slot] = ev.vn

            def down_pass(ci, cpar, w2buf, nkt, zcopy):
                for sub in range(4):
                    slot = sub % 2
                    w = [(ve, ve.vn)] if sub == 0 else []
                    if slot in z_slot:
                        w.append(z_slot[slot])
                    first = True
                    for nh in range(2):
                        for kt in range(nkt):
                            s.do("tensor", lambda e, sub=sub, nh=nh, kt=kt, slot=slot, w2buf=w2buf, cpar=cpar: e.matmul(
                                p_z[slot][:, nh * 512:(nh + 1) * 512],
                                s_hgT[:, cpar * 4096 + kt * 512 + sub * 128:
                                      cpar * 4096 + kt * 512 + (sub + 1) * 128],
                                w2buf[:, kt * 1024 + nh * 512:kt * 1024 + (nh + 1) * 512],
                                start=(kt == 0), stop=(kt == nkt - 1)),
                                inc=(mm, 1) if (nh == 1 and kt == nkt - 1) else None,
                                waits=w if first else ())
                            first = False
                    zc = (ci * 4 + sub) * 1024
                    if zcopy:
                        s.do("vector", lambda e, zc=zc, slot=slot: e.tensor_copy(
                            s_zacc[:, zc:zc + 1024], p_z[slot][:, :]),
                            inc=(ve, 1), waits=[mm, st])
                    else:
                        s.do("vector", lambda e, zc=zc, slot=slot: e.tensor_add(
                            s_zacc[:, zc:zc + 1024], s_zacc[:, zc:zc + 1024],
                            p_z[slot][:, :]),
                            inc=(ve, 1), waits=[mm])
                    z_slot[slot] = (ve, ve.vn)

            def gate_pass(ci, cpar, n, with_cw):
                for mt in range(n):
                    w = [ev, mm] if mt == 0 else []
                    s.do("vector", lambda e, mt=mt, cpar=cpar: e.tensor_mul(
                        s_hgT[:, cpar * 4096 + mt * 512:cpar * 4096 + (mt + 1) * 512],
                        s_hu[:, cpar * 4096 + mt * 512:cpar * 4096 + (mt + 1) * 512],
                        s_sg[:, cpar * 4096 + mt * 512:cpar * 4096 + (mt + 1) * 512]),
                        inc=(ve, 1), waits=w)
                if with_cw:
                    for mt in range(n):
                        s.do("vector", lambda e, mt=mt, cpar=cpar: e.tensor_mul(
                            s_hgT[:, cpar * 4096 + mt * 512:cpar * 4096 + (mt + 1) * 512],
                            s_hgT[:, cpar * 4096 + mt * 512:cpar * 4096 + (mt + 1) * 512],
                            s_cwb[:, cpar * 512:(cpar + 1) * 512]),
                            inc=(ve, 1), waits=[ve] if mt == 0 else ())

            def cwrep(ci, cpar, ei):
                s.do("tensor", lambda e, ci=ci, ei=ei: e.matmul(
                    p_cw[:, :], s_sel16[:, ei * 128:(ei + 1) * 128],
                    s_cwT[0:16, ci * 512:(ci + 1) * 512],
                    start=True, stop=True),
                    inc=(mm, 1), waits=[(ev, cw_ev[0])])
                s.do("scalar", lambda e, cpar=cpar: e.copy(
                    s_cwb[:, cpar * 512:(cpar + 1) * 512], p_cw[:, :]),
                    inc=(ev, 1), waits=[mm, (ve, ve.vn)])
                cw_ev[0] = ev.vn

            dbg = FFN_DBG or {}
            nhv = dbg.get("nh", NH)
            nexp = dbg.get("ne", NE)
            do_shared = dbg.get("sh", True)
            do_epi = dbg.get("epi", True)
            do_rout = dbg.get("rout", True)
            do_exp = dbg.get("exp", True)
            hv_prev_mm = 0
            for hv in range(nhv):
                mmafter = {}
                tok0 = hv * TPH
                for ci in range(SPAN):
                    r0 = tok0 + ci * 512
                    s.do("sync", lambda e, r0=r0: e.dma_start(
                        out=(s_x[:].rearrange("p (a m) -> p a m", a=4) if in_ext
                             else s_x[:]),
                        in_=ld_ap(r0)),
                        inc=(ld, 16), waits=[ve, st, (mm, mm.vn)])
                    if dbg.get("ln", True):
                        emit_ln(s, s_x, s_stats, s_scr,
                                lambda sub: s_x[:, sub * D:(sub + 1) * D], ld)
                    if dbg.get("tr", True):
                        emit_tr(s, lambda sub, kt: s_x[:, sub * D + kt * 128:sub * D + (kt + 1) * 128],
                                lambda sub, kt, ci=ci: s_xnT[
                                    :, kt * TPH + ci * 512 + sub * 128:
                                    kt * TPH + ci * 512 + (sub + 1) * 128],
                                p_z[0], trx, first_wait=(ve, ve.vn),
                                evac_first_wait=(mm, mm.vn))
                if not do_rout:
                    continue
                for g in range(NSUB):
                    ci, sub = g // 4, g % 4
                    for kt in range(8):
                        w = [ev, (ev, cw_ev[0])] if (g == 0 and kt == 0) else []
                        s.do("tensor", lambda e, ci=ci, sub=sub, g=g, kt=kt: e.matmul(
                            p_cw[:, g * 16:(g + 1) * 16],
                            s_xnT[:, kt * TPH + ci * 512 + sub * 128:
                                  kt * TPH + ci * 512 + (sub + 1) * 128],
                            s_WgT[:, kt * 16:(kt + 1) * 16],
                            start=(kt == 0), stop=(kt == 7)),
                            inc=(mm, 1) if kt == 7 else None, waits=w)
                NCOL = NSUB * 16
                s.do("scalar", lambda e: e.copy(s_sc[:, 0:NCOL], p_cw[:, 0:NCOL]),
                     inc=(ev, 1), waits=[mm])
                cw_ev[0] = ev.vn
                for g in range(NSUB):
                    s.do("vector", lambda e, g=g: e.reduce_max(
                        s_stats[:, g:g + 1], s_sc[:, g * 16:(g + 1) * 16], axis=AX.X),
                        inc=(ve, 1), waits=[ev] if g == 0 else ())
                s.do("vector", lambda e: e.tensor_scalar_mul(
                    s_stats[:, 16:16 + NSUB], s_stats[:, 0:NSUB], -1.0),
                    inc=(ve, 1), waits=[ve])
                for g in range(NSUB):
                    s.do("scalar", lambda e, g=g: e.activation(
                        s_es[:, g * 16:(g + 1) * 16], s_sc[:, g * 16:(g + 1) * 16],
                        AF.Exp, bias=s_stats[:, 16 + g:17 + g],
                        accum_out=s_stats[:, 32 + g:33 + g]),
                        inc=(ev, 1), waits=[ve] if g == 0 else ())
                s.do("vector", lambda e: e.reciprocal(
                    s_stats[:, 48:48 + NSUB], s_stats[:, 32:32 + NSUB]),
                    inc=(ve, 1), waits=[ev])
                for g in range(NSUB):
                    s.do("vector", lambda e, g=g: e.tensor_single_scalar(
                        s_msk[:, g * 16:(g + 1) * 16], s_sc[:, g * 16:(g + 1) * 16],
                        s_stats[:, g:g + 1], ALU.is_ge),
                        inc=(ve, 1), waits=[ve] if g == 0 else ())
                s.do("vector", lambda e: e.scalar_tensor_tensor(
                    s_cw[:, 0:NCOL], s_msk[:, 0:NCOL], -1e30, s_sc[:, 0:NCOL],
                    ALU.mult, ALU.add), inc=(ve, 1), waits=[ve])
                for g in range(NSUB):
                    s.do("vector", lambda e, g=g: e.reduce_max(
                        s_stats[:, 64 + g:65 + g], s_cw[:, g * 16:(g + 1) * 16], axis=AX.X),
                        inc=(ve, 1), waits=[ve] if g == 0 else ())
                for g in range(NSUB):
                    s.do("vector", lambda e, g=g: e.tensor_single_scalar(
                        s_cw[:, g * 16:(g + 1) * 16], s_sc[:, g * 16:(g + 1) * 16],
                        s_stats[:, 64 + g:65 + g], ALU.is_ge),
                        inc=(ve, 1), waits=[ve] if g == 0 else ())
                s.do("vector", lambda e: e.tensor_max(
                    s_msk[:, 0:NCOL], s_msk[:, 0:NCOL], s_cw[:, 0:NCOL]),
                    inc=(ve, 1), waits=[ve])
                s.do("vector", lambda e: e.tensor_mul(
                    s_cw[:, 0:NCOL], s_es[:, 0:NCOL], s_msk[:, 0:NCOL]),
                    inc=(ve, 1), waits=[ve])
                for g in range(NSUB):
                    s.do("vector", lambda e, g=g: e.tensor_single_scalar(
                        s_cw[:, g * 16:(g + 1) * 16], s_cw[:, g * 16:(g + 1) * 16],
                        s_stats[:, 48 + g:49 + g], ALU.mult),
                        inc=(ve, 1), waits=[ve] if g == 0 else ())
                for g in range(NSUB):
                    slot = g % 2
                    w = [ve] if g == 0 else []
                    if slot in trc:
                        w.append((ev, trc[slot]))
                    s.do("tensor", lambda e, g=g, slot=slot: e.transpose(
                        p_z[1][0:16, slot * 512:slot * 512 + 128],
                        s_cw[:, g * 16:(g + 1) * 16], s_identf[:, :]),
                        inc=(mm, 1), waits=w)
                    s.do("scalar", lambda e, g=g, slot=slot: e.copy(
                        s_cwT[0:16, g * 128:(g + 1) * 128],
                        p_z[1][0:16, slot * 512:slot * 512 + 128]),
                        inc=(ev, 1), waits=[mm])
                    trc[slot] = ev.vn
                cw_ev[0] = ev.vn
                z_slot[0] = (ev, ev.vn)
                z_slot[1] = (ev, ev.vn)

                if not do_exp:
                    continue
                for ei in range(nexp):
                    par = ei % 2
                    bfree = (mm, mmafter.get(ei - 2, hv_prev_mm))
                    s.do("scalar", lambda e, ei=ei, par=par: e.dma_start(
                        out=s_w1e[par][:],
                        in_=w1bf_d[:, ei * 8192:(ei + 1) * 8192]),
                        inc=(wls, 16), waits=[bfree])
                    s.do("scalar", lambda e, ei=ei, par=par: e.dma_start(
                        out=s_w2e[par][:, 0:4 * 1024],
                        in_=w2bf_d[:, ei * 4096:(ei + 1) * 4096]),
                        inc=(wls, 16))
                    wneed = (wls, wls.vn)
                    for ci in range(SPAN):
                        cpar = ci % 2
                        dstx = (lambda mt, cpar=cpar:
                                ("copy", s_hu[:, cpar * 4096 + mt * 512:cpar * 4096 + (mt + 1) * 512])
                                if mt < 4 else
                                ("silu", s_sg[:, cpar * 4096 + (mt - 4) * 512:cpar * 4096 + (mt - 3) * 512]))
                        up_pass(ci, s_w1e[par], dstx, [wneed] if ci == 0 else [])
                        cwrep(ci, cpar, ei)
                        gate_pass(ci, cpar, 4, True)
                        if ci >= 1:
                            down_pass(ci - 1, (ci - 1) % 2, s_w2e[par], 4, (ei == 0))
                    down_pass(SPAN - 1, (SPAN - 1) % 2, s_w2e[par], 4, (ei == 0))
                    mmafter[ei] = mm.vn
                if not do_shared:
                    continue
                s.do("scalar", lambda e: e.dma_start(
                    out=s_w1e[0][:], in_=ws1a_d[:, :]),
                    inc=(wls, 16), waits=[(mm, mmafter.get(nexp - 2, hv_prev_mm))])
                s.do("scalar", lambda e: e.dma_start(
                    out=s_w2e[0][:], in_=ws1b_d[:, :]),
                    inc=(wls, 16))
                s.do("scalar", lambda e: e.dma_start(
                    out=s_w1e[1][:], in_=ws2_d[:, :]),
                    inc=(wls, 16), waits=[(mm, mmafter.get(nexp - 1, hv_prev_mm))])
                wneed = (wls, wls.vn)
                for ci in range(SPAN):
                    cpar = ci % 2
                    up_pass(ci, s_w1e[0],
                            lambda mt, cpar=cpar: ("copy", s_hu[:, cpar * 4096 + mt * 512:cpar * 4096 + (mt + 1) * 512]),
                            [wneed] if ci == 0 else [])
                    up_pass(ci, s_w2e[0],
                            lambda mt, cpar=cpar: ("silu", s_sg[:, cpar * 4096 + mt * 512:cpar * 4096 + (mt + 1) * 512]),
                            [])
                    gate_pass(ci, cpar, 8, False)
                    if ci >= 1:
                        down_pass(ci - 1, (ci - 1) % 2, s_w1e[1], 8, False)
                down_pass(SPAN - 1, (SPAN - 1) % 2, s_w1e[1], 8, False)
                hv_prev_mm = mm.vn
                if not do_epi:
                    continue
                for ci in range(SPAN):
                    r0 = tok0 + ci * 512
                    s.do("sync", lambda e, r0=r0: e.dma_start(
                        out=(s_x[:].rearrange("p (a m) -> p a m", a=4) if in_ext
                             else s_x[:]),
                        in_=ld_ap(r0)),
                        inc=(ld, 16), waits=[ve, st])
                    for sub in range(4):
                        zc = (ci * 4 + sub) * 1024
                        s.do("vector", lambda e, sub=sub, zc=zc: e.scalar_tensor_tensor(
                            s_x[:, sub * D:(sub + 1) * D], s_zacc[:, zc:zc + 1024],
                            0.5, s_x[:, sub * D:(sub + 1) * D], ALU.mult, ALU.add),
                            inc=(ve, 1), waits=[ld, ve] if sub == 0 else ())
                    s.do("sync", lambda e, r0=r0: e.dma_start(
                        out=st_ap(r0),
                        in_=(s_x[:].rearrange("p (a m) -> p a m", a=4) if out_ext
                             else s_x[:])),
                        inc=(st, 16), waits=[ve])
        return phase

    def phaseMLP(s, pctx):
        psb = lambda n, sh, dt=F32: pctx.enter_context(nc.sbuf_tensor(n, list(sh), dt))
        pps = lambda n, sh: pctx.enter_context(nc.psum_tensor(n, list(sh), F32))
        s_x = psb("m_x", (128, 4 * D))
        s_scr = psb("m_scr", (128, 2 * D))
        s_stats = psb("m_stats", (128, 128))
        s_ain = psb("m_ain", (128, NSUB * D))
        s_ainT = psb("m_ainT", (128, 8 * TPH), BF16)
        s_w1 = psb("m_w1", (128, 8 * DIN), BF16)
        s_w2 = psb("m_w2", (128, 16 * D), BF16)
        s_gatT = psb("m_gatT", (128, 16 * 512), BF16)
        p_up = [pps(f"m_pup{i}", (128, 512)) for i in range(3)]
        p_z = [pps(f"m_pz{i}", (128, 1024)) for i in range(2)]
        trx = {}
        up_slot_ve = {}
        z_slot = {}
        upg = [0]

        s.do("gpsimd", lambda e: e.dma_start(out=s_w1[:], in_=sW1gT_d[:, :]),
             inc=(wld, 16))
        s.do("gpsimd", lambda e: e.dma_start(out=s_w2[:], in_=sW2T_d[:, :]),
             inc=(wld, 16))
        wneed = (wld, wld.vn)
        gidx = [0]
        for hv in range(NH):
            tok0 = hv * TPH
            for ci in range(SPAN):
                r0 = tok0 + ci * 512
                s.do("sync", lambda e, r0=r0: e.dma_start(
                    out=s_x[:], in_=x1_d[:, r0 * 8:r0 * 8 + 4096]),
                    inc=(ld, 16), waits=[ve, st, (mm, mm.vn)])
                emit_ln(s, s_x, s_stats, s_scr,
                        lambda sub, ci=ci: s_ain[:, (ci * 4 + sub) * D:(ci * 4 + sub + 1) * D], ld)
                emit_tr(s, lambda sub, kt, ci=ci: s_ain[
                            :, (ci * 4 + sub) * D + kt * 128:(ci * 4 + sub) * D + (kt + 1) * 128],
                        lambda sub, kt, ci=ci: s_ainT[
                            :, kt * TPH + ci * 512 + sub * 128:
                            kt * TPH + ci * 512 + (sub + 1) * 128],
                        p_z[0], trx, first_wait=(ve, ve.vn),
                        evac_first_wait=(mm, mm.vn))
            z_slot[0] = (ev, ev.vn)
            z_slot[1] = (ev, ev.vn)
            for ci in range(SPAN):
                r0 = tok0 + ci * 512
                for mt in range(16):
                    slot = upg[0] % 3
                    upg[0] += 1
                    w = [ev] if (ci == 0 and mt == 0) else []
                    if hv == 0 and ci == 0 and mt == 0:
                        w.append(wneed)
                    if slot in up_slot_ve:
                        w.append((ve, up_slot_ve[slot]))
                    for kt in range(8):
                        s.do("tensor", lambda e, ci=ci, mt=mt, kt=kt, slot=slot: e.matmul(
                            p_up[slot][:, :],
                            s_w1[:, kt * DIN + mt * 128:kt * DIN + (mt + 1) * 128],
                            s_ainT[:, kt * TPH + ci * 512:kt * TPH + (ci + 1) * 512],
                            start=(kt == 0), stop=False),
                            inc=None, waits=w)
                    s.do("tensor", lambda e, mt=mt, slot=slot: e.matmul(
                        p_up[slot][:, :], s_b1eff[0:1, mt * 128:(mt + 1) * 128],
                        s_ones[0:1, 0:512], start=False, stop=True),
                        inc=(mm, 1))
                    s.do("vector", lambda e, mt=mt, slot=slot: e.tensor_scalar(
                        s_gatT[:, mt * 512:(mt + 1) * 512], p_up[slot][:, :],
                        -1.0, 1.0, ALU.max, ALU.min),
                        inc=(ve, 1), waits=[mm])
                    up_slot_ve[slot] = ve.vn
                for sub in range(4):
                    slot = sub % 2
                    w = [ve]
                    if slot in z_slot:
                        w.append(z_slot[slot])
                    first = True
                    for nh in range(2):
                        for kt in range(16):
                            s.do("tensor", lambda e, sub=sub, nh=nh, kt=kt, slot=slot: e.matmul(
                                p_z[slot][:, nh * 512:(nh + 1) * 512],
                                s_gatT[:, kt * 512 + sub * 128:kt * 512 + (sub + 1) * 128],
                                s_w2[:, kt * 1024 + nh * 512:kt * 1024 + (nh + 1) * 512],
                                start=(kt == 0), stop=False),
                                inc=None, waits=w if first else ())
                            first = False
                        s.do("tensor", lambda e, nh=nh, slot=slot: e.matmul(
                            p_z[slot][:, nh * 512:(nh + 1) * 512],
                            s_ones[0:1, 0:128],
                            s_b2row[0:1, nh * 512:(nh + 1) * 512],
                            start=False, stop=True),
                            inc=(mm, 1) if nh == 1 else None)
                    col = (ci * 4 + sub)
                    s.do("vector", lambda e, sub=sub, col=col, slot=slot: e.tensor_add(
                        s_x[:, sub * D:(sub + 1) * D], p_z[slot][:, :],
                        s_ain[:, col * D:(col + 1) * D]),
                        inc=(ve, 1),
                        waits=[mm, (st, st.vn)] if sub == 0 else [mm])
                    gc = gidx[0]
                    gidx[0] += 1
                    s.do("vector", lambda e, sub=sub, gc=gc: e.reduce_max(
                        s_gcols[:, gc:gc + 1], s_x[:, sub * D:(sub + 1) * D],
                        axis=AX.X, apply_absolute_value=True),
                        inc=(ve, 1), waits=[ve])
                    z_slot[slot] = (ve, ve.vn)
                s.do("sync", lambda e, r0=r0: e.dma_start(
                    out=u1_d[:, r0 * 8:r0 * 8 + 4096], in_=s_x[:]),
                    inc=(st, 16), waits=[ve])
        NG = gidx[0]
        s.do("vector", lambda e: e.memset(s_scr[:, 0:128], 0.0), inc=(ve, 1))
        s.do("vector", lambda e, NG=NG: e.reduce_max(
            s_scr[:, 0:1], s_gcols[:, 0:NG], axis=AX.X), inc=(ve, 1), waits=[ve])
        s.do("tensor", lambda e: e.transpose(
            p_up[0][:, 0:128], s_scr[:, 0:128], s_identf[:, :]),
            inc=(mm, 1), waits=[ve])
        s.do("vector", lambda e: e.reduce_max(
            s_stats[0:1, 0:1], p_up[0][0:1, 0:128], axis=AX.X),
            inc=(ve, 1), waits=[mm])
        s.do("vector", lambda e: e.tensor_single_scalar(
            s_gsc[0:1, 0:16], s_gmaskb[0:1, 0:16], s_stats[0:1, 0:1], ALU.mult),
            inc=(ve, 1), waits=[ve])
        s.do("sync", lambda e: e.dma_start(out=gam_in[:, :], in_=s_gsc[0:1, :]),
             inc=(st, 16), waits=[ve])

    def phaseC(kind, op, groups, src, dst, sem):
        def phase(s, pctx):
            s.do("gpsimd", lambda e: e.collective_compute(
                kind, op, replica_groups=groups,
                ins=[src.ap().opt()], outs=[dst.ap().opt()]),
                inc=(sem, "cc"))
            s.do("gpsimd", lambda e: e.nop(), waits=[sem])
        return phase

    def phaseP2(s, pctx):
        psb = lambda n, sh, dt=F32: pctx.enter_context(nc.sbuf_tensor(n, list(sh), dt))
        s_u1 = psb("q_u1", (128, 4 * D))
        s_x1 = psb("q_x1", (128, 4 * D))
        s_scr = psb("q_scr", (128, 2 * D))
        s_stats = psb("q_stats", (128, 128))
        s_gall = psb("q_gall", (128, 16))
        s_ulnT = psb("q_ulnT", (128, 8 * 512), BF16)
        p_t = pctx.enter_context(nc.psum_tensor("q_pt", [128, 1024], F32))
        trx = {}

        s.do("sync", lambda e: e.dma_start(
            out=s_gall[:], in_=bass.AP(gam_out, 0, [[0, 128], [1, 16]])),
            inc=(ld, 16))
        s.do("vector", lambda e: e.tensor_mul(s_gall[:], s_gall[:], s_gmaskb[:]),
             inc=(ve, 1), waits=[ld])
        s.do("vector", lambda e: e.reduce_sum(s_qscale[:], s_gall[:], axis=AX.X),
             inc=(ve, 1), waits=[ve])
        s.do("vector", lambda e: e.tensor_scalar_add(s_qscale[:], s_qscale[:], EPS),
             inc=(ve, 1), waits=[ve])
        s.do("vector", lambda e: e.reciprocal(s_qscale[:], s_qscale[:]),
             inc=(ve, 1), waits=[ve])
        s.do("vector", lambda e: e.tensor_scalar_mul(s_qscale[:], s_qscale[:], QB),
             inc=(ve, 1), waits=[ve])
        for ci in range(NCH1):
            r0 = ci * 512
            s.do("sync", lambda e, r0=r0: e.dma_start(
                out=s_u1[:], in_=u1_d[:, r0 * 8:r0 * 8 + 4096]),
                inc=(ld, 16), waits=[ve, st, (mm, mm.vn)])
            s.do("sync", lambda e, r0=r0: e.dma_start(
                out=s_x1[:], in_=x1_d[:, r0 * 8:r0 * 8 + 4096]),
                inc=(ld, 16))
            s.do("vector", lambda e: e.tensor_scalar(
                s_u1[:], s_u1[:], s_qscale[:, 0:1], -(QB - EPS), ALU.mult, ALU.max),
                inc=(ve, 1), waits=[ld, ve])
            s.do("vector", lambda e: e.tensor_scalar_min(s_u1[:], s_u1[:], QB - EPS),
                 inc=(ve, 1), waits=[ve])
            s.do("vector", lambda e: e.tensor_add(s_x1[:], s_x1[:], s_u1[:]),
                 inc=(ve, 1), waits=[ve])
            s.do("sync", lambda e, r0=r0: e.dma_start(
                out=x2p_d[:, r0 * 8:r0 * 8 + 4096], in_=s_x1[:]),
                inc=(st, 16), waits=[ve])
            emit_ln(s, s_u1, s_stats, s_scr,
                    lambda sub: s_u1[:, sub * D:(sub + 1) * D], ve)
            emit_tr(s, lambda sub, kt: s_u1[:, sub * D + kt * 128:sub * D + (kt + 1) * 128],
                    lambda sub, kt: s_ulnT[:, kt * 512 + sub * 128:kt * 512 + (sub + 1) * 128],
                    p_t, trx, first_wait=(ve, ve.vn),
                    evac_first_wait=(st, st.vn))
            s.do("sync", lambda e, r0=r0: e.dma_start(
                out=ulnag_in[:, r0 * 8:r0 * 8 + 4096], in_=s_ulnT[:]),
                inc=(st, 16), waits=[ev])

    cc3s = [DSem(nc, ctx, f"cc3_{i}") for i in range(NCH4)]

    def phaseP34(s, pctx):
        psb = lambda n, sh, dt=F32: pctx.enter_context(nc.sbuf_tensor(n, list(sh), dt))
        pps = lambda n, sh: pctx.enter_context(nc.psum_tensor(n, list(sh), F32))
        s_ipw = psb("p3_ipw", (128, 8 * 1024), BF16)
        s_uln = [psb("p3_uln0", (128, 8 * 512), BF16)]
        s_rng = psb("p3_rng", (128, 4 * 516))
        s_ub = psb("p3_ub", (128, 512), BF16)
        s_sz = psb("p3_sz", (128, 4 * 512))
        s_xda = psb("p3_xda", (96, CH4))
        p_up = [pps(f"p3_pup{i}", (128, 512)) for i in range(3)]
        p_xp = pps("p3_pxp", (96, 512))
        s_u4 = [psb(f"p4_u{i}", (128, 4 * CH4)) for i in range(2)]
        s_dsz = psb("p4_dsz", (128, 4 * CH4))
        s_y4 = psb("p4_y4", (128, 4 * CH4))
        s_a4 = psb("p4_a4", (128, 4 * CH4))
        s_b4 = psb("p4_b4", (128, 4 * CH4))
        s_h4 = [psb(f"p4_h{i}", (128, CH4)) for i in range(4)]
        s_ym = [psb(f"p4_ym{i}", (128, CH4)) for i in range(2)]
        s_ygb = psb("p4_ygb", (128, 4 * CH4), BF16)
        s_xd4 = [psb(f"p4_xd{i}", (96, CH4)) for i in range(2)]
        s_ab = psb("p4_ab", (128, 512))
        s_dtin = psb("p4_dtin", (64, CH4), BF16)
        s_selb = psb("p4_selb", (32, 32 * 128), BF16)
        s_bcb = s_dtin
        p_br = pps("p4_pbr", (128, CH4))
        p_cr = pps("p4_pcr", (128, CH4))
        NQ = CH4 // 512
        NB = T_loc // 512

        up_slot_ev = {}
        upg = [0]
        xp_ev = [0]
        hinit_done = [0]
        u_ready = {}
        u_freed = {}

        s.do("gpsimd", lambda e: e.dma_start(out=s_ipw[:], in_=ipT_d[:, :]),
             inc=(wld, 16))
        wneed = (wld, wld.vn)
        s.do("sync", lambda e: e.dma_start(out=s_selb[:], in_=selbc[:, :]),
             inc=(ld, 16))

        def p3_part(gi):
            r, ci3 = gi // NB, gi % NB
            c4 = gi // 2
            half = gi % 2
            buf = s_uln[0]
            ub4 = s_u4[c4 % 2]
            s.do("sync", lambda e, r=r, ci3=ci3, buf=buf: e.dma_start(
                out=buf[:],
                in_=ulnag_out[r * 128:(r + 1) * 128,
                              ci3 * 4096:(ci3 + 1) * 4096]),
                inc=(ld, 16), waits=[(mm, mm.vn)])
            lneed = (ld, ld.vn)
            for mt in range(8):
                slot = upg[0] % 3
                upg[0] += 1
                w = [wneed, lneed] if mt == 0 else []
                if slot in up_slot_ev:
                    w.append((ev, up_slot_ev[slot]))
                for kt in range(8):
                    s.do("tensor", lambda e, mt=mt, kt=kt, slot=slot, buf=buf: e.matmul(
                        p_up[slot][:, :],
                        s_ipw[:, kt * 1024 + mt * 128:kt * 1024 + (mt + 1) * 128],
                        buf[:, kt * 512:(kt + 1) * 512],
                        start=(kt == 0), stop=(kt == 7)),
                        inc=(mm, 1) if kt == 7 else None, waits=w)
                if mt < 4:
                    s.do("scalar", lambda e, mt=mt, slot=slot: e.copy(
                        s_rng[:, mt * 516 + 3:mt * 516 + 515], p_up[slot][:, :]),
                        inc=(ev, 1), waits=[mm, (ve, ve.vn)])
                else:
                    ct = mt - 4
                    s.do("scalar", lambda e, ct=ct, slot=slot: e.activation(
                        s_sz[:, ct * 512:(ct + 1) * 512], p_up[slot][:, :],
                        AF.Silu, bias=s_ipbz[:, ct:ct + 1]),
                        inc=(ev, 1), waits=[mm, (st, st.vn)])
                up_slot_ev[slot] = ev.vn
            if gi == 0:
                for ct in range(4):
                    s.do("vector", lambda e, ct=ct: e.memset(
                        s_rng[:, ct * 516:ct * 516 + 3], 0.0), inc=(ve, 1))
            for ct in range(4):
                base = ct * 516
                uap = lambda ct=ct: ub4[:, ct * CH4 + half * 512:ct * CH4 + half * 512 + 512]
                w = [ev, ve] if ct == 0 else []
                if ct == 0 and (c4 - 2) in u_freed:
                    w.append((ve, u_freed[c4 - 2]))
                s.do("vector", lambda e, ct=ct, base=base, uap=uap: e.tensor_single_scalar(
                    uap(), s_rng[:, base:base + 512],
                    s_convw[:, ct * 4:ct * 4 + 1], ALU.mult),
                    inc=(ve, 1), waits=w)
                for k in range(1, 4):
                    s.do("vector", lambda e, ct=ct, base=base, k=k, uap=uap: e.scalar_tensor_tensor(
                        uap(), s_rng[:, base + k:base + k + 512],
                        s_convw[:, ct * 4 + k:ct * 4 + k + 1],
                        uap(), ALU.mult, ALU.add),
                        inc=(ve, 1), waits=[ve])
            for ct in range(4):
                s.do("scalar", lambda e, ct=ct: e.copy(
                    s_rng[:, ct * 516:ct * 516 + 3],
                    s_rng[:, ct * 516 + 512:ct * 516 + 515]),
                    inc=(ev, 1), waits=[(ve, ve.vn)] if ct == 0 else ())
            for ct in range(4):
                uap = lambda ct=ct: ub4[:, ct * CH4 + half * 512:ct * CH4 + half * 512 + 512]
                s.do("scalar", lambda e, ct=ct, uap=uap: e.activation(
                    uap(), uap(),
                    AF.Silu, bias=s_cbeff[:, ct:ct + 1]),
                    inc=(ev, 1), waits=[(ve, ve.vn)] if ct == 0 else ())
            if half == 1:
                u_ready[c4] = ev.vn
            s.do("sync", lambda e, gi=gi: e.dma_start(
                out=sz_d.ap().rearrange("(a p) t -> p a t", p=128)[:, :, gi * 512:(gi + 1) * 512],
                in_=s_sz[:].rearrange("p (a m) -> p a m", a=4)),
                inc=(st, 16), waits=[ev])
            for kt in range(4):
                s.do("scalar", lambda e, kt=kt, ub4=ub4, half=half: e.copy(
                    s_ub[:, :],
                    ub4[:, kt * CH4 + half * 512:kt * CH4 + half * 512 + 512]),
                    inc=(ev, 1), waits=[(mm, mm.vn), ev])
                w = [ev]
                if kt == 0 and xp_ev[0]:
                    w.append((ev, xp_ev[0]))
                s.do("tensor", lambda e, kt=kt: e.matmul(
                    p_xp[:, :], s_xpT[:, kt * 96:(kt + 1) * 96],
                    s_ub[:, :],
                    start=(kt == 0), stop=(kt == 3)),
                    inc=(mm, 1), waits=w)
            s.do("scalar", lambda e, half=half: e.copy(
                s_xda[:, half * 512:(half + 1) * 512], p_xp[0:96, :]),
                inc=(ev, 1), waits=[mm, (st, st.vn)] if half == 0 else [mm])
            xp_ev[0] = ev.vn

        def c3_part(c4):
            s.do("sync", lambda e, c4=c4: e.dma_start(
                out=xdbl_in[c4 * 96:(c4 + 1) * 96, :], in_=s_xda[:]),
                inc=(st, 16), waits=[(ev, xp_ev[0])])
            s.do("gpsimd", lambda e, c4=c4: e.collective_compute(
                "AllReduce", ALU.add, replica_groups=G4,
                ins=[xdbl_in.ap()[c4 * 96:(c4 + 1) * 96, :].opt()],
                outs=[xdbl_out.ap()[c4 * 96:(c4 + 1) * 96, :].opt()]),
                inc=(cc3s[c4], "cc"), waits=[(st, st.vn)])

        def p4_part(c4):
            ubuf, xdb = s_u4[c4 % 2], s_xd4[c4 % 2]
            s.do("sync", lambda e, c4=c4, xdb=xdb: e.dma_start(
                out=xdb[:], in_=xdbl_out[c4 * 96:(c4 + 1) * 96, :]),
                inc=(ld, 16), waits=[(cc3s[c4], 1), (mm, mm.vn), (ev, ev.vn)])
            lneed = (ld, ld.vn)
            uneed = (ev, u_ready[c4])
            s.do("scalar", lambda e, xdb=xdb: e.copy(s_dtin[:, :], xdb[0:64, :]),
                 inc=(ev, 1), waits=[lneed, (mm, mm.vn)])
            dtin_done = (ev, ev.vn)
            regions = [p_br[:, 0:512], p_br[:, 512:1024],
                       p_cr[:, 0:512], p_cr[:, 512:1024]]
            dt_reg_ev = {}
            for mt in range(4):
                for nq in range(NQ):
                    gidx = mt * NQ + nq
                    reg = gidx % 4
                    w = []
                    if gidx == 0:
                        w += [dtin_done, (ve, ve.vn), (gp, gp.vn)]
                    if reg in dt_reg_ev:
                        w.append((ev, dt_reg_ev[reg]))
                    s.do("tensor", lambda e, mt=mt, nq=nq, reg=reg: e.matmul(
                        regions[reg], s_dtpT[:, mt * 128:(mt + 1) * 128],
                        s_dtin[:, nq * 512:(nq + 1) * 512],
                        start=True, stop=True),
                        inc=(mm, 1), waits=w)
                    cc0 = mt * CH4 + nq * 512
                    s.do("scalar", lambda e, mt=mt, cc0=cc0, reg=reg: e.activation(
                        s_dsz[:, cc0:cc0 + 512], regions[reg], AF.Relu,
                        bias=s_dtpb[:, mt:mt + 1]),
                        inc=(ev, 1),
                        waits=[mm, (ve, ve.vn), (st, st.vn)] if gidx == 0 else [mm])
                    s.do("scalar", lambda e, mt=mt, reg=reg: e.activation(
                        s_ab[:, 0:512], regions[reg], AF.Abs,
                        bias=s_dtpb[:, mt:mt + 1]),
                        inc=(ev, 1), waits=[(ve, ve.vn)])
                    s.do("scalar", lambda e: e.activation(
                        s_ab[:, 0:512], s_ab[:, 0:512], AF.Exp, scale=-1.0),
                        inc=(ev, 1), waits=[ev])
                    s.do("scalar", lambda e: e.activation(
                        s_ab[:, 0:512], s_ab[:, 0:512], AF.Ln, bias=1.0),
                        inc=(ev, 1), waits=[ev])
                    dt_reg_ev[reg] = ev.vn
                    s.do("vector", lambda e, cc0=cc0: e.tensor_add(
                        s_dsz[:, cc0:cc0 + 512], s_dsz[:, cc0:cc0 + 512],
                        s_ab[:, 0:512]),
                        inc=(ve, 1), waits=[ev])
            dt_done_ve = (ve, ve.vn)
            s.do("scalar", lambda e, xdb=xdb: e.copy(s_bcb[0:32, :], xdb[64:96, :]),
                 inc=(ev, 1), waits=[lneed, (mm, mm.vn)])
            bc_done = (ev, ev.vn)
            prev_hinit = hinit_done[0]
            b_done = {}
            ym_done = {}
            scan_done = {}
            ymv = {}
            ya_done = {}
            for sv in range(S):
                for nq in range(NQ):
                    w = []
                    if sv == 0 and nq == 0:
                        w += [bc_done, (ev, ev.vn)]
                    if sv >= 1 and nq == 0:
                        w.append((ve, b_done[sv - 1]))
                    s.do("tensor", lambda e, sv=sv, nq=nq: e.matmul(
                        p_br[:, nq * 512:(nq + 1) * 512],
                        s_selb[:, sv * 128:(sv + 1) * 128],
                        s_bcb[0:32, nq * 512:(nq + 1) * 512],
                        start=True, stop=True),
                        inc=(mm, 1), waits=w)
                br_mm = (mm, mm.vn)
                for ct in range(4):
                    w = []
                    if (ct, sv - 1) in scan_done:
                        w.append((ve, scan_done[(ct, sv - 1)]))
                    if sv == 0 and ct == 0:
                        w.append(dt_done_ve)
                    s.do("scalar", lambda e, ct=ct, sv=sv: e.activation(
                        s_a4[:, ct * CH4:(ct + 1) * CH4],
                        s_dsz[:, ct * CH4:(ct + 1) * CH4],
                        AF.Exp, scale=s_negA[:, ct * 16 + sv:ct * 16 + sv + 1]),
                        inc=(ev, 1), waits=w)
                a_done = (ev, ev.vn)
                for ct in range(4):
                    w = [br_mm] if ct == 0 else []
                    if sv == 0 and ct == 0:
                        w.append(uneed)
                    s.do("vector", lambda e, ct=ct, ubuf=ubuf: e.tensor_mul(
                        s_b4[:, ct * CH4:(ct + 1) * CH4],
                        ubuf[:, ct * CH4:(ct + 1) * CH4], p_br[:, :]),
                        inc=(ve, 1), waits=w)
                b_done[sv] = ve.vn
                for ct in range(4):
                    col = ct * 16 + sv
                    init = 0.0 if c4 == 0 else s_hinit[:, col:col + 1]
                    w = [a_done] if ct == 0 else []
                    if c4 > 0 and sv == 0 and ct == 0:
                        w.append((ev, prev_hinit))
                    if sv >= 1 and (sv - 1, ct) in ymv:
                        w.append((ve, ymv[(sv - 1, ct)]))
                    if ct == 0:
                        w.append(ve)
                    s.do("vector", lambda e, ct=ct, init=init: e.tensor_tensor_scan(
                        s_h4[ct][:, :], s_a4[:, ct * CH4:(ct + 1) * CH4],
                        s_b4[:, ct * CH4:(ct + 1) * CH4], init,
                        ALU.mult, ALU.add),
                        inc=(ve, 1), waits=w)
                    scan_done[(ct, sv)] = ve.vn
                    s.do("scalar", lambda e, ct=ct, col=col: e.copy(
                        s_hinit[:, col:col + 1], s_h4[ct][:, CH4 - 1:CH4]),
                        inc=(ev, 1), waits=[(ve, scan_done[(ct, sv)])])
                for nq in range(NQ):
                    w = []
                    if nq == 0:
                        w.append((ve, ym_done[sv - 1]) if sv >= 1 else bc_done)
                    s.do("tensor", lambda e, sv=sv, nq=nq: e.matmul(
                        p_cr[:, nq * 512:(nq + 1) * 512],
                        s_selb[:, (16 + sv) * 128:(17 + sv) * 128],
                        s_bcb[0:32, nq * 512:(nq + 1) * 512],
                        start=True, stop=True),
                        inc=(mm, 1), waits=w)
                cr_mm = (mm, mm.vn)
                if sv == 0:
                    for ct in range(4):
                        w = [cr_mm, ve, (gp, gp.vn)] if ct == 0 else ()
                        s.do("vector", lambda e, ct=ct: e.tensor_mul(
                            s_y4[:, ct * CH4:(ct + 1) * CH4], s_h4[ct][:, :],
                            p_cr[:, :]),
                            inc=(ve, 1), waits=w)
                        ymv[(sv, ct)] = ve.vn
                else:
                    ya_cur = {}
                    for kind, ct in [("m", 0), ("m", 1), ("a", 0), ("m", 2),
                                     ("a", 1), ("m", 3), ("a", 2), ("a", 3)]:
                        if kind == "m":
                            w = [cr_mm, ve] if ct == 0 else []
                            w.append((gp, ya_done.get((sv - 1, (ct % 2) + 2), 0)))
                            if ct >= 2:
                                w.append((gp, ya_cur[ct - 2]))
                            s.do("vector", lambda e, ct=ct: e.tensor_mul(
                                s_ym[ct % 2][:, :], s_h4[ct][:, :], p_cr[:, :]),
                                inc=(ve, 1), waits=w)
                            ymv[(sv, ct)] = ve.vn
                        else:
                            s.do("gpsimd", lambda e, ct=ct: e.tensor_add(
                                s_y4[:, ct * CH4:(ct + 1) * CH4],
                                s_y4[:, ct * CH4:(ct + 1) * CH4],
                                s_ym[ct % 2][:, :]),
                                inc=(gp, 1), waits=[(ve, ymv[(sv, ct)]), gp])
                            ya_done[(sv, ct)] = gp.vn
                            ya_cur[ct] = gp.vn
                ym_done[sv] = ve.vn
            hinit_done[0] = ev.vn
            s.do("sync", lambda e, c4=c4: e.dma_start(
                out=s_dsz[:].rearrange("p (a m) -> p a m", a=4),
                in_=sz_d.ap().rearrange("(a p) t -> p a t", p=128)[
                    :, :, c4 * CH4:(c4 + 1) * CH4]),
                inc=(ld, 16), waits=[(ve, ve.vn), (ev, ev.vn)])
            szneed = (ld, ld.vn)
            for ct in range(4):
                s.do("vector", lambda e, ct=ct, ubuf=ubuf: e.scalar_tensor_tensor(
                    s_y4[:, ct * CH4:(ct + 1) * CH4],
                    ubuf[:, ct * CH4:(ct + 1) * CH4],
                    s_Dcs[:, ct:ct + 1], s_y4[:, ct * CH4:(ct + 1) * CH4],
                    ALU.mult, ALU.add),
                    inc=(ve, 1), waits=[ve, gp] if ct == 0 else ())
            u_freed[c4] = ve.vn
            for ct in range(4):
                s.do("vector", lambda e, ct=ct: e.tensor_mul(
                    s_ygb[:, ct * CH4:(ct + 1) * CH4],
                    s_y4[:, ct * CH4:(ct + 1) * CH4],
                    s_dsz[:, ct * CH4:(ct + 1) * CH4]),
                    inc=(ve, 1), waits=[szneed, (st, st.vn), ve] if ct == 0 else ())
            s.do("sync", lambda e, c4=c4: e.dma_start(
                out=ygag_in.ap().rearrange("(a p) t -> p a t", p=128)[
                    :, :, c4 * CH4:(c4 + 1) * CH4],
                in_=s_ygb[:].rearrange("p (a m) -> p a m", a=4)),
                inc=(st, 16), waits=[ve])

        for c4 in range(NCH4):
            p3_part(2 * c4)
            p3_part(2 * c4 + 1)
            c3_part(c4)
            if c4 >= 1:
                p4_part(c4 - 1)
        p4_part(NCH4 - 1)

    def phaseP5a(s, pctx):
        psb = lambda n, sh, dt=F32: pctx.enter_context(nc.sbuf_tensor(n, list(sh), dt))
        pps = lambda n, sh: pctx.enter_context(nc.psum_tensor(n, list(sh), F32))
        s_opw = psb("p5_opw", (128, 16 * 1024), BF16)
        s_yg = [psb(f"p5_yg{i}", (128, 16 * 512), BF16) for i in range(2)]
        s_x2p = psb("p5_x2p", (128, 4 * D))
        p_z = [pps(f"p5_pz{i}", (128, 1024)) for i in range(2)]
        z_slot = {}

        s.do("scalar", lambda e: e.dma_start(
            out=s_opw[:], in_=opbf_d[:, :]),
            inc=(wls, 16))
        wneed = (wls, wls.vn)
        qcell = []

        def ygload(e, ci, buf):
            if not qcell:
                qcell.append(e.snap(nc.partition_id() % 4 * T_loc))
            return e.dma_start(
                out=buf[:].rearrange("p (a m) -> p a m", a=16),
                in_=ygag_out.ap().rearrange("(a p) t -> p a t", p=128)[
                    :, :, bass.ds(qcell[0] + ci * 512, 512)])

        for ci in range(NCH1):
            buf = s_yg[ci % 2]
            s.do("sync", lambda e, ci=ci, buf=buf: ygload(e, ci, buf),
                 inc=(ld, 16), waits=[(mm, mm.vn)])
            s.do("sync", lambda e, ci=ci: e.dma_start(
                out=s_x2p[:], in_=x2p_d[:, ci * 4096:(ci + 1) * 4096]),
                inc=(ld, 16), waits=[(ve, ve.vn), (st, st.vn)])
            lneed = (ld, ld.vn)
            for sub in range(4):
                slot = sub % 2
                w = [lneed, wneed] if sub == 0 else []
                if slot in z_slot:
                    w.append(z_slot[slot])
                first = True
                for nh in range(2):
                    for kt in range(16):
                        s.do("tensor", lambda e, sub=sub, nh=nh, kt=kt, slot=slot, buf=buf: e.matmul(
                            p_z[slot][:, nh * 512:(nh + 1) * 512],
                            buf[:, kt * 512 + sub * 128:kt * 512 + (sub + 1) * 128],
                            s_opw[:, kt * 1024 + nh * 512:kt * 1024 + (nh + 1) * 512],
                            start=(kt == 0), stop=(kt == 15)),
                            inc=(mm, 1) if (nh == 1 and kt == 15) else None,
                            waits=w if first else ())
                        first = False
                s.do("vector", lambda e, sub=sub, slot=slot: e.tensor_add(
                    s_x2p[:, sub * D:(sub + 1) * D], s_x2p[:, sub * D:(sub + 1) * D],
                    p_z[slot][:, :]),
                    inc=(ve, 1), waits=[mm])
                z_slot[slot] = (ve, ve.vn)
            s.do("sync", lambda e, ci=ci: e.dma_start(
                out=x2_d[:, ci * 4096:(ci + 1) * 4096], in_=s_x2p[:]),
                inc=(st, 16), waits=[ve])

    phases = [
        phaseW,
        make_ffn(x_sh, x1_d.ap(), "fa", in_ext=True),
        phaseMLP,
        phaseC("AllReduce", ALU.max, G8, gam_in, gam_out, cc1),
        phaseP2,
        phaseC("AllGather", ALU.bypass, G4, ulnag_in, ulnag_out, cc2),
        phaseP34,
        phaseC("AllGather", ALU.bypass, G4, ygag_in, ygag_out, cc4),
        phaseP5a,
        make_ffn(x2_d.ap(), out_sh, "fb", out_ext=True),
    ]
    np_ = NPHASES if NPHASES is not None else len(phases)
    for ph in phases[:np_]:
        run_phase(ph)

    for sem in SEM.values():
        assert sem.vn < 60000, (sem.name, sem.vn)
    ctx.close()
    return nc


def host_inputs(inputs, L):
    T_loc = L // 4
    g = lambda k: np.asarray(inputs[k])
    x = g("L0_attn_x")
    co = np.ascontiguousarray

    def cs(v, nk):
        return co(v.reshape(nk, 128).T)

    W1 = g("L0_attn_W1")
    W2 = g("L0_attn_W2")

    def tiles(wT, nk):
        K, M = wT.shape
        assert K == nk * 128
        return co(wT.reshape(nk, 128, M).transpose(1, 0, 2).reshape(128, nk * M))

    Ws1 = g("L0_ffn_Ws1")
    base = dict(
        ffn_W1=co(np.stack([tiles(g("L0_ffn_W1")[e], 8) for e in range(NE)])),
        ffn_W2=co(np.stack([tiles(g("L0_ffn_W2")[e], 4) for e in range(NE)])),
        ffn_Ws1a=tiles(Ws1[:, 0:HS], 8),
        ffn_Ws1b=tiles(Ws1[:, HS:2 * HS], 8),
        ffn_Ws2=tiles(g("L0_ffn_Ws2"), 8),
        ffn_WgT=tiles(co(g("L0_ffn_Wg").T), 8),
        aW1T=tiles(co(W1.T), 8),
        aW2T=tiles(co(W2.T), 16),
        b1_row=co(g("L0_attn_b1").reshape(1, DIN)),
        b2_row=co(g("L0_attn_b2").reshape(1, D)),
        b1ln_cs=cs(g("L0_attn_ln1_b"), 8).astype(BF),
        g1_cs=cs(g("L0_attn_ln1_g"), 8),
        b2ln_cs=cs(g("L0_attn_ln2_b"), 8).astype(BF),
        g2_cs=cs(g("L0_attn_ln2_g"), 8),
        opT=tiles(co(g("L0_attn_out_proj_w").T), 16),
        identf=np.eye(128, dtype=np.float32),
        ones_row=np.ones((1, 512), dtype=BF),
    )
    sel16 = np.zeros((16, 16 * 128), dtype=BF)
    for e in range(16):
        sel16[e, e * 128:(e + 1) * 128] = 1.0
    base["sel16"] = sel16
    selbc = np.zeros((32, 32 * 128), dtype=BF)
    for j in range(32):
        selbc[j, j * 128:(j + 1) * 128] = 1.0
    base["selbc"] = selbc

    ipw = g("L0_attn_in_proj_w")
    xpw = g("L0_attn_x_proj_w")
    dtw = g("L0_attn_dt_proj_w")
    dtb = g("L0_attn_dt_proj_b")
    cw = g("L0_attn_conv_w")[:, 0, :]
    cb = g("L0_attn_conv_b")
    Al = g("L0_attn_A_log")
    Dp = g("L0_attn_D_param")

    maps = []
    for c in range(NCORES):
        b, q = c // 4, c % 4
        sl = slice(q * 512, (q + 1) * 512)
        ip_sl = np.concatenate([ipw[sl], ipw[DIN + q * 512:DIN + (q + 1) * 512]], axis=0)
        gm = np.zeros((1, 16), dtype=np.float32)
        gm[0, q] = 1.0
        m = dict(base)
        m.update(
            x_sh=co(x[b, q * T_loc:(q + 1) * T_loc]),
            ipT_sl=tiles(co(ip_sl.T), 8),
            xpT_sl=tiles(co(xpw[:, sl].T), 4),
            dtpT_sl=co(dtw[sl].T),
            dtpb_cs=cs(dtb[sl], 4),
            convw_cs=co(cw[sl].reshape(4, 128, 4).transpose(1, 0, 2).reshape(128, 16)),
            convb_cs=cs(cb[sl], 4),
            A_cs=co(Al[sl].reshape(4, 128, 16).transpose(1, 0, 2).reshape(128, 64)),
            D_cs=cs(Dp[sl], 4),
            gmask=gm,
        )
        maps.append(m)
    return maps


def kernel(**inputs):
    L = L_FULL
    nc = build_program(L)
    maps = host_inputs(inputs, L)
    res = run_bass_kernel_spmd(nc, maps, core_ids=list(range(NCORES)))
    T_loc = L // 4
    out = np.empty((2, L, D), dtype=np.float32)
    for c in range(NCORES):
        b, q = c // 4, c % 4
        out[b, q * T_loc:(q + 1) * T_loc] = res.results[c]["out_sh"]
    return out


# version 11
# speedup vs baseline: 1.0722x; 1.0100x over previous
import numpy as np
import ml_dtypes
import contextlib

import concourse.bass as bass
import concourse.mybir as mybir
from concourse.bass_utils import run_bass_kernel_spmd

F32 = mybir.dt.float32
BF16 = mybir.dt.bfloat16
AF = mybir.ActivationFunctionType
ALU = mybir.AluOpType
AX = mybir.AxisListType

D = 1024
DIN = 2048
S = 16
RK = 64
NE = 16
H = 512
HS = 1024
EPS = 1e-5
QB = 128.0
NCORES = 8
BF = ml_dtypes.bfloat16
ENGS = ("sync", "tensor", "vector", "scalar", "gpsimd")
L_FULL = 16384
NPHASES = None
FFN_DBG = None


class DSem:
    def __init__(self, nc, ctx, name):
        self.h = ctx.enter_context(nc.semaphore(name))
        self.vn = 0
        self.name = name


class Script:

    def __init__(self):
        self.q = {e: [] for e in ENGS}

    def do(self, eng, fn, inc=None, waits=(), tag=""):
        wl = []
        for w in waits:
            if isinstance(w, DSem):
                wl.append((w, w.vn))
            else:
                wl.append(w)
        target = None
        if inc is not None:
            sem, k = inc
            sem.vn += 1 if k == "cc" else k
            target = (sem, k)

        def run(e, wl=wl, fn=fn, target=target, tag=tag):
            for (ws, wn) in wl:
                if wn > 0:
                    e.wait_ge(ws.h, wn)
            inst = fn(e)
            if target is not None:
                sem, k = target
                assert inst is not None, f"op with inc returned None ({tag})"
                if k == "cc":
                    inst.then_inc(sem.h)
                else:
                    inst.then_inc(sem.h, k)

        self.q[eng].append(run)

    def emit(self, blk):
        for name in ENGS:
            ops = self.q[name]
            if not ops:
                continue

            def body(e, ops=ops):
                for op in ops:
                    op(e)

            getattr(blk, name)(body)


def build_program(L):
    T_loc = L // 4
    NCH1 = T_loc // 512
    SPAN = min(2, NCH1)
    NH = NCH1 // SPAN
    NCH3 = L // 512
    CH4 = 1024
    NCH4 = L // CH4
    DCH = 512
    TPH = SPAN * 512
    NSUB = SPAN * 4

    nc = bass.Bass("TRN2", target_bir_lowering=False)

    def inp(name, shape, dt=F32):
        return nc.dram_tensor(name, list(shape), dt, kind="ExternalInput").ap()

    x_sh = inp("x_sh", (T_loc, D))
    ffn_W1 = inp("ffn_W1", (NE, 128, 8 * 1024))
    ffn_W2 = inp("ffn_W2", (NE, 128, 4 * 1024))
    ffn_Ws1a = inp("ffn_Ws1a", (128, 8 * 1024))
    ffn_Ws1b = inp("ffn_Ws1b", (128, 8 * 1024))
    ffn_Ws2 = inp("ffn_Ws2", (128, 8 * 1024))
    ffn_WgT = inp("ffn_WgT", (128, 8 * 16))
    aW1T = inp("aW1T", (128, 8 * DIN))
    aW2T = inp("aW2T", (128, 16 * D))
    b1_row = inp("b1_row", (1, DIN))
    b2_row = inp("b2_row", (1, D))
    b1ln_cs = inp("b1ln_cs", (128, 8), BF16)
    g1_cs = inp("g1_cs", (128, 8))
    b2ln_cs = inp("b2ln_cs", (128, 8), BF16)
    g2_cs = inp("g2_cs", (128, 8))
    ipT_sl = inp("ipT_sl", (128, 8 * 1024))
    opT = inp("opT", (128, 16 * 1024))
    xpT_sl = inp("xpT_sl", (128, 4 * 96))
    dtpT_sl = inp("dtpT_sl", (RK, DCH))
    dtpb_cs = inp("dtpb_cs", (128, 4))
    convw_cs = inp("convw_cs", (128, 16))
    convb_cs = inp("convb_cs", (128, 4))
    A_cs = inp("A_cs", (128, 64))
    D_cs = inp("D_cs", (128, 4))
    identf = inp("identf", (128, 128))
    ones_row = inp("ones_row", (1, 512), BF16)
    sel16 = inp("sel16", (16, 16 * 128), BF16)
    selbc = inp("selbc", (32, 32 * 128), BF16)
    gmask = inp("gmask", (1, 16))

    out_sh = nc.dram_tensor("out_sh", [T_loc, D], F32, kind="ExternalOutput").ap()

    def scr(name, shape, dt=F32):
        return nc.dram_tensor(name, list(shape), dt)

    x1_d = scr("x1_d", (128, 8 * T_loc))
    u1_d = scr("u1_d", (128, 8 * T_loc))
    x2p_d = scr("x2p_d", (128, 8 * T_loc))
    x2_d = scr("x2_d", (128, 8 * T_loc))
    u_d = scr("u_d", (DCH, L))
    sz_d = scr("sz_d", (DCH, L))
    sW1gT_d = scr("sW1gT_d", (128, 8 * DIN), BF16)
    w1bf_d = scr("w1bf_d", (128, NE * 8 * 1024), BF16)
    w2bf_d = scr("w2bf_d", (128, NE * 4 * 1024), BF16)
    ws1a_d = scr("ws1a_d", (128, 8 * 1024), BF16)
    ws1b_d = scr("ws1b_d", (128, 8 * 1024), BF16)
    ws2_d = scr("ws2_d", (128, 8 * 1024), BF16)
    opbf_d = scr("opbf_d", (128, 16 * 1024), BF16)
    sW2T_d = scr("sW2T_d", (128, 16 * D), BF16)
    ipT_d = scr("ipT_d", (128, 8 * 1024), BF16)
    gam_in = scr("gam_in", (1, 16))
    gam_out = scr("gam_out", (1, 16))
    ulnag_in = scr("ulnag_in", (128, 8 * T_loc), BF16)
    ulnag_out = scr("ulnag_out", (4 * 128, 8 * T_loc), BF16)
    xdbl_in = scr("xdbl_in", (L // 1024 * 96, 1024))
    xdbl_out = scr("xdbl_out", (L // 1024 * 96, 1024))
    ygag_in = scr("ygag_in", (DCH, L), BF16)
    ygag_out = scr("ygag_out", (DIN, L), BF16)

    G4 = [[0, 1, 2, 3], [4, 5, 6, 7]]
    G8 = [list(range(NCORES))]

    ctx = contextlib.ExitStack()
    sb = lambda name, shape, dt=F32: ctx.enter_context(nc.sbuf_tensor(name, list(shape), dt))

    s_identf = sb("s_identf", (128, 128))
    s_ones = sb("s_ones", (1, 512), BF16)
    s_sel16 = sb("s_sel16", (16, 16 * 128), BF16)
    s_b1eff = sb("s_b1eff", (1, DIN), BF16)
    s_b2row = sb("s_b2row", (1, D), BF16)
    s_WgT = sb("s_WgT", (128, 8 * 16), BF16)
    s_qscale = sb("s_qscale", (128, 1))
    s_gmaskb = sb("s_gmaskb", (128, 16))
    s_negA = sb("s_negA", (128, 64))
    s_Dcs = sb("s_Dcs", (128, 4))
    s_cbeff = sb("s_cbeff", (128, 4))
    s_ipbz = sb("s_ipbz", (128, 4))
    s_dtpb = sb("s_dtpb", (128, 4))
    s_convw = sb("s_convw", (128, 16))
    s_xpT = sb("s_xpT", (128, 4 * 96), BF16)
    s_dtpT = sb("s_dtpT", (64, 4 * 128), BF16)
    s_hinit = sb("s_hinit", (128, 64))
    s_gsc = sb("s_gsc", (1, 16))
    s_gcols = sb("s_gcols", (128, 64))

    SEM = {}
    for name in ["ld", "uld", "xld", "wld", "wls", "st", "st3", "st4", "mm", "ev", "ve", "gp", "cc1", "cc2", "cc3", "cc4"]:
        SEM[name] = DSem(nc, ctx, name)
    ld, wld, st = SEM["ld"], SEM["wld"], SEM["st"]
    st3 = SEM["st3"]
    st4 = SEM["st4"]
    uld = SEM["uld"]
    xld = SEM["xld"]
    wls = SEM["wls"]
    gp = SEM["gp"]
    mm, ev, ve = SEM["mm"], SEM["ev"], SEM["ve"]
    cc1, cc2, cc3, cc4 = SEM["cc1"], SEM["cc2"], SEM["cc3"], SEM["cc4"]

    def run_phase(builder):
        with nc.Block() as blk, contextlib.ExitStack() as pctx:
            s = Script()
            builder(s, pctx)
            s.do("sync", lambda e: e.nop(), waits=[st, st3, st4, wld, wls, uld, xld])
            s.emit(blk)
        nc.all_engine_barrier()

    def emit_ln(s, s_x, s_stats, s_scr, norm_out, wait_loaded):
        for sub in range(4):
            s.do("scalar", lambda e, sub=sub: e.activation(
                s_scr[:, 0:D], s_x[:, sub * D:(sub + 1) * D], AF.Square,
                accum_out=s_stats[:, 16 + sub:17 + sub]),
                inc=(ev, 1), waits=[wait_loaded, ev] if sub == 0 else [ev])
            s.do("scalar", lambda e, sub=sub: e.activation(
                s_scr[:, D:2 * D], s_x[:, sub * D:(sub + 1) * D], AF.Copy,
                accum_out=s_stats[:, sub:sub + 1]),
                inc=(ev, 1), waits=[ev])
        ST = s_stats
        s.do("vector", lambda e: e.tensor_scalar_mul(ST[:, 32:36], ST[:, 0:4], 1.0 / D),
             inc=(ve, 1), waits=[ev])
        s.do("vector", lambda e: e.scalar_tensor_tensor(
            ST[:, 48:52], ST[:, 32:36], -1.0, ST[:, 32:36], ALU.mult, ALU.mult),
            inc=(ve, 1), waits=[ve])
        s.do("vector", lambda e: e.scalar_tensor_tensor(
            ST[:, 64:68], ST[:, 16:20], 1.0 / D, ST[:, 48:52], ALU.mult, ALU.add),
            inc=(ve, 1), waits=[ve])
        s.do("vector", lambda e: e.tensor_scalar_add(ST[:, 80:84], ST[:, 64:68], EPS),
             inc=(ve, 1), waits=[ve])
        s.do("scalar", lambda e: e.activation(ST[:, 64:68], ST[:, 80:84], AF.Ln),
             inc=(ev, 1), waits=[ve])
        s.do("scalar", lambda e: e.activation(ST[:, 96:100], ST[:, 64:68], AF.Exp,
                                              scale=-0.5), inc=(ev, 1), waits=[ev])
        s.do("vector", lambda e: e.scalar_tensor_tensor(
            ST[:, 112:116], ST[:, 32:36], -1.0, ST[:, 96:100], ALU.mult, ALU.mult),
            inc=(ve, 1), waits=[ev])
        for sub in range(4):
            s.do("vector", lambda e, sub=sub: e.tensor_scalar(
                norm_out(sub), s_x[:, sub * D:(sub + 1) * D],
                ST[:, 96 + sub:97 + sub], ST[:, 112 + sub:113 + sub],
                ALU.mult, ALU.add),
                inc=(ve, 1), waits=[ve] if sub == 0 else ())

    def emit_tr(s, src_ap, dst_col, p4, slots, first_wait=None, evac_first_wait=None):
        for sub in range(4):
            for kt in range(8):
                g = sub * 8 + kt
                slot = g % 2
                w = []
                if first_wait is not None and g == 0:
                    w.append(first_wait)
                if slot in slots:
                    w.append((ev, slots[slot]))
                s.do("tensor", lambda e, sub=sub, kt=kt, slot=slot: e.transpose(
                    p4[:, slot * 512:slot * 512 + 128],
                    src_ap(sub, kt), s_identf[:, :]),
                    inc=(mm, 1), waits=w)
                ew = [mm]
                if evac_first_wait is not None and g == 0:
                    ew.append(evac_first_wait)
                s.do("scalar", lambda e, sub=sub, kt=kt, slot=slot: e.copy(
                    dst_col(sub, kt), p4[:, slot * 512:slot * 512 + 128]),
                    inc=(ev, 1), waits=ew)
                slots[slot] = ev.vn

    def phaseW(s, pctx):
        psb = lambda n, sh, dt=F32: pctx.enter_context(nc.sbuf_tensor(n, list(sh), dt))
        s_w1u = psb("w_w1u", (128, 8 * DIN), BF16)
        s_ipTu = psb("w_ipTu", (128, 8 * 1024), BF16)
        s_sc2T = psb("w_sc2T", (128, 16 * D), BF16)
        s_b1c = psb("w_b1c", (128, 8), BF16)
        s_b2c = psb("w_b2c", (128, 8), BF16)
        s_g1 = psb("w_g1", (128, 8))
        s_g2 = psb("w_g2", (128, 8))
        s_bacc = psb("w_bacc", (1, DIN))
        s_b1raw = psb("w_b1raw", (1, DIN))
        s_ipbrow = psb("w_ipbrow", (1, 1024))
        s_tmpc = psb("w_tmpc", (128, 16))
        s_w1g = psb("w_w1g", (128, 8 * DIN), BF16)
        p_b = pctx.enter_context(nc.psum_tensor("w_pb", [1, 512], F32))
        p_t = pctx.enter_context(nc.psum_tensor("w_pt", [128, 8], F32))

        for o, i in [
            (s_identf[:], identf[:, :]), (s_ones[:], ones_row[:, :]),
            (s_sel16[:], sel16[:, :]),
            (s_g1[:], g1_cs[:, :]), (s_g2[:], g2_cs[:, :]),
            (s_negA[:], A_cs[:, :]), (s_Dcs[:], D_cs[:, :]),
            (s_dtpb[:], dtpb_cs[:, :]), (s_convw[:], convw_cs[:, :]),
            (s_cbeff[:], convb_cs[:, :]), (s_b1raw[:], b1_row[:, :]),
            (s_gmaskb[:], bass.AP(gmask.tensor, 0, [[0, 128], [1, 16]])),
            (s_b1c[:], b1ln_cs[:, :]), (s_b2c[:], b2ln_cs[:, :]),
        ]:
            s.do("sync", lambda e, o=o, i=i: e.dma_start(out=o, in_=i), inc=(ld, 16))
        for o, i in [
            (s_w1u[:], aW1T[:, :]),
            (s_sc2T[:], aW2T[:, :]),
            (s_ipTu[:], ipT_sl[:, :]),
            (s_WgT[:], ffn_WgT[:, :]),
            (s_xpT[:], xpT_sl[:, :]),
            (s_dtpT[:], dtpT_sl[:, :]),
            (s_b2row[:], b2_row[:, :]),
        ]:
            s.do("gpsimd", lambda e, o=o, i=i: e.dma_start(out=o, in_=i), inc=(wld, 16))
        for o, i in [
            (w1bf_d.ap().rearrange("p (e m) -> p e m", e=NE),
             ffn_W1.rearrange("e p m -> p e m")),
            (w2bf_d.ap().rearrange("p (e m) -> p e m", e=NE),
             ffn_W2.rearrange("e p m -> p e m")),
            (ws1a_d.ap(), ffn_Ws1a[:, :]),
            (ws1b_d.ap(), ffn_Ws1b[:, :]),
            (ws2_d.ap(), ffn_Ws2[:, :]),
            (opbf_d.ap(), opT[:, :]),
        ]:
            s.do("gpsimd", lambda e, o=o, i=i: e.dma_start(out=o, in_=i), inc=(wld, 16))

        s.do("scalar", lambda e: e.activation(s_w1u[:], s_w1u[:], AF.Sign),
             inc=(ev, 1), waits=[ld, wld])
        s.do("scalar", lambda e: e.activation(s_sc2T[:], s_sc2T[:], AF.Sign),
             inc=(ev, 1))
        s.do("scalar", lambda e: e.activation(s_negA[:], s_negA[:], AF.Exp),
             inc=(ev, 1))
        s.do("vector", lambda e: e.tensor_scalar_mul(s_negA[:], s_negA[:], -1.0),
             inc=(ve, 1), waits=[ev])
        for c in range(4):
            if c > 0:
                s.do("tensor", lambda e: e.nop(), waits=[ev])
            for k in range(8):
                s.do("tensor", lambda e, c=c, k=k: e.matmul(
                    p_b[:, :], s_b1c[:, k:k + 1],
                    s_w1u[:, k * DIN + c * 512:k * DIN + (c + 1) * 512],
                    start=(k == 0), stop=(k == 7)),
                    inc=(mm, 1) if k == 7 else None,
                    waits=[ev, ve] if (c == 0 and k == 0) else ())
            s.do("scalar", lambda e, c=c: e.copy(
                s_bacc[0:1, c * 512:(c + 1) * 512], p_b[0:1, :]), inc=(ev, 1), waits=[mm])
        for c in range(2):
            s.do("tensor", lambda e: e.nop(), waits=[ev])
            for k in range(8):
                s.do("tensor", lambda e, c=c, k=k: e.matmul(
                    p_b[:, :], s_b2c[:, k:k + 1],
                    s_ipTu[:, k * 1024 + c * 512:k * 1024 + (c + 1) * 512],
                    start=(k == 0), stop=(k == 7)),
                    inc=(mm, 1) if k == 7 else None)
            s.do("scalar", lambda e, c=c: e.copy(
                s_ipbrow[0:1, c * 512:(c + 1) * 512], p_b[0:1, :]), inc=(ev, 1), waits=[mm])
        s.do("vector", lambda e: e.tensor_add(s_bacc[0:1, :], s_bacc[0:1, :],
                                              s_b1raw[0:1, :]), inc=(ve, 1), waits=[ev])
        s.do("vector", lambda e: e.tensor_copy(s_b1eff[0:1, :], s_bacc[0:1, :]),
             inc=(ve, 1), waits=[ve])
        for k in range(8):
            s.do("scalar", lambda e, k=k: e.mul(
                s_w1g[:, k * DIN:(k + 1) * DIN], s_w1u[:, k * DIN:(k + 1) * DIN],
                s_g1[:, k:k + 1]), inc=(ev, 1), waits=[ev] if k == 0 else ())
        for k in range(8):
            s.do("scalar", lambda e, k=k: e.mul(
                s_ipTu[:, k * 1024:(k + 1) * 1024], s_ipTu[:, k * 1024:(k + 1) * 1024],
                s_g2[:, k:k + 1]), inc=(ev, 1))
        s.do("sync", lambda e: e.dma_start(out=sW1gT_d[:, :], in_=s_w1g[:]),
             inc=(st, 16), waits=[ev])
        s.do("sync", lambda e: e.dma_start(out=sW2T_d[:, :], in_=s_sc2T[:]),
             inc=(st, 16))
        s.do("sync", lambda e: e.dma_start(out=ipT_d[:, :], in_=s_ipTu[:]),
             inc=(st, 16))
        for h8 in range(8):
            s.do("tensor", lambda e, h=h8: e.transpose(
                p_t[:, h:h + 1], s_ipbrow[0:1, h * 128:(h + 1) * 128],
                s_identf[0:1, 0:1]),
                inc=(mm, 1) if h8 == 7 else None, waits=[ev] if h8 == 0 else ())
        s.do("vector", lambda e: e.tensor_add(s_tmpc[:, 0:4], s_convw[:, 0:16:4],
                                              s_convw[:, 1:16:4]), inc=(ve, 1), waits=[mm])
        s.do("vector", lambda e: e.tensor_add(s_tmpc[:, 4:8], s_convw[:, 2:16:4],
                                              s_convw[:, 3:16:4]), inc=(ve, 1))
        s.do("vector", lambda e: e.tensor_add(s_tmpc[:, 8:12], s_tmpc[:, 0:4],
                                              s_tmpc[:, 4:8]), inc=(ve, 1), waits=[ve])
        s.do("vector", lambda e: e.tensor_mul(s_tmpc[:, 12:16], p_t[:, 0:4],
                                              s_tmpc[:, 8:12]), inc=(ve, 1), waits=[ve])
        s.do("vector", lambda e: e.tensor_add(s_cbeff[:, 0:4], s_cbeff[:, 0:4],
                                              s_tmpc[:, 12:16]), inc=(ve, 1), waits=[ve])
        s.do("vector", lambda e: e.tensor_copy(s_ipbz[:, 0:4], p_t[:, 4:8]), inc=(ve, 1))

    def make_ffn(in_dram, out_dram, tag, in_ext=False, out_ext=False):
        def ld_ap(r0):
            if in_ext:
                return in_dram[r0:r0 + 512, :].rearrange("(a p) m -> p a m", p=128)
            return in_dram[:, r0 * 8:r0 * 8 + 4096]

        def st_ap(r0):
            if out_ext:
                return out_dram[r0:r0 + 512, :].rearrange("(a p) m -> p a m", p=128)
            return out_dram[:, r0 * 8:r0 * 8 + 4096]
        def phase(s, pctx):
            psb = lambda n, sh, dt=F32: pctx.enter_context(nc.sbuf_tensor(tag + n, list(sh), dt))
            pps = lambda n, sh: pctx.enter_context(nc.psum_tensor(tag + n, list(sh), F32))
            s_x = psb("f_x", (128, 4 * D))
            s_scr = psb("f_scr", (128, 2 * D))
            s_stats = psb("f_stats", (128, 128))
            s_xnT = psb("f_xnT", (128, 8 * TPH), BF16)
            s_sc = psb("f_sc", (128, NSUB * 16))
            s_es = psb("f_es", (128, NSUB * 16))
            s_msk = psb("f_msk", (128, NSUB * 16))
            s_cw = psb("f_cw", (128, NSUB * 16))
            s_cwT = psb("f_cwT", (16, TPH), BF16)
            s_w1e = [psb(f"f_w1e{i}", (128, 8 * 1024), BF16) for i in range(2)]
            s_w2e = [psb("f_w2e0", (128, 8 * 1024), BF16),
                     psb("f_w2e1", (128, 4 * 1024), BF16)]
            s_hu = psb("f_hu", (128, 2 * 8 * 512), BF16)
            s_sg = psb("f_sg", (128, 2 * 8 * 512), BF16)
            s_hgT = psb("f_hgT", (128, 2 * 8 * 512), BF16)
            s_cwb = psb("f_cwb", (128, 2 * 512), BF16)
            s_zacc = psb("f_zacc", (128, NSUB * 1024))
            p_up = [pps(f"f_pup{i}", (128, 512)) for i in range(3)]
            p_cw = pps("f_pcw", (128, 512))
            p_z = [pps(f"f_pz{i}", (128, 1024)) for i in range(2)]

            up_slot_ev = {}
            z_slot = {}
            cw_ev = [0]
            trx = {}
            trc = {}
            upg = [0]

            def up_pass(ci, wbuf, dst, first_waits):
                for mt in range(8):
                    slot = upg[0] % 3
                    upg[0] += 1
                    w = list(first_waits) if mt == 0 else []
                    if slot in up_slot_ev:
                        w.append((ev, up_slot_ev[slot]))
                    for kt in range(8):
                        s.do("tensor", lambda e, ci=ci, mt=mt, kt=kt, slot=slot, wbuf=wbuf: e.matmul(
                            p_up[slot][:, :],
                            wbuf[:, kt * 1024 + mt * 128:kt * 1024 + (mt + 1) * 128],
                            s_xnT[:, kt * TPH + ci * 512:kt * TPH + (ci + 1) * 512],
                            start=(kt == 0), stop=(kt == 7)),
                            inc=(mm, 1) if kt == 7 else None, waits=w)
                    kind, ap = dst(mt)
                    if kind == "copy":
                        s.do("scalar", lambda e, ap=ap, slot=slot: e.copy(
                            ap, p_up[slot][:, :]), inc=(ev, 1), waits=[mm])
                    else:
                        s.do("scalar", lambda e, ap=ap, slot=slot: e.activation(
                            ap, p_up[slot][:, :], AF.Silu), inc=(ev, 1), waits=[mm])
                    up_slot_ev[slot] = ev.vn

            def down_pass(ci, cpar, w2buf, nkt, zcopy):
                for sub in range(4):
                    slot = sub % 2
                    w = [(ve, ve.vn)] if sub == 0 else []
                    if slot in z_slot:
                        w.append(z_slot[slot])
                    first = True
                    for nh in range(2):
                        for kt in range(nkt):
                            s.do("tensor", lambda e, sub=sub, nh=nh, kt=kt, slot=slot, w2buf=w2buf, cpar=cpar: e.matmul(
                                p_z[slot][:, nh * 512:(nh + 1) * 512],
                                s_hgT[:, cpar * 4096 + kt * 512 + sub * 128:
                                      cpar * 4096 + kt * 512 + (sub + 1) * 128],
                                w2buf[:, kt * 1024 + nh * 512:kt * 1024 + (nh + 1) * 512],
                                start=(kt == 0), stop=(kt == nkt - 1)),
                                inc=(mm, 1) if (nh == 1 and kt == nkt - 1) else None,
                                waits=w if first else ())
                            first = False
                    zc = (ci * 4 + sub) * 1024
                    if zcopy:
                        s.do("vector", lambda e, zc=zc, slot=slot: e.tensor_copy(
                            s_zacc[:, zc:zc + 1024], p_z[slot][:, :]),
                            inc=(ve, 1), waits=[mm, st])
                    else:
                        s.do("vector", lambda e, zc=zc, slot=slot: e.tensor_add(
                            s_zacc[:, zc:zc + 1024], s_zacc[:, zc:zc + 1024],
                            p_z[slot][:, :]),
                            inc=(ve, 1), waits=[mm])
                    z_slot[slot] = (ve, ve.vn)

            def gate_pass(ci, cpar, n, with_cw):
                for mt in range(n):
                    w = [ev, mm] if mt == 0 else []
                    s.do("vector", lambda e, mt=mt, cpar=cpar: e.tensor_mul(
                        s_hgT[:, cpar * 4096 + mt * 512:cpar * 4096 + (mt + 1) * 512],
                        s_hu[:, cpar * 4096 + mt * 512:cpar * 4096 + (mt + 1) * 512],
                        s_sg[:, cpar * 4096 + mt * 512:cpar * 4096 + (mt + 1) * 512]),
                        inc=(ve, 1), waits=w)
                if with_cw:
                    for mt in range(n):
                        s.do("vector", lambda e, mt=mt, cpar=cpar: e.tensor_mul(
                            s_hgT[:, cpar * 4096 + mt * 512:cpar * 4096 + (mt + 1) * 512],
                            s_hgT[:, cpar * 4096 + mt * 512:cpar * 4096 + (mt + 1) * 512],
                            s_cwb[:, cpar * 512:(cpar + 1) * 512]),
                            inc=(ve, 1), waits=[ve] if mt == 0 else ())

            def cwrep(ci, cpar, ei):
                s.do("tensor", lambda e, ci=ci, ei=ei: e.matmul(
                    p_cw[:, :], s_sel16[:, ei * 128:(ei + 1) * 128],
                    s_cwT[0:16, ci * 512:(ci + 1) * 512],
                    start=True, stop=True),
                    inc=(mm, 1), waits=[(ev, cw_ev[0])])
                s.do("scalar", lambda e, cpar=cpar: e.copy(
                    s_cwb[:, cpar * 512:(cpar + 1) * 512], p_cw[:, :]),
                    inc=(ev, 1), waits=[mm, (ve, ve.vn)])
                cw_ev[0] = ev.vn

            dbg = FFN_DBG or {}
            nhv = dbg.get("nh", NH)
            nexp = dbg.get("ne", NE)
            do_shared = dbg.get("sh", True)
            do_epi = dbg.get("epi", True)
            do_rout = dbg.get("rout", True)
            do_exp = dbg.get("exp", True)
            hv_prev_mm = 0
            for hv in range(nhv):
                mmafter = {}
                tok0 = hv * TPH
                for ci in range(SPAN):
                    r0 = tok0 + ci * 512
                    s.do("sync", lambda e, r0=r0: e.dma_start(
                        out=(s_x[:].rearrange("p (a m) -> p a m", a=4) if in_ext
                             else s_x[:]),
                        in_=ld_ap(r0)),
                        inc=(ld, 16), waits=[ve, st, (mm, mm.vn)])
                    if dbg.get("ln", True):
                        emit_ln(s, s_x, s_stats, s_scr,
                                lambda sub: s_x[:, sub * D:(sub + 1) * D], ld)
                    if dbg.get("tr", True):
                        emit_tr(s, lambda sub, kt: s_x[:, sub * D + kt * 128:sub * D + (kt + 1) * 128],
                                lambda sub, kt, ci=ci: s_xnT[
                                    :, kt * TPH + ci * 512 + sub * 128:
                                    kt * TPH + ci * 512 + (sub + 1) * 128],
                                p_z[0], trx, first_wait=(ve, ve.vn),
                                evac_first_wait=(mm, mm.vn))
                if not do_rout:
                    continue
                for g in range(NSUB):
                    ci, sub = g // 4, g % 4
                    for kt in range(8):
                        w = [ev, (ev, cw_ev[0])] if (g == 0 and kt == 0) else []
                        s.do("tensor", lambda e, ci=ci, sub=sub, g=g, kt=kt: e.matmul(
                            p_cw[:, g * 16:(g + 1) * 16],
                            s_xnT[:, kt * TPH + ci * 512 + sub * 128:
                                  kt * TPH + ci * 512 + (sub + 1) * 128],
                            s_WgT[:, kt * 16:(kt + 1) * 16],
                            start=(kt == 0), stop=(kt == 7)),
                            inc=(mm, 1) if kt == 7 else None, waits=w)
                NCOL = NSUB * 16
                s.do("scalar", lambda e: e.copy(s_sc[:, 0:NCOL], p_cw[:, 0:NCOL]),
                     inc=(ev, 1), waits=[mm])
                cw_ev[0] = ev.vn
                for g in range(NSUB):
                    s.do("vector", lambda e, g=g: e.reduce_max(
                        s_stats[:, g:g + 1], s_sc[:, g * 16:(g + 1) * 16], axis=AX.X),
                        inc=(ve, 1), waits=[ev] if g == 0 else ())
                s.do("vector", lambda e: e.tensor_scalar_mul(
                    s_stats[:, 16:16 + NSUB], s_stats[:, 0:NSUB], -1.0),
                    inc=(ve, 1), waits=[ve])
                for g in range(NSUB):
                    s.do("scalar", lambda e, g=g: e.activation(
                        s_es[:, g * 16:(g + 1) * 16], s_sc[:, g * 16:(g + 1) * 16],
                        AF.Exp, bias=s_stats[:, 16 + g:17 + g],
                        accum_out=s_stats[:, 32 + g:33 + g]),
                        inc=(ev, 1), waits=[ve] if g == 0 else ())
                s.do("vector", lambda e: e.reciprocal(
                    s_stats[:, 48:48 + NSUB], s_stats[:, 32:32 + NSUB]),
                    inc=(ve, 1), waits=[ev])
                for g in range(NSUB):
                    s.do("vector", lambda e, g=g: e.tensor_single_scalar(
                        s_msk[:, g * 16:(g + 1) * 16], s_sc[:, g * 16:(g + 1) * 16],
                        s_stats[:, g:g + 1], ALU.is_ge),
                        inc=(ve, 1), waits=[ve] if g == 0 else ())
                s.do("vector", lambda e: e.scalar_tensor_tensor(
                    s_cw[:, 0:NCOL], s_msk[:, 0:NCOL], -1e30, s_sc[:, 0:NCOL],
                    ALU.mult, ALU.add), inc=(ve, 1), waits=[ve])
                for g in range(NSUB):
                    s.do("vector", lambda e, g=g: e.reduce_max(
                        s_stats[:, 64 + g:65 + g], s_cw[:, g * 16:(g + 1) * 16], axis=AX.X),
                        inc=(ve, 1), waits=[ve] if g == 0 else ())
                for g in range(NSUB):
                    s.do("vector", lambda e, g=g: e.tensor_single_scalar(
                        s_cw[:, g * 16:(g + 1) * 16], s_sc[:, g * 16:(g + 1) * 16],
                        s_stats[:, 64 + g:65 + g], ALU.is_ge),
                        inc=(ve, 1), waits=[ve] if g == 0 else ())
                s.do("vector", lambda e: e.tensor_max(
                    s_msk[:, 0:NCOL], s_msk[:, 0:NCOL], s_cw[:, 0:NCOL]),
                    inc=(ve, 1), waits=[ve])
                s.do("vector", lambda e: e.tensor_mul(
                    s_cw[:, 0:NCOL], s_es[:, 0:NCOL], s_msk[:, 0:NCOL]),
                    inc=(ve, 1), waits=[ve])
                for g in range(NSUB):
                    s.do("vector", lambda e, g=g: e.tensor_single_scalar(
                        s_cw[:, g * 16:(g + 1) * 16], s_cw[:, g * 16:(g + 1) * 16],
                        s_stats[:, 48 + g:49 + g], ALU.mult),
                        inc=(ve, 1), waits=[ve] if g == 0 else ())
                for g in range(NSUB):
                    slot = g % 2
                    w = [ve] if g == 0 else []
                    if slot in trc:
                        w.append((ev, trc[slot]))
                    s.do("tensor", lambda e, g=g, slot=slot: e.transpose(
                        p_z[1][0:16, slot * 512:slot * 512 + 128],
                        s_cw[:, g * 16:(g + 1) * 16], s_identf[:, :]),
                        inc=(mm, 1), waits=w)
                    s.do("scalar", lambda e, g=g, slot=slot: e.copy(
                        s_cwT[0:16, g * 128:(g + 1) * 128],
                        p_z[1][0:16, slot * 512:slot * 512 + 128]),
                        inc=(ev, 1), waits=[mm])
                    trc[slot] = ev.vn
                cw_ev[0] = ev.vn
                z_slot[0] = (ev, ev.vn)
                z_slot[1] = (ev, ev.vn)

                if not do_exp:
                    continue
                for ei in range(nexp):
                    par = ei % 2
                    bfree = (mm, mmafter.get(ei - 2, hv_prev_mm))
                    s.do("scalar", lambda e, ei=ei, par=par: e.dma_start(
                        out=s_w1e[par][:],
                        in_=w1bf_d[:, ei * 8192:(ei + 1) * 8192]),
                        inc=(wls, 16), waits=[bfree])
                    s.do("scalar", lambda e, ei=ei, par=par: e.dma_start(
                        out=s_w2e[par][:, 0:4 * 1024],
                        in_=w2bf_d[:, ei * 4096:(ei + 1) * 4096]),
                        inc=(wls, 16))
                    wneed = (wls, wls.vn)
                    for ci in range(SPAN):
                        cpar = ci % 2
                        dstx = (lambda mt, cpar=cpar:
                                ("copy", s_hu[:, cpar * 4096 + mt * 512:cpar * 4096 + (mt + 1) * 512])
                                if mt < 4 else
                                ("silu", s_sg[:, cpar * 4096 + (mt - 4) * 512:cpar * 4096 + (mt - 3) * 512]))
                        up_pass(ci, s_w1e[par], dstx, [wneed] if ci == 0 else [])
                        cwrep(ci, cpar, ei)
                        gate_pass(ci, cpar, 4, True)
                        if ci >= 1:
                            down_pass(ci - 1, (ci - 1) % 2, s_w2e[par], 4, (ei == 0))
                    down_pass(SPAN - 1, (SPAN - 1) % 2, s_w2e[par], 4, (ei == 0))
                    mmafter[ei] = mm.vn
                if not do_shared:
                    continue
                s.do("scalar", lambda e: e.dma_start(
                    out=s_w1e[0][:], in_=ws1a_d[:, :]),
                    inc=(wls, 16), waits=[(mm, mmafter.get(nexp - 2, hv_prev_mm))])
                s.do("scalar", lambda e: e.dma_start(
                    out=s_w2e[0][:], in_=ws1b_d[:, :]),
                    inc=(wls, 16))
                s.do("scalar", lambda e: e.dma_start(
                    out=s_w1e[1][:], in_=ws2_d[:, :]),
                    inc=(wls, 16), waits=[(mm, mmafter.get(nexp - 1, hv_prev_mm))])
                wneed = (wls, wls.vn)
                for ci in range(SPAN):
                    cpar = ci % 2
                    up_pass(ci, s_w1e[0],
                            lambda mt, cpar=cpar: ("copy", s_hu[:, cpar * 4096 + mt * 512:cpar * 4096 + (mt + 1) * 512]),
                            [wneed] if ci == 0 else [])
                    up_pass(ci, s_w2e[0],
                            lambda mt, cpar=cpar: ("silu", s_sg[:, cpar * 4096 + mt * 512:cpar * 4096 + (mt + 1) * 512]),
                            [])
                    gate_pass(ci, cpar, 8, False)
                    if ci >= 1:
                        down_pass(ci - 1, (ci - 1) % 2, s_w1e[1], 8, False)
                down_pass(SPAN - 1, (SPAN - 1) % 2, s_w1e[1], 8, False)
                hv_prev_mm = mm.vn
                if not do_epi:
                    continue
                for ci in range(SPAN):
                    r0 = tok0 + ci * 512
                    s.do("sync", lambda e, r0=r0: e.dma_start(
                        out=(s_x[:].rearrange("p (a m) -> p a m", a=4) if in_ext
                             else s_x[:]),
                        in_=ld_ap(r0)),
                        inc=(ld, 16), waits=[ve, st])
                    for sub in range(4):
                        zc = (ci * 4 + sub) * 1024
                        s.do("vector", lambda e, sub=sub, zc=zc: e.scalar_tensor_tensor(
                            s_x[:, sub * D:(sub + 1) * D], s_zacc[:, zc:zc + 1024],
                            0.5, s_x[:, sub * D:(sub + 1) * D], ALU.mult, ALU.add),
                            inc=(ve, 1), waits=[ld, ve] if sub == 0 else ())
                    s.do("sync", lambda e, r0=r0: e.dma_start(
                        out=st_ap(r0),
                        in_=(s_x[:].rearrange("p (a m) -> p a m", a=4) if out_ext
                             else s_x[:])),
                        inc=(st, 16), waits=[ve])
        return phase

    def phaseMLP(s, pctx):
        psb = lambda n, sh, dt=F32: pctx.enter_context(nc.sbuf_tensor(n, list(sh), dt))
        pps = lambda n, sh: pctx.enter_context(nc.psum_tensor(n, list(sh), F32))
        s_x = psb("m_x", (128, 4 * D))
        s_scr = psb("m_scr", (128, 2 * D))
        s_stats = psb("m_stats", (128, 128))
        s_ain = psb("m_ain", (128, NSUB * D))
        s_ainT = psb("m_ainT", (128, 8 * TPH), BF16)
        s_w1 = psb("m_w1", (128, 8 * DIN), BF16)
        s_w2 = psb("m_w2", (128, 16 * D), BF16)
        s_gatT = psb("m_gatT", (128, 16 * 512), BF16)
        p_up = [pps(f"m_pup{i}", (128, 512)) for i in range(3)]
        p_z = [pps(f"m_pz{i}", (128, 1024)) for i in range(2)]
        trx = {}
        up_slot_ve = {}
        z_slot = {}
        upg = [0]

        s.do("gpsimd", lambda e: e.dma_start(out=s_w1[:], in_=sW1gT_d[:, :]),
             inc=(wld, 16))
        s.do("gpsimd", lambda e: e.dma_start(out=s_w2[:], in_=sW2T_d[:, :]),
             inc=(wld, 16))
        wneed = (wld, wld.vn)
        gidx = [0]
        for hv in range(NH):
            tok0 = hv * TPH
            for ci in range(SPAN):
                r0 = tok0 + ci * 512
                s.do("sync", lambda e, r0=r0: e.dma_start(
                    out=s_x[:], in_=x1_d[:, r0 * 8:r0 * 8 + 4096]),
                    inc=(ld, 16), waits=[ve, st, (mm, mm.vn)])
                emit_ln(s, s_x, s_stats, s_scr,
                        lambda sub, ci=ci: s_ain[:, (ci * 4 + sub) * D:(ci * 4 + sub + 1) * D], ld)
                emit_tr(s, lambda sub, kt, ci=ci: s_ain[
                            :, (ci * 4 + sub) * D + kt * 128:(ci * 4 + sub) * D + (kt + 1) * 128],
                        lambda sub, kt, ci=ci: s_ainT[
                            :, kt * TPH + ci * 512 + sub * 128:
                            kt * TPH + ci * 512 + (sub + 1) * 128],
                        p_z[0], trx, first_wait=(ve, ve.vn),
                        evac_first_wait=(mm, mm.vn))
            z_slot[0] = (ev, ev.vn)
            z_slot[1] = (ev, ev.vn)
            for ci in range(SPAN):
                r0 = tok0 + ci * 512
                for mt in range(16):
                    slot = upg[0] % 3
                    upg[0] += 1
                    w = [ev] if (ci == 0 and mt == 0) else []
                    if hv == 0 and ci == 0 and mt == 0:
                        w.append(wneed)
                    if slot in up_slot_ve:
                        w.append((ve, up_slot_ve[slot]))
                    for kt in range(8):
                        s.do("tensor", lambda e, ci=ci, mt=mt, kt=kt, slot=slot: e.matmul(
                            p_up[slot][:, :],
                            s_w1[:, kt * DIN + mt * 128:kt * DIN + (mt + 1) * 128],
                            s_ainT[:, kt * TPH + ci * 512:kt * TPH + (ci + 1) * 512],
                            start=(kt == 0), stop=False),
                            inc=None, waits=w)
                    s.do("tensor", lambda e, mt=mt, slot=slot: e.matmul(
                        p_up[slot][:, :], s_b1eff[0:1, mt * 128:(mt + 1) * 128],
                        s_ones[0:1, 0:512], start=False, stop=True),
                        inc=(mm, 1))
                    s.do("vector", lambda e, mt=mt, slot=slot: e.tensor_scalar(
                        s_gatT[:, mt * 512:(mt + 1) * 512], p_up[slot][:, :],
                        -1.0, 1.0, ALU.max, ALU.min),
                        inc=(ve, 1), waits=[mm])
                    up_slot_ve[slot] = ve.vn
                for sub in range(4):
                    slot = sub % 2
                    w = [ve]
                    if slot in z_slot:
                        w.append(z_slot[slot])
                    first = True
                    for nh in range(2):
                        for kt in range(16):
                            s.do("tensor", lambda e, sub=sub, nh=nh, kt=kt, slot=slot: e.matmul(
                                p_z[slot][:, nh * 512:(nh + 1) * 512],
                                s_gatT[:, kt * 512 + sub * 128:kt * 512 + (sub + 1) * 128],
                                s_w2[:, kt * 1024 + nh * 512:kt * 1024 + (nh + 1) * 512],
                                start=(kt == 0), stop=False),
                                inc=None, waits=w if first else ())
                            first = False
                        s.do("tensor", lambda e, nh=nh, slot=slot: e.matmul(
                            p_z[slot][:, nh * 512:(nh + 1) * 512],
                            s_ones[0:1, 0:128],
                            s_b2row[0:1, nh * 512:(nh + 1) * 512],
                            start=False, stop=True),
                            inc=(mm, 1) if nh == 1 else None)
                    col = (ci * 4 + sub)
                    s.do("vector", lambda e, sub=sub, col=col, slot=slot: e.tensor_add(
                        s_x[:, sub * D:(sub + 1) * D], p_z[slot][:, :],
                        s_ain[:, col * D:(col + 1) * D]),
                        inc=(ve, 1),
                        waits=[mm, (st, st.vn)] if sub == 0 else [mm])
                    gc = gidx[0]
                    gidx[0] += 1
                    s.do("vector", lambda e, sub=sub, gc=gc: e.reduce_max(
                        s_gcols[:, gc:gc + 1], s_x[:, sub * D:(sub + 1) * D],
                        axis=AX.X, apply_absolute_value=True),
                        inc=(ve, 1), waits=[ve])
                    z_slot[slot] = (ve, ve.vn)
                s.do("sync", lambda e, r0=r0: e.dma_start(
                    out=u1_d[:, r0 * 8:r0 * 8 + 4096], in_=s_x[:]),
                    inc=(st, 16), waits=[ve])
        NG = gidx[0]
        s.do("vector", lambda e: e.memset(s_scr[:, 0:128], 0.0), inc=(ve, 1))
        s.do("vector", lambda e, NG=NG: e.reduce_max(
            s_scr[:, 0:1], s_gcols[:, 0:NG], axis=AX.X), inc=(ve, 1), waits=[ve])
        s.do("tensor", lambda e: e.transpose(
            p_up[0][:, 0:128], s_scr[:, 0:128], s_identf[:, :]),
            inc=(mm, 1), waits=[ve])
        s.do("vector", lambda e: e.reduce_max(
            s_stats[0:1, 0:1], p_up[0][0:1, 0:128], axis=AX.X),
            inc=(ve, 1), waits=[mm])
        s.do("vector", lambda e: e.tensor_single_scalar(
            s_gsc[0:1, 0:16], s_gmaskb[0:1, 0:16], s_stats[0:1, 0:1], ALU.mult),
            inc=(ve, 1), waits=[ve])
        s.do("sync", lambda e: e.dma_start(out=gam_in[:, :], in_=s_gsc[0:1, :]),
             inc=(st, 16), waits=[ve])

    def phaseC(kind, op, groups, src, dst, sem):
        def phase(s, pctx):
            s.do("gpsimd", lambda e: e.collective_compute(
                kind, op, replica_groups=groups,
                ins=[src.ap().opt()], outs=[dst.ap().opt()]),
                inc=(sem, "cc"))
            s.do("gpsimd", lambda e: e.nop(), waits=[sem])
        return phase

    def phaseP2(s, pctx):
        psb = lambda n, sh, dt=F32: pctx.enter_context(nc.sbuf_tensor(n, list(sh), dt))
        s_u1 = psb("q_u1", (128, 4 * D))
        s_x1 = psb("q_x1", (128, 4 * D))
        s_scr = psb("q_scr", (128, 2 * D))
        s_stats = psb("q_stats", (128, 128))
        s_gall = psb("q_gall", (128, 16))
        s_ulnT = psb("q_ulnT", (128, 8 * 512), BF16)
        p_t = pctx.enter_context(nc.psum_tensor("q_pt", [128, 1024], F32))
        trx = {}

        s.do("sync", lambda e: e.dma_start(
            out=s_gall[:], in_=bass.AP(gam_out, 0, [[0, 128], [1, 16]])),
            inc=(ld, 16))
        s.do("vector", lambda e: e.tensor_mul(s_gall[:], s_gall[:], s_gmaskb[:]),
             inc=(ve, 1), waits=[ld])
        s.do("vector", lambda e: e.reduce_sum(s_qscale[:], s_gall[:], axis=AX.X),
             inc=(ve, 1), waits=[ve])
        s.do("vector", lambda e: e.tensor_scalar_add(s_qscale[:], s_qscale[:], EPS),
             inc=(ve, 1), waits=[ve])
        s.do("vector", lambda e: e.reciprocal(s_qscale[:], s_qscale[:]),
             inc=(ve, 1), waits=[ve])
        s.do("vector", lambda e: e.tensor_scalar_mul(s_qscale[:], s_qscale[:], QB),
             inc=(ve, 1), waits=[ve])
        for ci in range(NCH1):
            r0 = ci * 512
            s.do("sync", lambda e, r0=r0: e.dma_start(
                out=s_u1[:], in_=u1_d[:, r0 * 8:r0 * 8 + 4096]),
                inc=(ld, 16), waits=[ve, st, (mm, mm.vn)])
            s.do("sync", lambda e, r0=r0: e.dma_start(
                out=s_x1[:], in_=x1_d[:, r0 * 8:r0 * 8 + 4096]),
                inc=(ld, 16))
            s.do("vector", lambda e: e.tensor_scalar(
                s_u1[:], s_u1[:], s_qscale[:, 0:1], -(QB - EPS), ALU.mult, ALU.max),
                inc=(ve, 1), waits=[ld, ve])
            s.do("vector", lambda e: e.tensor_scalar_min(s_u1[:], s_u1[:], QB - EPS),
                 inc=(ve, 1), waits=[ve])
            s.do("vector", lambda e: e.tensor_add(s_x1[:], s_x1[:], s_u1[:]),
                 inc=(ve, 1), waits=[ve])
            s.do("sync", lambda e, r0=r0: e.dma_start(
                out=x2p_d[:, r0 * 8:r0 * 8 + 4096], in_=s_x1[:]),
                inc=(st, 16), waits=[ve])
            emit_ln(s, s_u1, s_stats, s_scr,
                    lambda sub: s_u1[:, sub * D:(sub + 1) * D], ve)
            emit_tr(s, lambda sub, kt: s_u1[:, sub * D + kt * 128:sub * D + (kt + 1) * 128],
                    lambda sub, kt: s_ulnT[:, kt * 512 + sub * 128:kt * 512 + (sub + 1) * 128],
                    p_t, trx, first_wait=(ve, ve.vn),
                    evac_first_wait=(st, st.vn))
            s.do("sync", lambda e, r0=r0: e.dma_start(
                out=ulnag_in[:, r0 * 8:r0 * 8 + 4096], in_=s_ulnT[:]),
                inc=(st, 16), waits=[ev])

    cc3s = [DSem(nc, ctx, f"cc3_{i}") for i in range(NCH4)]

    def phaseP34(s, pctx):
        psb = lambda n, sh, dt=F32: pctx.enter_context(nc.sbuf_tensor(n, list(sh), dt))
        pps = lambda n, sh: pctx.enter_context(nc.psum_tensor(n, list(sh), F32))
        s_ipw = psb("p3_ipw", (128, 8 * 1024), BF16)
        s_uln = [psb("p3_uln0", (128, 8 * 512), BF16)]
        s_rng = [psb(f"p3_rng{i}", (128, 4 * 516), BF16) for i in range(2)]
        s_ub = psb("p3_ub", (128, 512), BF16)
        s_sz2 = [psb(f"p3_sz{i}", (128, 512)) for i in range(2)]
        s_xda = psb("p3_xda", (96, CH4))
        p_up = [pps(f"p3_pup{i}", (128, 512)) for i in range(3)]
        p_xp = pps("p3_pxp", (96, 512))
        s_u4 = [psb(f"p4_u{i}", (128, 4 * CH4)) for i in range(2)]
        s_dsz = psb("p4_dsz", (128, 4 * CH4))
        s_y4 = psb("p4_y4", (128, 4 * CH4))
        s_a4 = psb("p4_a4", (128, 4 * CH4))
        s_b4 = psb("p4_b4", (128, 4 * CH4))
        s_h4 = [psb(f"p4_h{i}", (128, CH4)) for i in range(4)]
        s_ym = [psb(f"p4_ym{i}", (128, CH4)) for i in range(2)]
        s_ygb = psb("p4_ygb", (128, 4 * CH4), BF16)
        s_xd4 = [psb(f"p4_xd{i}", (96, CH4)) for i in range(2)]
        s_ab = psb("p4_ab", (128, 512))
        s_dtin = psb("p4_dtin", (64, CH4), BF16)
        s_selb = psb("p4_selb", (32, 32 * 128), BF16)
        s_bcb = psb("p4_bcb", (32, CH4), BF16)
        p_br = pps("p4_pbr", (128, CH4))
        p_cr = pps("p4_pcr", (128, CH4))
        NQ = CH4 // 512
        NB = T_loc // 512

        up_slot_ev = {}
        upg = [0]
        xp_ev = [0]
        xp_mm = [0]
        uln_freed = [0]
        p3a_ev = {}
        ring_freed = {}
        ring_init = [0]
        tail_ev = {}
        sz_st3 = {}
        hinit_done = [0]
        u_ready = {}
        u_freed = {}
        xd_freed = {}
        xd_loaded = {}
        dt_freed = {}
        dt_done = {}
        bc_done_d = {}
        sel_freed = {}
        xdbl_st = {}

        s.do("gpsimd", lambda e: e.dma_start(out=s_ipw[:], in_=ipT_d[:, :]),
             inc=(wld, 16))
        wneed = (wld, wld.vn)
        s.do("sync", lambda e: e.dma_start(out=s_selb[:], in_=selbc[:, :]),
             inc=(ld, 16))
        selb_ld = ld.vn

        def p3a(gi):
            r, ci3 = gi // NB, gi % NB
            half = gi % 2
            buf = s_uln[0]
            rngb = s_rng[gi % 2]
            s.do("scalar", lambda e, r=r, ci3=ci3, buf=buf: e.dma_start(
                out=buf[:],
                in_=ulnag_out[r * 128:(r + 1) * 128,
                              ci3 * 4096:(ci3 + 1) * 4096]),
                inc=(uld, 16), waits=[(mm, uln_freed[0])])
            lneed = (uld, uld.vn)
            for mt in range(8):
                slot = upg[0] % 3
                upg[0] += 1
                w = [wneed, lneed] if mt == 0 else []
                if slot in up_slot_ev:
                    w.append((ev, up_slot_ev[slot]))
                for kt in range(8):
                    s.do("tensor", lambda e, mt=mt, kt=kt, slot=slot, buf=buf: e.matmul(
                        p_up[slot][:, :],
                        s_ipw[:, kt * 1024 + mt * 128:kt * 1024 + (mt + 1) * 128],
                        buf[:, kt * 512:(kt + 1) * 512],
                        start=(kt == 0), stop=(kt == 7)),
                        inc=(mm, 1) if kt == 7 else None, waits=w)
                if mt < 4:
                    w2 = [mm]
                    if (gi - 2) in ring_freed:
                        w2.append((ve, ring_freed[gi - 2]))
                    s.do("scalar", lambda e, mt=mt, slot=slot, rngb=rngb: e.copy(
                        rngb[:, mt * 516 + 3:mt * 516 + 515], p_up[slot][:, :]),
                        inc=(ev, 1), waits=w2)
                else:
                    ct = mt - 4
                    zs = ct % 2
                    zsem = st3 if zs == 0 else st4
                    w2 = [mm]
                    if zs in sz_st3:
                        w2.append((zsem, sz_st3[zs]))
                    s.do("scalar", lambda e, ct=ct, slot=slot, zs=zs: e.activation(
                        s_sz2[zs][:, :], p_up[slot][:, :],
                        AF.Silu, bias=s_ipbz[:, ct:ct + 1]),
                        inc=(ev, 1), waits=w2)
                    s.do("sync", lambda e, ct=ct, gi=gi, zs=zs: e.dma_start(
                        out=sz_d.ap().rearrange("(a p) t -> p a t", p=128)[
                            :, ct, gi * 512:(gi + 1) * 512],
                        in_=s_sz2[zs][:, :]),
                        inc=(zsem, 16), waits=[ev])
                    sz_st3[zs] = zsem.vn
                up_slot_ev[slot] = ev.vn
            uln_freed[0] = mm.vn
            p3a_ev[gi] = ev.vn
            if gi == 0:
                for ct in range(4):
                    s.do("vector", lambda e, ct=ct: e.memset(
                        s_rng[0][:, ct * 516:ct * 516 + 3], 0.0), inc=(ve, 1))
                ring_init[0] = ve.vn

        def p3b(gi):
            c4, half = gi // 2, gi % 2
            ub4 = s_u4[c4 % 2]
            rngb = s_rng[gi % 2]
            for ct in range(4):
                base = ct * 516
                uap = lambda ct=ct: ub4[:, ct * CH4 + half * 512:ct * CH4 + half * 512 + 512]
                w = []
                if ct == 0:
                    w.append((ev, p3a_ev[gi]))
                    if gi == 0:
                        w.append((ve, ring_init[0]))
                    else:
                        w.append((ev, tail_ev[gi - 1]))
                    if (c4 - 2) in u_freed:
                        w.append((ve, u_freed[c4 - 2]))
                s.do("vector", lambda e, ct=ct, base=base, uap=uap, rngb=rngb: e.tensor_single_scalar(
                    uap(), rngb[:, base:base + 512],
                    s_convw[:, ct * 4:ct * 4 + 1], ALU.mult),
                    inc=(ve, 1), waits=w)
                for k in range(1, 4):
                    s.do("vector", lambda e, ct=ct, base=base, k=k, uap=uap, rngb=rngb: e.scalar_tensor_tensor(
                        uap(), rngb[:, base + k:base + k + 512],
                        s_convw[:, ct * 4 + k:ct * 4 + k + 1],
                        uap(), ALU.mult, ALU.add),
                        inc=(ve, 1), waits=[ve])
            ring_freed[gi] = ve.vn
            nrngb = s_rng[(gi + 1) % 2]
            for ct in range(4):
                w = [(ev, p3a_ev[gi])]
                if gi >= 1:
                    w.append((ve, ring_freed[gi - 1]))
                s.do("scalar", lambda e, ct=ct, rngb=rngb, nrngb=nrngb: e.copy(
                    nrngb[:, ct * 516:ct * 516 + 3],
                    rngb[:, ct * 516 + 512:ct * 516 + 515]),
                    inc=(ev, 1), waits=w if ct == 0 else ())
            tail_ev[gi] = ev.vn
            for ct in range(4):
                uap = lambda ct=ct: ub4[:, ct * CH4 + half * 512:ct * CH4 + half * 512 + 512]
                s.do("scalar", lambda e, ct=ct, uap=uap: e.activation(
                    uap(), uap(),
                    AF.Silu, bias=s_cbeff[:, ct:ct + 1]),
                    inc=(ev, 1), waits=[(ve, ring_freed[gi])] if ct == 0 else ())
            if half == 1:
                u_ready[c4] = ev.vn
            for kt in range(4):
                wc = [ev]
                if kt == 0:
                    wc.append((mm, xp_mm[0]))
                else:
                    wc.append(mm)
                s.do("scalar", lambda e, kt=kt, ub4=ub4, half=half: e.copy(
                    s_ub[:, :],
                    ub4[:, kt * CH4 + half * 512:kt * CH4 + half * 512 + 512]),
                    inc=(ev, 1), waits=wc)
                w = [ev]
                if kt == 0 and xp_ev[0]:
                    w.append((ev, xp_ev[0]))
                s.do("tensor", lambda e, kt=kt: e.matmul(
                    p_xp[:, :], s_xpT[:, kt * 96:(kt + 1) * 96],
                    s_ub[:, :],
                    start=(kt == 0), stop=(kt == 3)),
                    inc=(mm, 1), waits=w)
            xp_mm[0] = mm.vn
            w = [mm]
            if half == 0 and (c4 - 1) in xdbl_st:
                w.append((st, xdbl_st[c4 - 1]))
            s.do("scalar", lambda e, half=half: e.copy(
                s_xda[:, half * 512:(half + 1) * 512], p_xp[0:96, :]),
                inc=(ev, 1), waits=w)
            xp_ev[0] = ev.vn

        def c3_part(c4):
            s.do("sync", lambda e, c4=c4: e.dma_start(
                out=xdbl_in[c4 * 96:(c4 + 1) * 96, :], in_=s_xda[:]),
                inc=(st, 16), waits=[(ev, xp_ev[0])])
            xdbl_st[c4] = st.vn
            s.do("gpsimd", lambda e, c4=c4: e.collective_compute(
                "AllReduce", ALU.add, replica_groups=G4,
                ins=[xdbl_in.ap()[c4 * 96:(c4 + 1) * 96, :].opt()],
                outs=[xdbl_out.ap()[c4 * 96:(c4 + 1) * 96, :].opt()]),
                inc=(cc3s[c4], "cc"), waits=[(st, st.vn)])
            xdb = s_xd4[c4 % 2]
            w = [(cc3s[c4], 1)]
            if (c4 - 2) in xd_freed:
                w += [(mm, xd_freed[c4 - 2][0]), (ev, xd_freed[c4 - 2][1])]
            s.do("sync", lambda e, c4=c4, xdb=xdb: e.dma_start(
                out=xdb[:], in_=xdbl_out[c4 * 96:(c4 + 1) * 96, :]),
                inc=(xld, 16), waits=w)
            xd_loaded[c4] = xld.vn

        def dtprep(c4):
            xdb = s_xd4[c4 % 2]
            lneed = (xld, xd_loaded[c4])
            w = [lneed]
            if (c4 - 1) in dt_freed:
                w.append((mm, dt_freed[c4 - 1]))
            s.do("scalar", lambda e, xdb=xdb: e.copy(s_dtin[:, :], xdb[0:64, :]),
                 inc=(ev, 1), waits=w)
            dtin_done = (ev, ev.vn)
            regions = [p_br[:, 0:512], p_br[:, 512:1024],
                       p_cr[:, 0:512], p_cr[:, 512:1024]]
            dt_reg_ev = {}
            for mt in range(4):
                for nq in range(NQ):
                    gidx = mt * NQ + nq
                    reg = gidx % 4
                    w = []
                    if gidx == 0:
                        w += [dtin_done, (ve, ve.vn), (gp, gp.vn)]
                    if reg in dt_reg_ev:
                        w.append((ev, dt_reg_ev[reg]))
                    s.do("tensor", lambda e, mt=mt, nq=nq, reg=reg: e.matmul(
                        regions[reg], s_dtpT[:, mt * 128:(mt + 1) * 128],
                        s_dtin[:, nq * 512:(nq + 1) * 512],
                        start=True, stop=True),
                        inc=(mm, 1), waits=w)
                    cc0 = mt * CH4 + nq * 512
                    s.do("scalar", lambda e, mt=mt, cc0=cc0, reg=reg: e.activation(
                        s_dsz[:, cc0:cc0 + 512], regions[reg], AF.Relu,
                        bias=s_dtpb[:, mt:mt + 1]),
                        inc=(ev, 1),
                        waits=[mm, (ve, ve.vn)] if gidx == 0 else [mm])
                    s.do("scalar", lambda e, mt=mt, reg=reg: e.activation(
                        s_ab[:, 0:512], regions[reg], AF.Abs,
                        bias=s_dtpb[:, mt:mt + 1]),
                        inc=(ev, 1), waits=[(ve, ve.vn)])
                    s.do("scalar", lambda e: e.activation(
                        s_ab[:, 0:512], s_ab[:, 0:512], AF.Exp, scale=-1.0),
                        inc=(ev, 1), waits=[ev])
                    s.do("scalar", lambda e: e.activation(
                        s_ab[:, 0:512], s_ab[:, 0:512], AF.Ln, bias=1.0),
                        inc=(ev, 1), waits=[ev])
                    dt_reg_ev[reg] = ev.vn
                    s.do("vector", lambda e, cc0=cc0: e.tensor_add(
                        s_dsz[:, cc0:cc0 + 512], s_dsz[:, cc0:cc0 + 512],
                        s_ab[:, 0:512]),
                        inc=(ve, 1), waits=[ev])
            dt_freed[c4] = mm.vn
            dt_done[c4] = ve.vn
            w = [lneed, (ld, selb_ld)]
            if (c4 - 1) in sel_freed:
                w.append((mm, sel_freed[c4 - 1]))
            s.do("scalar", lambda e, xdb=xdb: e.copy(s_bcb[:, :], xdb[64:96, :]),
                 inc=(ev, 1), waits=w)
            bc_done_d[c4] = ev.vn
            xd_freed[c4] = (mm.vn, ev.vn)

        def sloop(c4):
            ubuf = s_u4[c4 % 2]
            uneed = (ev, u_ready[c4])
            dt_done_ve = (ve, dt_done[c4])
            bc_done = (ev, bc_done_d[c4])
            prev_hinit = hinit_done[0]
            b_done = {}
            ym_done = {}
            scan_done = {}
            ymv = {}
            ya_done = {}
            for sv in range(S):
                for nq in range(NQ):
                    w = []
                    if sv == 0 and nq == 0:
                        w += [bc_done, (ev, ev.vn)]
                    if sv >= 1 and nq == 0:
                        w.append((ve, b_done[sv - 1]))
                    s.do("tensor", lambda e, sv=sv, nq=nq: e.matmul(
                        p_br[:, nq * 512:(nq + 1) * 512],
                        s_selb[:, sv * 128:(sv + 1) * 128],
                        s_bcb[0:32, nq * 512:(nq + 1) * 512],
                        start=True, stop=True),
                        inc=(mm, 1), waits=w)
                br_mm = (mm, mm.vn)
                for ct in range(4):
                    w = []
                    if (ct, sv - 1) in scan_done:
                        w.append((ve, scan_done[(ct, sv - 1)]))
                    if sv == 0 and ct == 0:
                        w.append(dt_done_ve)
                    s.do("scalar", lambda e, ct=ct, sv=sv: e.activation(
                        s_a4[:, ct * CH4:(ct + 1) * CH4],
                        s_dsz[:, ct * CH4:(ct + 1) * CH4],
                        AF.Exp, scale=s_negA[:, ct * 16 + sv:ct * 16 + sv + 1]),
                        inc=(ev, 1), waits=w)
                a_done = (ev, ev.vn)
                for ct in range(4):
                    w = [br_mm] if ct == 0 else []
                    if sv == 0 and ct == 0:
                        w.append(uneed)
                    s.do("vector", lambda e, ct=ct, ubuf=ubuf: e.tensor_mul(
                        s_b4[:, ct * CH4:(ct + 1) * CH4],
                        ubuf[:, ct * CH4:(ct + 1) * CH4], p_br[:, :]),
                        inc=(ve, 1), waits=w)
                b_done[sv] = ve.vn
                for ct in range(4):
                    col = ct * 16 + sv
                    init = 0.0 if c4 == 0 else s_hinit[:, col:col + 1]
                    w = [a_done] if ct == 0 else []
                    if c4 > 0 and sv == 0 and ct == 0:
                        w.append((ev, prev_hinit))
                    if sv >= 1 and (sv - 1, ct) in ymv:
                        w.append((ve, ymv[(sv - 1, ct)]))
                    if ct == 0:
                        w.append(ve)
                    s.do("vector", lambda e, ct=ct, init=init: e.tensor_tensor_scan(
                        s_h4[ct][:, :], s_a4[:, ct * CH4:(ct + 1) * CH4],
                        s_b4[:, ct * CH4:(ct + 1) * CH4], init,
                        ALU.mult, ALU.add),
                        inc=(ve, 1), waits=w)
                    scan_done[(ct, sv)] = ve.vn
                    s.do("scalar", lambda e, ct=ct, col=col: e.copy(
                        s_hinit[:, col:col + 1], s_h4[ct][:, CH4 - 1:CH4]),
                        inc=(ev, 1), waits=[(ve, scan_done[(ct, sv)])])
                for nq in range(NQ):
                    w = []
                    if nq == 0:
                        w.append((ve, ym_done[sv - 1]) if sv >= 1 else bc_done)
                    s.do("tensor", lambda e, sv=sv, nq=nq: e.matmul(
                        p_cr[:, nq * 512:(nq + 1) * 512],
                        s_selb[:, (16 + sv) * 128:(17 + sv) * 128],
                        s_bcb[0:32, nq * 512:(nq + 1) * 512],
                        start=True, stop=True),
                        inc=(mm, 1), waits=w)
                cr_mm = (mm, mm.vn)
                if sv == 0:
                    for ct in range(4):
                        w = [cr_mm, ve, (gp, gp.vn)] if ct == 0 else ()
                        s.do("vector", lambda e, ct=ct: e.tensor_mul(
                            s_y4[:, ct * CH4:(ct + 1) * CH4], s_h4[ct][:, :],
                            p_cr[:, :]),
                            inc=(ve, 1), waits=w)
                        ymv[(sv, ct)] = ve.vn
                else:
                    ya_cur = {}
                    for kind, ct in [("m", 0), ("m", 1), ("a", 0), ("m", 2),
                                     ("a", 1), ("m", 3), ("a", 2), ("a", 3)]:
                        if kind == "m":
                            w = [cr_mm, ve] if ct == 0 else []
                            w.append((gp, ya_done.get((sv - 1, (ct % 2) + 2), 0)))
                            if ct >= 2:
                                w.append((gp, ya_cur[ct - 2]))
                            s.do("vector", lambda e, ct=ct: e.tensor_mul(
                                s_ym[ct % 2][:, :], s_h4[ct][:, :], p_cr[:, :]),
                                inc=(ve, 1), waits=w)
                            ymv[(sv, ct)] = ve.vn
                        else:
                            s.do("gpsimd", lambda e, ct=ct: e.tensor_add(
                                s_y4[:, ct * CH4:(ct + 1) * CH4],
                                s_y4[:, ct * CH4:(ct + 1) * CH4],
                                s_ym[ct % 2][:, :]),
                                inc=(gp, 1), waits=[(ve, ymv[(sv, ct)]), gp])
                            ya_done[(sv, ct)] = gp.vn
                            ya_cur[ct] = gp.vn
                ym_done[sv] = ve.vn
            hinit_done[0] = ev.vn
            sel_freed[c4] = mm.vn
            s.do("sync", lambda e, c4=c4: e.dma_start(
                out=s_dsz[:].rearrange("p (a m) -> p a m", a=4),
                in_=sz_d.ap().rearrange("(a p) t -> p a t", p=128)[
                    :, :, c4 * CH4:(c4 + 1) * CH4]),
                inc=(ld, 16), waits=[(ve, ve.vn), (ev, ev.vn), (st3, st3.vn), (st4, st4.vn)])
            szneed = (ld, ld.vn)
            for ct in range(4):
                s.do("vector", lambda e, ct=ct, ubuf=ubuf: e.scalar_tensor_tensor(
                    s_y4[:, ct * CH4:(ct + 1) * CH4],
                    ubuf[:, ct * CH4:(ct + 1) * CH4],
                    s_Dcs[:, ct:ct + 1], s_y4[:, ct * CH4:(ct + 1) * CH4],
                    ALU.mult, ALU.add),
                    inc=(ve, 1), waits=[ve, gp] if ct == 0 else ())
            u_freed[c4] = ve.vn
            for ct in range(4):
                s.do("vector", lambda e, ct=ct: e.tensor_mul(
                    s_ygb[:, ct * CH4:(ct + 1) * CH4],
                    s_y4[:, ct * CH4:(ct + 1) * CH4],
                    s_dsz[:, ct * CH4:(ct + 1) * CH4]),
                    inc=(ve, 1), waits=[szneed, (st, st.vn), ve] if ct == 0 else ())
            s.do("sync", lambda e, c4=c4: e.dma_start(
                out=ygag_in.ap().rearrange("(a p) t -> p a t", p=128)[
                    :, :, c4 * CH4:(c4 + 1) * CH4],
                in_=s_ygb[:].rearrange("p (a m) -> p a m", a=4)),
                inc=(st, 16), waits=[ve])

        for c4 in range(NCH4):
            p3a(2 * c4)
            p3a(2 * c4 + 1)
            if c4 >= 2:
                sloop(c4 - 2)
            if c4 >= 1:
                dtprep(c4 - 1)
            p3b(2 * c4)
            p3b(2 * c4 + 1)
            c3_part(c4)
        sloop(NCH4 - 2)
        dtprep(NCH4 - 1)
        sloop(NCH4 - 1)

    def phaseP5a(s, pctx):
        psb = lambda n, sh, dt=F32: pctx.enter_context(nc.sbuf_tensor(n, list(sh), dt))
        pps = lambda n, sh: pctx.enter_context(nc.psum_tensor(n, list(sh), F32))
        s_opw = psb("p5_opw", (128, 16 * 1024), BF16)
        s_yg = [psb(f"p5_yg{i}", (128, 16 * 512), BF16) for i in range(2)]
        s_x2p = psb("p5_x2p", (128, 4 * D))
        p_z = [pps(f"p5_pz{i}", (128, 1024)) for i in range(2)]
        z_slot = {}

        s.do("scalar", lambda e: e.dma_start(
            out=s_opw[:], in_=opbf_d[:, :]),
            inc=(wls, 16))
        wneed = (wls, wls.vn)
        qcell = []

        def ygload(e, ci, buf):
            if not qcell:
                qcell.append(e.snap(nc.partition_id() % 4 * T_loc))
            return e.dma_start(
                out=buf[:].rearrange("p (a m) -> p a m", a=16),
                in_=ygag_out.ap().rearrange("(a p) t -> p a t", p=128)[
                    :, :, bass.ds(qcell[0] + ci * 512, 512)])

        for ci in range(NCH1):
            buf = s_yg[ci % 2]
            s.do("sync", lambda e, ci=ci, buf=buf: ygload(e, ci, buf),
                 inc=(ld, 16), waits=[(mm, mm.vn)])
            s.do("sync", lambda e, ci=ci: e.dma_start(
                out=s_x2p[:], in_=x2p_d[:, ci * 4096:(ci + 1) * 4096]),
                inc=(ld, 16), waits=[(ve, ve.vn), (st, st.vn)])
            lneed = (ld, ld.vn)
            for sub in range(4):
                slot = sub % 2
                w = [lneed, wneed] if sub == 0 else []
                if slot in z_slot:
                    w.append(z_slot[slot])
                first = True
                for nh in range(2):
                    for kt in range(16):
                        s.do("tensor", lambda e, sub=sub, nh=nh, kt=kt, slot=slot, buf=buf: e.matmul(
                            p_z[slot][:, nh * 512:(nh + 1) * 512],
                            buf[:, kt * 512 + sub * 128:kt * 512 + (sub + 1) * 128],
                            s_opw[:, kt * 1024 + nh * 512:kt * 1024 + (nh + 1) * 512],
                            start=(kt == 0), stop=(kt == 15)),
                            inc=(mm, 1) if (nh == 1 and kt == 15) else None,
                            waits=w if first else ())
                        first = False
                s.do("vector", lambda e, sub=sub, slot=slot: e.tensor_add(
                    s_x2p[:, sub * D:(sub + 1) * D], s_x2p[:, sub * D:(sub + 1) * D],
                    p_z[slot][:, :]),
                    inc=(ve, 1), waits=[mm])
                z_slot[slot] = (ve, ve.vn)
            s.do("sync", lambda e, ci=ci: e.dma_start(
                out=x2_d[:, ci * 4096:(ci + 1) * 4096], in_=s_x2p[:]),
                inc=(st, 16), waits=[ve])

    phases = [
        phaseW,
        make_ffn(x_sh, x1_d.ap(), "fa", in_ext=True),
        phaseMLP,
        phaseC("AllReduce", ALU.max, G8, gam_in, gam_out, cc1),
        phaseP2,
        phaseC("AllGather", ALU.bypass, G4, ulnag_in, ulnag_out, cc2),
        phaseP34,
        phaseC("AllGather", ALU.bypass, G4, ygag_in, ygag_out, cc4),
        phaseP5a,
        make_ffn(x2_d.ap(), out_sh, "fb", out_ext=True),
    ]
    np_ = NPHASES if NPHASES is not None else len(phases)
    for ph in phases[:np_]:
        run_phase(ph)

    for sem in SEM.values():
        assert sem.vn < 60000, (sem.name, sem.vn)
    ctx.close()
    return nc


def host_inputs(inputs, L):
    T_loc = L // 4
    g = lambda k: np.asarray(inputs[k])
    x = g("L0_attn_x")
    co = np.ascontiguousarray

    def cs(v, nk):
        return co(v.reshape(nk, 128).T)

    W1 = g("L0_attn_W1")
    W2 = g("L0_attn_W2")

    def tiles(wT, nk):
        K, M = wT.shape
        assert K == nk * 128
        return co(wT.reshape(nk, 128, M).transpose(1, 0, 2).reshape(128, nk * M))

    Ws1 = g("L0_ffn_Ws1")
    base = dict(
        ffn_W1=co(np.stack([tiles(g("L0_ffn_W1")[e], 8) for e in range(NE)])),
        ffn_W2=co(np.stack([tiles(g("L0_ffn_W2")[e], 4) for e in range(NE)])),
        ffn_Ws1a=tiles(Ws1[:, 0:HS], 8),
        ffn_Ws1b=tiles(Ws1[:, HS:2 * HS], 8),
        ffn_Ws2=tiles(g("L0_ffn_Ws2"), 8),
        ffn_WgT=tiles(co(g("L0_ffn_Wg").T), 8),
        aW1T=tiles(co(W1.T), 8),
        aW2T=tiles(co(W2.T), 16),
        b1_row=co(g("L0_attn_b1").reshape(1, DIN)),
        b2_row=co(g("L0_attn_b2").reshape(1, D)),
        b1ln_cs=cs(g("L0_attn_ln1_b"), 8).astype(BF),
        g1_cs=cs(g("L0_attn_ln1_g"), 8),
        b2ln_cs=cs(g("L0_attn_ln2_b"), 8).astype(BF),
        g2_cs=cs(g("L0_attn_ln2_g"), 8),
        opT=tiles(co(g("L0_attn_out_proj_w").T), 16),
        identf=np.eye(128, dtype=np.float32),
        ones_row=np.ones((1, 512), dtype=BF),
    )
    sel16 = np.zeros((16, 16 * 128), dtype=BF)
    for e in range(16):
        sel16[e, e * 128:(e + 1) * 128] = 1.0
    base["sel16"] = sel16
    selbc = np.zeros((32, 32 * 128), dtype=BF)
    for j in range(32):
        selbc[j, j * 128:(j + 1) * 128] = 1.0
    base["selbc"] = selbc

    ipw = g("L0_attn_in_proj_w")
    xpw = g("L0_attn_x_proj_w")
    dtw = g("L0_attn_dt_proj_w")
    dtb = g("L0_attn_dt_proj_b")
    cw = g("L0_attn_conv_w")[:, 0, :]
    cb = g("L0_attn_conv_b")
    Al = g("L0_attn_A_log")
    Dp = g("L0_attn_D_param")

    maps = []
    for c in range(NCORES):
        b, q = c // 4, c % 4
        sl = slice(q * 512, (q + 1) * 512)
        ip_sl = np.concatenate([ipw[sl], ipw[DIN + q * 512:DIN + (q + 1) * 512]], axis=0)
        gm = np.zeros((1, 16), dtype=np.float32)
        gm[0, q] = 1.0
        m = dict(base)
        m.update(
            x_sh=co(x[b, q * T_loc:(q + 1) * T_loc]),
            ipT_sl=tiles(co(ip_sl.T), 8),
            xpT_sl=tiles(co(xpw[:, sl].T), 4),
            dtpT_sl=co(dtw[sl].T),
            dtpb_cs=cs(dtb[sl], 4),
            convw_cs=co(cw[sl].reshape(4, 128, 4).transpose(1, 0, 2).reshape(128, 16)),
            convb_cs=cs(cb[sl], 4),
            A_cs=co(Al[sl].reshape(4, 128, 16).transpose(1, 0, 2).reshape(128, 64)),
            D_cs=cs(Dp[sl], 4),
            gmask=gm,
        )
        maps.append(m)
    return maps


def kernel(**inputs):
    L = L_FULL
    nc = build_program(L)
    maps = host_inputs(inputs, L)
    res = run_bass_kernel_spmd(nc, maps, core_ids=list(range(NCORES)))
    T_loc = L // 4
    out = np.empty((2, L, D), dtype=np.float32)
    for c in range(NCORES):
        b, q = c // 4, c % 4
        out[b, q * T_loc:(q + 1) * T_loc] = res.results[c]["out_sh"]
    return out
